# version 06
# baseline (speedup 1.0000x reference)
import numpy as np
import concourse.bass as bass
import concourse.mybir as mybir
from concourse.bass_utils import run_bass_kernel_spmd

F32 = mybir.dt.float32
BF16 = mybir.dt.bfloat16
AF = mybir.ActivationFunctionType
ALU = mybir.AluOpType
AX = mybir.AxisListType

ENGINES = ("sync", "scalar", "vector", "tensor", "gpsimd")
DMA_INC = 16



class Op:
    __slots__ = ("engine", "axis", "fn", "deps", "eidx", "tagged", "waits",
                 "clock", "rank", "name")

    def __repr__(self):
        return f"<Op {self.engine}/{self.axis}#{self.eidx} {self.name}>"


class Plan:

    def __init__(self):
        self.ops = []
        self.axis_count = {}
        self.last_write = {}
        self.readers = {}

    def _add(self, engine, axis, fn, reads, writes, deps, name):
        op = Op()
        op.engine = engine
        op.axis = axis
        op.fn = fn
        op.name = name
        op.tagged = "_dma" in axis
        op.waits = None
        op.clock = None
        op.rank = None
        d = []
        for k in reads:
            w = self.last_write.get(k)
            if w is not None:
                d.append(w)
        for k in writes:
            w = self.last_write.get(k)
            if w is not None:
                d.append(w)
            rs = self.readers.get(k)
            if rs:
                d.extend(rs)
        for x in deps:
            if x is not None:
                d.append(x)
        op.deps = d
        op.eidx = self.axis_count.get(axis, 0)
        self.axis_count[axis] = op.eidx + 1
        for k in reads:
            self.readers.setdefault(k, []).append(op)
        for k in writes:
            self.last_write[k] = op
            self.readers[k] = []
        self.ops.append(op)
        return op

    def op(self, engine, fn, reads=(), writes=(), deps=(), name=""):
        return self._add(engine, engine, fn, reads, writes, deps, name)

    def dma(self, engine, out, in_, reads=(), writes=(), deps=(), name="", ch=0,
            **kw):
        def fn(eng, _out=out, _in=in_, _kw=kw):
            return eng.dma_start(_out, _in, **_kw)
        return self._add(engine, "%s_dma%d" % (engine, ch), fn, reads, writes,
                         deps, name)

    def barrier_all_dma(self, engine="gpsimd"):
        deps = []
        for axis, cnt in self.axis_count.items():
            if "_dma" in axis and cnt > 0:
                for o in reversed(self.ops):
                    if o.axis == axis:
                        deps.append(o)
                        break
        self._add(engine, engine, None, (), (), deps, "final_dma_barrier")

    def finalize(self):
        observed = {e: {} for e in ENGINES}
        axis_ops = {}
        for op in self.ops:
            axis_ops.setdefault(op.axis, []).append(op)
        issued = {}
        for op in self.ops:
            obs = observed[op.engine]
            need = {}
            for dop in op.deps:
                v = dop.eidx + 1
                if need.get(dop.axis, 0) < v:
                    need[dop.axis] = v
            if "_dma" in op.axis and op.eidx > 0:
                v = op.eidx
                if need.get(op.axis, 0) < v:
                    need[op.axis] = v
            waits = []
            for axis, v in need.items():
                if axis == "tensor" and op.engine == "tensor":
                    continue
                if obs.get(axis, 0) >= v:
                    continue
                waits.append((axis, v))
            for dop in op.deps:
                c = dop.clock
                if c:
                    for a, v in c.items():
                        if obs.get(a, 0) < v:
                            obs[a] = v
            for axis, v in need.items():
                if obs.get(axis, 0) < v:
                    obs[axis] = v
            for axis, v in waits:
                if obs.get(axis, 0) < v:
                    obs[axis] = v
            op.waits = waits
            for axis, v in waits:
                axis_ops[axis][v - 1].tagged = True
            clk = dict(obs)
            if clk.get(op.axis, 0) < op.eidx + 1:
                clk[op.axis] = op.eidx + 1
            op.clock = clk
            issued[op.axis] = op.eidx + 1
        self.rank_map = {}
        for axis, ops in axis_ops.items():
            r = 0
            ranks = [0] * len(ops)
            for i, o in enumerate(ops):
                if o.tagged:
                    r += 1
                    o.rank = r
                ranks[i] = r
            self.rank_map[axis] = ranks
        self.axis_ops = axis_ops

    def emit(self, nc, sems):
        per_engine = {e: [] for e in ENGINES}
        for op in self.ops:
            per_engine[op.engine].append(op)

        def make_runner(engine_name):
            ops = per_engine[engine_name]

            def run(eng):
                for op in ops:
                    for axis, v in op.waits:
                        r = self.rank_map[axis][v - 1]
                        assert r > 0, (op, axis, v)
                        mult = DMA_INC if "_dma" in axis else 1
                        eng.wait_ge(sems[axis], r * mult)
                    if op.fn is None:
                        continue
                    inst = op.fn(eng)
                    if op.tagged:
                        inst.then_inc(sems[op.axis],
                                      DMA_INC if "_dma" in op.axis else 1)
            return run

        with nc.Block() as block:
            block.sync(make_runner("sync"))
            block.scalar(make_runner("scalar"))
            block.vector(make_runner("vector"))
            block.tensor(make_runner("tensor"))
            block.gpsimd(make_runner("gpsimd"))

    def used_axes(self):
        return [a for a, c in self.axis_count.items() if c > 0]



D = 1024
DC = 8
DFF = 4096
NH = 16
DH = 64
NPAIR = 8
NS = 64
WIN = 128
NE = 8
NEG = -1.0e30


class KB:

    def __init__(self, nc, p, T, ctx):
        self.nc = nc
        self.p = p
        self.T = T
        self.TC = T // 128
        self.NTG = T // 512
        e = ctx.enter_context
        self.x = e(nc.sbuf_tensor("x", [128, self.TC * 1024], F32))
        self.hnT = e(nc.sbuf_tensor("hnT", [128, DC * T], BF16))
        self.AR = max(36864, 8192 + 16 * T)
        self.ar = e(nc.sbuf_tensor("ar", [128, self.AR], BF16))
        self.zf = e(nc.sbuf_tensor("zf", [128, 1024], F32))
        self.zbf = e(nc.sbuf_tensor("zbf", [128, 1024], BF16))
        self.zT = e(nc.sbuf_tensor("zT", [128, 512], F32))
        self.probs = e(nc.sbuf_tensor("probs", [128, 1024], F32))
        self.st_sb = e(nc.sbuf_tensor("st_sb", [64, 2048], BF16))
        self.ab_sb = e(nc.sbuf_tensor("ab_sb", [64, 512], F32))
        self.cw_sb = e(nc.sbuf_tensor("cw_sb", [64, 1024], BF16))
        self.bw_sb = e(nc.sbuf_tensor("bw_sb", [128, DC * 64], BF16))
        self.acol = e(nc.sbuf_tensor("acol", [64, 1], F32))
        st = lambda n: e(nc.sbuf_tensor(n, [128, 16], F32))
        self.nsum = st("nsum")
        self.sumsq = st("sumsq")
        self.var = st("var")
        self.rstd = st("rstd")
        self.nmr = st("nmr")
        self.tmpst = st("tmpst")
        self.negmax = st("negmax")
        self.rstd_o = st("rstd_o")
        self.nmr_o = st("nmr_o")
        self.esum = st("esum")
        self.recip = st("recip")
        self.alpha1 = e(nc.sbuf_tensor("alpha1", [128, 32], F32))
        self.alpha2 = e(nc.sbuf_tensor("alpha2", [128, 8], F32))
        self.bup_col = e(nc.sbuf_tensor("bup_col", [128, 32], F32))
        self.gcol = e(nc.sbuf_tensor("gcol", [128, 8], F32))
        self.bcol = e(nc.sbuf_tensor("bcol", [128, 8], F32))
        self.qkvcol = e(nc.sbuf_tensor("qkvcol", [128, 24], F32))
        self.nscol = e(nc.sbuf_tensor("nscol", [128, 8], F32))
        self.coeff = e(nc.sbuf_tensor("coeff", [128, NE * 16], F32))
        self.logit = e(nc.sbuf_tensor("logit", [128, 16 * 8], F32))
        self.selw = e(nc.sbuf_tensor("selw", [128, 16], F32))
        self.found = e(nc.sbuf_tensor("found", [128, 16], F32))
        self.gwT = e(nc.sbuf_tensor("gwT", [128, 64], F32))
        self.ones_bf = e(nc.sbuf_tensor("ones_bf", [1, 128], BF16))
        self.row0 = e(nc.sbuf_tensor("row0", [1, 1024], BF16))
        self.row1 = e(nc.sbuf_tensor("row1", [1, 1024], BF16))
        self.row2 = e(nc.sbuf_tensor("row2", [1, 1024], BF16))
        self.vspread = e(nc.sbuf_tensor("vspread", [32, 128], F32))
        self.cpool = e(nc.sbuf_tensor("cpool", [128, 8], F32))
        self.cpool_n = 0
        self.ident_bf = e(nc.sbuf_tensor("ident_bf", [128, 128], BF16))
        self.ident_f = e(nc.sbuf_tensor("ident_f", [128, 128], F32))
        self.mask = e(nc.sbuf_tensor("mask", [128, 256], BF16))
        self.psBIG = e(nc.psum_tensor("psBIG", [128, 2048], F32))
        self.psB0 = e(nc.psum_tensor("psB0", [128, 512], F32))
        self.psB1 = e(nc.psum_tensor("psB1", [128, 512], F32))
        self.psT = e(nc.psum_tensor("psT", [128, 1024], BF16))
        self.psS = e(nc.psum_tensor("psS", [128, 512], F32))
        assert nc.psum_banks_remaining == 0, nc.psum_banks_remaining
        self.sbuf_left = nc.sbuf_bytes_remaining

    def ark(self, lo, n):
        return ["ar%d" % i for i in range(lo // 512, (lo + n + 511) // 512)]

    def pbk(self, lo, n):
        return ["pb%d" % i for i in range(lo // 512, (lo + n + 511) // 512)]

    def mm(self, out, lhsT, rhs, start, stop, reads, writes, name="mm",
           tile_position=None):
        return self.p.op(
            "tensor",
            lambda e, o=out, a=lhsT, b=rhs, s=start, t=stop, tp=tile_position:
                e.matmul(o, a, b, start=s, stop=t, skip_group_check=True,
                         tile_position=tp),
            reads=reads, writes=writes, name=name)

    def tr(self, out, in_, ident, reads, writes, name="tr", ident_key="idb"):
        return self.p.op(
            "tensor",
            lambda e, o=out, a=in_, i=ident: e.transpose(o, a, i),
            reads=list(reads) + [ident_key], writes=writes, name=name)

    def stat(self, kind, s):
        tiles = {
            "nm": (self.negmax, "nm", self.sumsq, "sumsq"),
            "es": (self.esum, "es", self.nsum, "nsum"),
            "rc": (self.recip, "rc", self.rstd, "rstd"),
            "tm": (self.tmpst, "tmpst", self.var, "var"),
        }[kind]
        if s < 16:
            t, kp = tiles[0], tiles[1]
        else:
            t, kp = tiles[2], tiles[3]
        c = s % 16
        return t[:, c:c + 1], f"{kp}{c}"

    def register_const(self, v):
        i = self.cpool_n
        self.cpool_n += 1
        assert i < 8
        self.p.op("vector", lambda e, c=self.cpool[:, i:i + 1], vv=v:
                  e.memset(c, vv), writes=["cpool"], name="cpool")
        self.nc.const_aps.register(F32, v, self.cpool[:, i:i + 1])

    def act(self, out, in_, func, bias=0.0, scale=1.0, reads=(), writes=(),
            accum_out=None, name="act"):
        extra = []
        if (isinstance(bias, float) and bias != 0.0) or \
           (isinstance(scale, float) and scale not in (1.0,)):
            extra = ["cpool"]

        def fn(e, o=out, a=in_, f=func, b=bias, s=scale, ac=accum_out):
            if ac is None:
                return e.activation(o, a, f, bias=b, scale=s)
            return e.activation(o, a, f, bias=b, scale=s, accum_out=ac)
        return self.p.op("scalar", fn, reads=list(reads) + extra, writes=writes,
                         name=name)

    def vop(self, fn, reads=(), writes=(), name="vop", eng="vector"):
        return self.p.op(eng, fn, reads=reads, writes=writes, name=name)




def load_col_tile(kb, vec_ap, F, dest, dcol, key, prescale=None):
    P = F // 128
    kb.p.dma("sync", kb.vspread[0:P, 0:128],
             vec_ap.rearrange("(a j) -> a j", j=128),
             writes=["vspread"], name="colspread", ch=1)
    kb.tr(kb.psS[0:128, 0:P], kb.vspread[0:P, 0:128], kb.ident_f[0:P, 0:P],
          reads=["vspread"], writes=["psS"], name="coltr", ident_key="idf")
    kb.vop(lambda e, o=dest[:, dcol:dcol + P], i=kb.psS[0:128, 0:P]:
           e.tensor_copy(o, i),
           reads=["psS"], writes=[key], name="colcopy")
    if prescale is not None:
        kb.vop(lambda e, o=dest[:, dcol:dcol + P], s=prescale:
               e.tensor_scalar_mul(o, o, s),
               reads=[key], writes=[key], name="colscale")


def load_row(kb, row_tile, vec_ap, F, key):
    kb.p.dma("gpsimd", row_tile[0:1, 0:F], vec_ap[None, :], writes=[key],
             name="rowload", ch=2)


def load_wblock(kb, dram_ap_2d, dest, dcol, key, engine="gpsimd", ch=0):
    R, W = dram_ap_2d.shape
    a = R // 128
    kb.p.dma(engine,
             dest[:, dcol:dcol + a * W].rearrange("p (a j) -> p a j", a=a),
             dram_ap_2d.rearrange("(a p) j -> p a j", p=128),
             writes=[key] if isinstance(key, str) else key, name="wblock", ch=ch)



EPS_LN = 1e-5
EPS_RMS = 1e-6


def _ln_stats(kb, src, srckey, c, eps, rms=False, dst=None):
    p = kb.p
    rstd_t, nmr_t, kp = (kb.rstd, kb.nmr, "") if dst is None else dst
    kb.act(kb.zbf[:, :], src, AF.Square, accum_out=kb.sumsq[:, c:c + 1],
           reads=[srckey], writes=["zbf", f"sumsq{c}"], name="sq")
    if not rms:
        kb.act(kb.zbf[:, :], src, AF.Identity, scale=-1.0,
               accum_out=kb.nsum[:, c:c + 1],
               reads=[srckey], writes=["zbf", f"nsum{c}"], name="nsumacc")
        p.op("vector", lambda e, o=kb.tmpst[:, c:c + 1], i=kb.nsum[:, c:c + 1]:
             e.tensor_scalar_mul(o, i, 1.0 / D),
             reads=[f"nsum{c}"], writes=[f"tmpst{c}"], name="nmean")
        p.op("vector", lambda e, o=kb.var[:, c:c + 1], i=kb.tmpst[:, c:c + 1]:
             e.tensor_mul(o, i, i),
             reads=[f"tmpst{c}"], writes=[f"var{c}"], name="mean2")
        p.op("vector", lambda e, o=kb.var[:, c:c + 1], a=kb.sumsq[:, c:c + 1],
             b=kb.var[:, c:c + 1]:
             e.scalar_tensor_tensor(o, a, 1.0 / D, b, ALU.mult, ALU.subtract),
             reads=[f"sumsq{c}", f"var{c}"], writes=[f"var{c}"], name="var")
    else:
        p.op("vector", lambda e, o=kb.var[:, c:c + 1], i=kb.sumsq[:, c:c + 1]:
             e.tensor_scalar_mul(o, i, 1.0 / D),
             reads=[f"sumsq{c}"], writes=[f"var{c}"], name="varrms")
    kb.act(rstd_t[:, c:c + 1], kb.var[:, c:c + 1], AF.Ln, bias=float(eps),
           reads=[f"var{c}"], writes=[f"{kp}rstd{c}"], name="ln")
    kb.act(rstd_t[:, c:c + 1], rstd_t[:, c:c + 1], AF.Exp, scale=-0.5,
           reads=[f"{kp}rstd{c}"], writes=[f"{kp}rstd{c}"], name="exprs")
    if not rms:
        p.op("vector", lambda e, o=nmr_t[:, c:c + 1], a=kb.tmpst[:, c:c + 1],
             b=rstd_t[:, c:c + 1]: e.tensor_mul(o, a, b),
             reads=[f"tmpst{c}", f"{kp}rstd{c}"], writes=[f"{kp}nmr{c}"],
             name="nmr")


def _transpose_to_hnT(kb, c, gb=None):
    T = kb.T
    for j in range(DC):
        kb.tr(kb.psT[:, j * 128:(j + 1) * 128], kb.zbf[:, j * 128:(j + 1) * 128],
              kb.ident_bf[:, :], reads=["zbf"], writes=["psT"], name="trh")
    hview = kb.hnT[:, :].rearrange("p (a t) -> p a t", a=DC)[:, :, c * 128:(c + 1) * 128]
    pview = kb.psT[:, :].rearrange("p (a t) -> p a t", a=DC)
    hkeys = [f"hnT{j}" for j in range(DC)]
    ptk = ["psT"]
    if gb is None:
        kb.vop(lambda e, o=hview, i=pview: e.tensor_copy(o, i),
               reads=ptk, writes=hkeys, name="hevac")
    else:
        mode, g, b = gb
        for j in range(DC):
            dst = kb.hnT[:, j * T + c * 128: j * T + (c + 1) * 128]
            src = kb.psT[:, j * 128:(j + 1) * 128]
            if mode == "gb":
                kb.act(dst, src, AF.Identity, bias=b[:, j:j + 1], scale=g[:, j:j + 1],
                       reads=["psT", "gcol", "bcol"], writes=[f"hnT{j}"],
                       name="hevacgb")
            else:
                kb.act(dst, src, AF.Identity, scale=g[:, j:j + 1],
                       reads=["psT", "gcol"], writes=[f"hnT{j}"], name="hevacs")


def ln_stage(kb, kind, g1col=None, b1col=None, gate=False):
    T = kb.T
    for c in range(kb.TC):
        xc = kb.x[:, c * 1024:(c + 1) * 1024]
        xk = f"x{c}"
        if kind == "single_gb_dram":
            kb.p.dma("sync", kb.probs[:, 0:1024], kb.x1_scr[c * 128:(c + 1) * 128, :],
                     reads=[f"x1s{c}"], writes=["probs"], name="x1load", ch=0)
            _ln_stats(kb, kb.probs[:, 0:1024], "probs", c, EPS_LN)
            kb.act(kb.zbf[:, :], kb.probs[:, 0:1024], AF.Identity,
                   bias=kb.nmr[:, c:c + 1], scale=kb.rstd[:, c:c + 1],
                   reads=["probs", f"rstd{c}", f"nmr{c}"], writes=["zbf"],
                   name="appbf")
            _transpose_to_hnT(kb, c, gb=("gb", g1col, b1col))
            continue
        if kind == "double_gb":
            _ln_stats(kb, xc, xk, c, EPS_LN,
                      dst=(kb.rstd_o, kb.nmr_o, "o"))
        else:
            _ln_stats(kb, xc, xk, c, EPS_LN)
        if kind == "plain":
            if gate:
                kb.act(kb.zf[:, :], xc, AF.Identity, bias=kb.nmr[:, c:c + 1],
                       scale=kb.rstd[:, c:c + 1],
                       reads=[xk, f"rstd{c}", f"nmr{c}"], writes=["zf"], name="app32")
                kb.act(kb.zbf[:, :], kb.zf[:, :], AF.Identity,
                       reads=["zf"], writes=["zbf"], name="appbf")
                _gate_logits(kb, c)
            else:
                kb.act(kb.zbf[:, :], xc, AF.Identity, bias=kb.nmr[:, c:c + 1],
                       scale=kb.rstd[:, c:c + 1],
                       reads=[xk, f"rstd{c}", f"nmr{c}"], writes=["zbf"], name="appbf")
            _transpose_to_hnT(kb, c)
        elif kind == "single_gb":
            kb.act(kb.zbf[:, :], xc, AF.Identity, bias=kb.nmr[:, c:c + 1],
                   scale=kb.rstd[:, c:c + 1],
                   reads=[xk, f"rstd{c}", f"nmr{c}"], writes=["zbf"], name="appbf")
            _transpose_to_hnT(kb, c, gb=("gb", g1col, b1col))
        elif kind == "double_gb":
            kb.act(kb.zf[:, :], xc, AF.Identity, bias=kb.nmr_o[:, c:c + 1],
                   scale=kb.rstd_o[:, c:c + 1],
                   reads=[xk, f"orstd{c}", f"onmr{c}"], writes=["zf"],
                   name="app32")
            _ln_stats(kb, kb.zf[:, :], "zf", c, EPS_LN)
            kb.act(kb.zbf[:, :], kb.zf[:, :], AF.Identity, bias=kb.nmr[:, c:c + 1],
                   scale=kb.rstd[:, c:c + 1],
                   reads=["zf", f"rstd{c}", f"nmr{c}"], writes=["zbf"], name="appbf")
            _transpose_to_hnT(kb, c, gb=("gb", g1col, b1col))
        elif kind == "double_rms":
            kb.act(kb.zf[:, :], xc, AF.Identity, bias=kb.nmr[:, c:c + 1],
                   scale=kb.rstd[:, c:c + 1],
                   reads=[xk, f"rstd{c}", f"nmr{c}"], writes=["zf"], name="app32")
            _ln_stats(kb, kb.zf[:, :], "zf", c, EPS_RMS, rms=True)
            kb.act(kb.zbf[:, :], kb.zf[:, :], AF.Identity, scale=kb.rstd[:, c:c + 1],
                   reads=["zf", f"rstd{c}"], writes=["zbf"], name="appbf")
            _transpose_to_hnT(kb, c, gb=("s", g1col, None))
        else:
            raise ValueError(kind)


def _gate_logits(kb, c):
    for r in range(2):
        for jj in range(4):
            j = r * 4 + jj
            kb.tr(kb.psS[:, jj * 128:(jj + 1) * 128],
                  kb.zf[:, j * 128:(j + 1) * 128], kb.ident_f[:, :],
                  reads=["zf"], writes=["psS"], name="trz", ident_key="idf")
        kb.vop(lambda e, o=kb.zT[:, 0:512], i=kb.psS[:, :]:
               e.tensor_copy(o, i), reads=["psS"],
               writes=["zT"], name="zTevac")
        for jj in range(4):
            dcc = r * 4 + jj
            kb.mm(kb.psB1[:, 0:8], kb.zT[:, jj * 128:(jj + 1) * 128],
                  kb.gwT[:, dcc * 8:(dcc + 1) * 8], start=(dcc == 0),
                  stop=(dcc == DC - 1),
                  reads=["zT", "gwT"], writes=["psB1"], name="gatemm")
    kb.vop(lambda e, o=kb.logit[:, c * 8:(c + 1) * 8], i=kb.psB1[:, 0:8]:
           e.tensor_copy(o, i), reads=["psB1"], writes=[f"logit{c}"], name="logevac")



FF_WUP = (0, 8192)
FF_WDN = (16384, 8192)
FF_HID = (32768, 4096)


def ffn_stage(kb, up_loader, dn_loader, act_fn, bias_col, scale_col, coeff_fn,
              add_b2=True, bias_key=None, scale_key=None):
    p = kb.p
    T = kb.T
    for hq in range(4):
        buf = hq % 2
        up_loader(hq, buf)
        dn_loader(hq, buf)
        upbase = FF_WUP[0] + buf * 8192
        dnbase = FF_WDN[0] + buf * 8192
        upkeys = kb.ark(upbase, 8192)
        dnkeys = kb.ark(dnbase, 8192)
        hidkeys = kb.ark(FF_HID[0], 4096)
        for tg in range(kb.NTG):
            for hc in range(8):
                ps = kb.psB0 if hc % 2 == 0 else kb.psB1
                psk = "psB0" if hc % 2 == 0 else "psB1"
                for dcc in range(DC):
                    kb.mm(ps[:, :],
                          kb.ar[:, upbase + dcc * 1024 + hc * 128:
                                upbase + dcc * 1024 + (hc + 1) * 128],
                          kb.hnT[:, dcc * T + tg * 512: dcc * T + tg * 512 + 512],
                          start=(dcc == 0), stop=(dcc == DC - 1),
                          reads=upkeys + [f"hnT{dcc}"], writes=[psk], name="ffA")
                ghc = hq * 8 + hc
                bias = bias_col[:, ghc:ghc + 1] if bias_col is not None else 0.0
                scale = scale_col[:, ghc:ghc + 1] if scale_col is not None else 1.0
                brd = ([bias_key] if bias_col is not None else []) + \
                      ([scale_key] if scale_col is not None else [])
                kb.act(kb.ar[:, FF_HID[0] + hc * 512: FF_HID[0] + (hc + 1) * 512],
                       ps[:, :], act_fn, bias=bias, scale=scale,
                       reads=[psk] + brd, writes=hidkeys, name="ffact")
            for dh in range(2):
                for i in range(4):
                    pslo = i * 512
                    pkeys = kb.pbk(pslo, 512)
                    with_b2 = (hq == 0 and add_b2)
                    for hc in range(8):
                        kb.mm(kb.psBIG[:, pslo:pslo + 512],
                              kb.ar[:, FF_HID[0] + hc * 512 + i * 128:
                                    FF_HID[0] + hc * 512 + (i + 1) * 128],
                              kb.ar[:, dnbase + hc * 1024 + dh * 512:
                                    dnbase + hc * 1024 + (dh + 1) * 512],
                              start=(hc == 0), stop=(hc == 7 and not with_b2),
                              reads=hidkeys + dnkeys, writes=pkeys, name="ffB")
                    if with_b2:
                        kb.mm(kb.psBIG[:, pslo:pslo + 512], kb.ones_bf[0:1, :],
                              kb.row0[0:1, dh * 512:(dh + 1) * 512],
                              start=False, stop=True,
                              reads=["ones", "row0"], writes=pkeys, name="ffb2")
                    tc = tg * 4 + i
                    coeff = coeff_fn(tc)
                    ckeys = []
                    if isinstance(coeff, tuple):
                        coeff, ck = coeff
                        ckeys = [ck]
                    xsl = kb.x[:, tc * 1024 + dh * 512: tc * 1024 + (dh + 1) * 512]
                    p.op("vector",
                         lambda e, o=xsl, a=kb.psBIG[:, pslo:pslo + 512], s=coeff:
                         e.scalar_tensor_tensor(o, a, s, o, ALU.mult, ALU.add),
                         reads=pkeys + [f"x{tc}"] + ckeys, writes=[f"x{tc}"],
                         name="ffevac")


def simple_up_loader(kb, w_dram):
    def load(hq, buf):
        base = FF_WUP[0] + buf * 8192
        load_wblock(kb, w_dram[:, hq * 1024:(hq + 1) * 1024], kb.ar, base,
                    kb.ark(base, 8192))
    return load


def simple_dn_loader(kb, w_dram):
    def load(hq, buf):
        base = FF_WDN[0] + buf * 8192
        load_wblock(kb, w_dram[hq * 1024:(hq + 1) * 1024, :], kb.ar, base,
                    kb.ark(base, 8192), ch=1)
    return load




def _sign_stripe(kb, dram_stripe, dest_base, rc, alpha_col_dst=None,
                 prescale_col=None, dkeys=()):
    p = kb.p
    p.dma("sync", kb.zf[:, :], dram_stripe, writes=["zf"], name="sgnload")
    if alpha_col_dst is not None:
        p.op("vector", lambda e, o=alpha_col_dst, i=kb.zf[:, :]:
             e.reduce_sum(o, i, AX.X, apply_absolute_value=True),
             reads=["zf"], writes=["alpha_w"], name="absred")
        p.op("vector", lambda e, o=alpha_col_dst: e.tensor_scalar_mul(o, o, 1.0 / D),
             reads=["alpha_w"], writes=["alpha_w"], name="alphasc")
    p.op("scalar", lambda e, o=kb.zbf[:, :], i=kb.zf[:, :]: e.sign(o, i),
         reads=["zf"], writes=["zbf"], name="sign")
    if prescale_col is not None:
        p.op("vector", lambda e, o=kb.zbf[:, :], s=prescale_col:
             e.tensor_scalar_mul(o, o, s),
             reads=["zbf", "alpha2_r"], writes=["zbf"], name="sgnscale")
    for j in range(DC):
        kb.tr(kb.psT[:, j * 128:(j + 1) * 128], kb.zbf[:, j * 128:(j + 1) * 128],
              kb.ident_bf[:, :], reads=["zbf"], writes=["psT"], name="sgntr")
    dview = kb.ar[:, dest_base: dest_base + 8192] \
        .rearrange("p (a j) -> p a j", a=8)[:, :, rc * 128:(rc + 1) * 128]
    pview = kb.psT[:, :].rearrange("p (a j) -> p a j", a=DC)
    kb.vop(lambda e, o=dview, i=pview: e.tensor_copy(o, i),
           reads=["psT"], writes=list(dkeys), name="sgnevac")


def sign_up_loader(kb, w1_dram, scr_up):
    def load(hq, buf):
        base = FF_WUP[0] + buf * 8192
        keys = kb.ark(base, 8192)
        for rc in range(8):
            ghc = hq * 8 + rc
            _sign_stripe(kb, w1_dram[ghc * 128:(ghc + 1) * 128, :], base, rc,
                         alpha_col_dst=kb.alpha1[:, ghc:ghc + 1], dkeys=keys)
        kb.p.dma("sync", scr_up[hq, :, :], kb.ar[:, base:base + 8192],
                 reads=keys, writes=[f"scru{hq}"], name="scrspill")
    return load


def sign_dn_loader(kb, w2_dram, scr_dn):
    def load(hq, buf):
        base = FF_WDN[0] + buf * 8192
        keys = kb.ark(base, 8192)
        for rc in range(8):
            _sign_stripe(kb, w2_dram[rc * 128:(rc + 1) * 128,
                                     hq * 1024:(hq + 1) * 1024],
                         base, rc, prescale_col=kb.alpha2[:, rc:rc + 1], dkeys=keys)
        kb.p.dma("sync", scr_dn[hq, :, :], kb.ar[:, base:base + 8192],
                 reads=keys, writes=[f"scrd{hq}"], name="scrspill2")
    return load


def alpha2_prepass(kb, w2_dram):
    p = kb.p
    for rc in range(8):
        for hh in range(4):
            p.dma("sync", kb.probs[:, 0:1024],
                  w2_dram[rc * 128:(rc + 1) * 128, hh * 1024:(hh + 1) * 1024],
                  writes=["probs"], name="a2load")
            p.op("vector", lambda e, o=kb.tmpst[:, hh:hh + 1], i=kb.probs[:, 0:1024]:
                 e.reduce_sum(o, i, AX.X, apply_absolute_value=True),
                 reads=["probs"], writes=[f"a2p{hh}"], name="a2red")
        p.op("vector", lambda e, o=kb.tmpst[:, 0:1], a=kb.tmpst[:, 0:1],
             b=kb.tmpst[:, 1:2]: e.tensor_add(o, a, b),
             reads=["a2p0", "a2p1"], writes=["a2p0"], name="a2add")
        p.op("vector", lambda e, o=kb.tmpst[:, 2:3], a=kb.tmpst[:, 2:3],
             b=kb.tmpst[:, 3:4]: e.tensor_add(o, a, b),
             reads=["a2p2", "a2p3"], writes=["a2p2"], name="a2add2")
        p.op("vector", lambda e, o=kb.alpha2[:, rc:rc + 1], a=kb.tmpst[:, 0:1],
             b=kb.tmpst[:, 2:3]: e.tensor_add(o, a, b),
             reads=["a2p0", "a2p2"], writes=["alpha2_r"], name="a2add3")
        p.op("vector", lambda e, o=kb.alpha2[:, rc:rc + 1]:
             e.tensor_scalar_mul(o, o, 1.0 / DFF),
             reads=["alpha2_r"], writes=["alpha2_r"], name="a2sc")


def scratch_up_loader(kb, scr_up):
    def load(hq, buf):
        base = FF_WUP[0] + buf * 8192
        kb.p.dma("gpsimd", kb.ar[:, base:base + 8192], scr_up[hq, :, :],
                 reads=[f"scru{hq}"], writes=kb.ark(base, 8192), name="scrload",
                 ch=0)
    return load


def scratch_dn_loader(kb, scr_dn):
    def load(hq, buf):
        base = FF_WDN[0] + buf * 8192
        kb.p.dma("gpsimd", kb.ar[:, base:base + 8192], scr_dn[hq, :, :],
                 reads=[f"scrd{hq}"], writes=kb.ark(base, 8192), name="scrload2",
                 ch=1)
    return load




def moe_prep_gate(kb, gate_w_dram):
    for dcc in range(DC):
        kb.p.dma("sync", kb.vspread[0:8, 0:128],
                 gate_w_dram[:, dcc * 128:(dcc + 1) * 128],
                 writes=["vspread"], name="gwload", ch=1)
        kb.tr(kb.psS[:, dcc * 8:(dcc + 1) * 8],
              kb.vspread[0:8, 0:128], kb.ident_f[0:8, 0:8],
              reads=["vspread"], writes=["psS"], name="gwtr", ident_key="idf")
    kb.vop(lambda e, o=kb.gwT[:, 0:64], i=kb.psS[:, 0:64]: e.tensor_copy(o, i),
           reads=["psS"], writes=["gwT"], name="gwevac")


def moe_gate_softmax(kb):
    p = kb.p
    p.op("vector", lambda e: e.memset(kb.found[:, 0:8], 1.0),
         writes=["ones8"], name="ones8")
    for tc in range(kb.TC):
        L = kb.logit[:, tc * 8:(tc + 1) * 8]
        lk = f"logit{tc}"
        nm = kb.negmax[:, tc:tc + 1]
        es = kb.esum[:, tc:tc + 1]
        p.op("vector", lambda e, o=nm, i=L: e.reduce_max(o, i, AX.X, negate=True),
             reads=[lk], writes=[f"nm{tc}"], name="gmax")
        kb.act(kb.zf[:, tc * 8:(tc + 1) * 8], L, AF.Exp, bias=nm,
               accum_out=es, reads=[lk, f"nm{tc}"], writes=["zf", f"es{tc}"],
               name="gexp")
        p.op("vector", lambda e, o=kb.recip[:, tc:tc + 1], i=es: e.reciprocal(o, i),
             reads=[f"es{tc}"], writes=[f"rc{tc}"], name="grecip")
        p.op("vector", lambda e, o=kb.selw[:, tc:tc + 1], i=kb.recip[:, tc:tc + 1]:
             e.tensor_scalar_mul(o, i, 0.5),
             reads=[f"rc{tc}"], writes=[f"sw{tc}"], name="gselw")
        p.op("vector", lambda e, o=kb.tmpst[:, tc:tc + 1], i=nm:
             e.tensor_scalar_mul(o, i, -1.0),
             reads=[f"nm{tc}"], writes=[f"gm{tc}"], name="gmaxpos")
        eq = kb.zf[:, 128 + tc * 8:128 + (tc + 1) * 8]
        neq = kb.zf[:, 256 + tc * 8:256 + (tc + 1) * 8]
        cp = kb.zf[:, 384 + tc * 8:384 + (tc + 1) * 8]
        sel = kb.zf[:, 512 + tc * 8:512 + (tc + 1) * 8]
        p.op("vector", lambda e, o=eq, i=L, s=kb.tmpst[:, tc:tc + 1]:
             e.tensor_single_scalar(o, i, s, ALU.is_equal),
             reads=[lk, f"gm{tc}"], writes=["zf"], name="geq")
        p.op("vector", lambda e, o=neq, i=eq:
             e.tensor_scalar(o, i, -1.0, 1.0, ALU.mult, ALU.add),
             reads=["zf"], writes=["zf"], name="gneq")
        p.op("vector", lambda e, o=cp, a=neq, b=kb.found[:, 0:8]:
             e.tensor_tensor_scan(o, a, b, 1.0, ALU.mult, ALU.mult),
             reads=["zf", "ones8"], writes=["zf"], name="gcum")
        p.op("vector", lambda e, o=sel[:, 1:8], a=eq[:, 1:8], b=cp[:, 0:7]:
             e.tensor_mul(o, a, b), reads=["zf"], writes=["zf"], name="gsel")
        p.op("vector", lambda e, o=sel[:, 0:1], i=eq[:, 0:1]: e.tensor_copy(o, i),
             reads=["zf"], writes=["zf"], name="gsel0")
        cview = kb.coeff[:, :].rearrange("p (e t) -> p t e", t=16)[:, tc, :]
        p.op("vector", lambda e, o=cview, i=sel, s=kb.selw[:, tc:tc + 1]:
             e.tensor_single_scalar(o, i, s, ALU.mult),
             reads=["zf", f"sw{tc}"], writes=[f"coef{tc}"], name="gcoef")


def moe_experts(kb, w1_3d, w2_3d, b1_2d, b2_2d):
    for ex in range(NE):
        load_col_tile(kb, b1_2d[ex], DFF, kb.bup_col, 0, "bup")
        load_row(kb, kb.row0, b2_2d[ex], 1024, "row0")
        up = simple_up_loader(kb, w1_3d[ex])
        dn = simple_dn_loader(kb, w2_3d[ex])

        def coeff_fn(tc, ex=ex):
            return (kb.coeff[:, ex * 16 + tc: ex * 16 + tc + 1], f"coef{tc}")
        ffn_stage(kb, up, dn, AF.Relu, kb.bup_col, None, coeff_fn, add_b2=True,
                  bias_key="bup", scale_key=None)




def _v_phase(kb, bias_row, bias_key):
    T = kb.T
    vbase = 8192
    wkeys = kb.ark(0, 8192)
    for tc in range(kb.TC):
        for nh in range(2):
            for dcc in range(DC):
                kb.mm(kb.psBIG[:, nh * 512:(nh + 1) * 512],
                      kb.hnT[:, dcc * T + tc * 128: dcc * T + (tc + 1) * 128],
                      kb.ar[:, dcc * 1024 + nh * 512: dcc * 1024 + (nh + 1) * 512],
                      start=(dcc == 0), stop=(dcc == DC - 1 and bias_row is None),
                      reads=wkeys + [f"hnT{dcc}"], writes=kb.pbk(nh * 512, 512),
                      name="vmm")
            if bias_row is not None:
                kb.mm(kb.psBIG[:, nh * 512:(nh + 1) * 512], kb.ones_bf[0:1, :],
                      bias_row[0:1, nh * 512:(nh + 1) * 512], start=False, stop=True,
                      reads=["ones", bias_key], writes=kb.pbk(nh * 512, 512),
                      name="vbias")
        kb.act(kb.ar[:, vbase + tc * 1024: vbase + (tc + 1) * 1024],
               kb.psBIG[:, 0:1024], AF.Identity,
               reads=kb.pbk(0, 1024), writes=kb.ark(vbase + tc * 1024, 1024),
               name="vevac")


def _qk_pair(kb, wq_flat, wk_flat, j, qbias, kbias):
    T = kb.T
    wqb, wkb = 2 * T, 2 * T + 1024
    load_wblock(kb, wq_flat[:, j * 128:(j + 1) * 128], kb.ar, wqb,
                kb.ark(wqb, 1024), ch=0)
    load_wblock(kb, wk_flat[:, j * 128:(j + 1) * 128], kb.ar, wkb,
                kb.ark(wkb, 1024), ch=1)
    for tgq in range(T // 512):
        for dcc in range(DC):
            kb.mm(kb.psB0[:, :], kb.ar[:, wqb + dcc * 128: wqb + (dcc + 1) * 128],
                  kb.hnT[:, dcc * T + tgq * 512: dcc * T + tgq * 512 + 512],
                  start=(dcc == 0), stop=(dcc == DC - 1),
                  reads=kb.ark(wqb, 1024) + [f"hnT{dcc}"], writes=["psB0"],
                  name="qmm")
        kb.act(kb.ar[:, tgq * 512:(tgq + 1) * 512], kb.psB0[:, :], AF.Identity,
               bias=(qbias[0][:, j:j + 1] if qbias else 0.0), scale=0.125,
               reads=["psB0"] + ([qbias[1]] if qbias else []),
               writes=kb.ark(tgq * 512, 512), name="qevac")
        for dcc in range(DC):
            kb.mm(kb.psB1[:, :], kb.ar[:, wkb + dcc * 128: wkb + (dcc + 1) * 128],
                  kb.hnT[:, dcc * T + tgq * 512: dcc * T + tgq * 512 + 512],
                  start=(dcc == 0), stop=(dcc == DC - 1),
                  reads=kb.ark(wkb, 1024) + [f"hnT{dcc}"], writes=["psB1"],
                  name="kmm")
        kb.act(kb.ar[:, T + tgq * 512: T + (tgq + 1) * 512], kb.psB1[:, :], AF.Identity,
               bias=(kbias[0][:, j:j + 1] if kbias else 0.0),
               reads=["psB1"] + ([kbias[1]] if kbias else []),
               writes=kb.ark(T + tgq * 512, 512), name="kevac")


def _l0_head_A(kb, j, h_loc, qc, ci):
    T = kb.T
    par = ci % 2
    s = ci % 32
    nm, nmk = kb.stat("nm", s)
    es, esk = kb.stat("es", s)
    rc, rck = kb.stat("rc", s)
    qk = kb.ark(qc * 128, 128)
    q_ap = kb.ar[h_loc:h_loc + 64, qc * 128:(qc + 1) * 128]
    nkc = 1 if qc == 0 else 2
    width = nkc * 128
    if qc == 0:
        kb.mm(kb.psS[:, par * 256: par * 256 + 128], q_ap,
              kb.ar[h_loc:h_loc + 64, T:T + 128],
              start=True, stop=True, reads=qk + kb.ark(T, 128),
              writes=["psS"], name="smm")
        mview = kb.mask[:, 128:256]
    else:
        kb.mm(kb.psS[:, par * 256: par * 256 + 256], q_ap,
              kb.ar[h_loc:h_loc + 64, T + (qc - 1) * 128: T + (qc + 1) * 128],
              start=True, stop=True,
              reads=qk + kb.ark(T + (qc - 1) * 128, 256), writes=["psS"],
              name="smm")
        mview = kb.mask[:, 0:256]
    pr = kb.probs[:, par * 512: par * 512 + width]
    prk = f"probs{par}"
    kb.p.op("vector", lambda e, o=pr, a=kb.psS[:, par * 256: par * 256 + width],
            b=mview: e.tensor_add(o, a, b),
            reads=["psS", "mask"], writes=[prk], name="smask")
    kb.p.op("vector", lambda e, o=nm, i=pr: e.reduce_max(o, i, AX.X, negate=True),
            reads=[prk], writes=[nmk], name="smax")
    P = kb.zbf[:, par * 512: par * 512 + width]
    Pk = f"zbfp{par}"
    kb.act(P, pr, AF.Exp, bias=nm, accum_out=es,
           reads=[prk, nmk], writes=[Pk, esk], name="sexp")
    kb.p.op("vector", lambda e, o=rc, i=es: e.reciprocal(o, i),
            reads=[esk], writes=[rck], name="srecip")
    kb.p.op("gpsimd", lambda e, o=P, s_=rc: e.tensor_scalar_mul(o, o, s_),
            reads=[Pk, rck], writes=[Pk], name="snormv")


def _l0_head_B(kb, j, h_loc, qc, ci):
    T = kb.T
    vbase, aobase = 8192, 8192 + 8 * T
    head = 2 * j + (1 if h_loc else 0)
    par = ci % 2
    nkc = 1 if qc == 0 else 2
    width = nkc * 128
    Pk = f"zbfp{par}"
    PTk = f"zbft{par}"
    for kk in range(nkc):
        slot = par * 2 + kk
        kb.tr(kb.psT[:, slot * 128:(slot + 1) * 128],
              kb.zbf[:, par * 512 + kk * 128: par * 512 + (kk + 1) * 128],
              kb.ident_bf[:, :], reads=[Pk], writes=["psT"], name="str")
    kb.vop(lambda e, o=kb.zbf[:, par * 512 + 256: par * 512 + 256 + width],
           i=kb.psT[:, par * 256: par * 256 + width]: e.tensor_copy(o, i),
           reads=["psT"], writes=[PTk],
           name="sTevac")
    psA = kb.psB0 if par == 0 else kb.psB1
    psAk = "psB0" if par == 0 else "psB1"
    for kk in range(nkc):
        kc = qc - 1 + kk if qc > 0 else 0
        kb.mm(psA[h_loc:h_loc + 64, 0:128],
              kb.ar[:, vbase + kc * 1024 + head * 64: vbase + kc * 1024 + head * 64 + 64],
              kb.zbf[:, par * 512 + 256 + kk * 128: par * 512 + 256 + (kk + 1) * 128],
              start=(kk == 0), stop=(kk == nkc - 1),
              reads=kb.ark(vbase + kc * 1024, 1024) + [PTk], writes=[psAk],
              name="aomm", tile_position=(0, h_loc))
    kb.act(kb.ar[h_loc:h_loc + 64, aobase + j * T + qc * 128:
                 aobase + j * T + (qc + 1) * 128],
           psA[h_loc:h_loc + 64, 0:128], AF.Identity,
           reads=[psAk], writes=kb.ark(aobase + j * T + qc * 128, 128),
           name="aoevac")


def _ssm_u_scan(kb, acol_key="acol"):
    T = kb.T
    p = kb.p
    p.op("vector", lambda e: e.memset(kb.st_sb[:, 0:512], 1.0),
         writes=["st"], name="stones")
    p.op("vector", lambda e, o=kb.ab_sb[:, 0:512], i=kb.st_sb[:, 0:512],
         s=kb.acol[:, 0:1]: e.tensor_single_scalar(o, i, s, ALU.mult),
         reads=["st", acol_key], writes=["ab"], name="abmul")
    for s in range(T // 512):
        ps = kb.psB0 if s % 2 == 0 else kb.psB1
        psk = "psB0" if s % 2 == 0 else "psB1"
        for dcc in range(DC):
            kb.mm(ps[0:64, :], kb.bw_sb[:, dcc * 64:(dcc + 1) * 64],
                  kb.hnT[:, dcc * T + s * 512: dcc * T + s * 512 + 512],
                  start=(dcc == 0), stop=(dcc == DC - 1),
                  reads=["bw", f"hnT{dcc}"], writes=[psk], name="umm")
        init = 0.0 if s == 0 else kb.st_sb[:, s * 512 - 1: s * 512]
        p.op("vector",
             lambda e, o=kb.st_sb[:, s * 512:(s + 1) * 512],
             a=kb.ab_sb[:, 0:512], b=ps[0:64, :], ini=init:
             e.tensor_tensor_scan(o, a, b, ini, ALU.mult, ALU.add),
             reads=["ab", psk, "st"], writes=["st"], name="scan")


def _l0_combine(kb, tc, wg_base=8192):
    T = kb.T
    aobase = 8192 + 8 * T
    p = kb.p
    kb.mm(kb.psB0[:, :], kb.st_sb[:, tc * 128:(tc + 1) * 128],
          kb.cw_sb[:, 0:512], start=True, stop=True,
          reads=["st", "cw"], writes=["psB0"], name="ssmmm0")
    kb.mm(kb.psB1[:, :], kb.st_sb[:, tc * 128:(tc + 1) * 128],
          kb.cw_sb[:, 512:1024], start=True, stop=True,
          reads=["st", "cw"], writes=["psB1"], name="ssmmm1")
    kb.act(kb.probs[:, 0:512], kb.psB0[:, :], AF.Identity,
           reads=["psB0"], writes=["probs"], name="ssmev0")
    kb.act(kb.probs[:, 512:1024], kb.psB1[:, :], AF.Identity,
           reads=["psB1"], writes=["probs"], name="ssmev1")
    for nh in range(2):
        pk = kb.pbk(nh * 512, 512)
        for j in range(NPAIR):
            kb.mm(kb.psBIG[:, nh * 512:(nh + 1) * 512],
                  kb.ar[:, aobase + j * T + tc * 128: aobase + j * T + (tc + 1) * 128],
                  kb.ar[:, j * 1024 + nh * 512: j * 1024 + (nh + 1) * 512],
                  start=(j == 0), stop=False,
                  reads=kb.ark(aobase + j * T + tc * 128, 128) +
                  kb.ark(j * 1024 + nh * 512, 512),
                  writes=pk, name="prmm")
        kb.mm(kb.psBIG[:, nh * 512:(nh + 1) * 512], kb.ones_bf[0:1, :],
              kb.row2[0:1, nh * 512:(nh + 1) * 512], start=False, stop=True,
              reads=["ones", "row2"], writes=pk, name="prbias")
    for nh in range(2):
        pk = kb.pbk(1024 + nh * 512, 512)
        for dcc in range(DC):
            kb.mm(kb.psBIG[:, 1024 + nh * 512: 1024 + (nh + 1) * 512],
                  kb.hnT[:, dcc * T + tc * 128: dcc * T + (tc + 1) * 128],
                  kb.ar[:, wg_base + dcc * 1024 + nh * 512:
                        wg_base + dcc * 1024 + (nh + 1) * 512],
                  start=(dcc == 0), stop=False,
                  reads=[f"hnT{dcc}"] + kb.ark(wg_base + dcc * 1024 + nh * 512, 512),
                  writes=pk, name="gmm")
        kb.mm(kb.psBIG[:, 1024 + nh * 512: 1024 + (nh + 1) * 512], kb.ones_bf[0:1, :],
              kb.row1[0:1, nh * 512:(nh + 1) * 512], start=False, stop=True,
              reads=["ones", "row1"], writes=pk, name="gbias")
    kb.act(kb.zf[:, :], kb.psBIG[:, 1024:2048], AF.Sigmoid,
           reads=kb.pbk(1024, 1024), writes=["zf"], name="gsig")
    xk = f"x{tc}"
    xv = kb.x[:, tc * 1024:(tc + 1) * 1024]
    for hh in range(2):
        xh = kb.x[:, tc * 1024 + hh * 512: tc * 1024 + (hh + 1) * 512]
        ph = kb.probs[:, hh * 512:(hh + 1) * 512]
        p.op("vector", lambda e, o=kb.zT[:, :],
             a=kb.psBIG[:, hh * 512:(hh + 1) * 512], b=ph:
             e.tensor_sub(o, a, b),
             reads=kb.pbk(hh * 512, 512) + ["probs"], writes=["zT"], name="csub")
        p.op("vector", lambda e, o=kb.zT[:, :], a=kb.zT[:, :],
             b=kb.zf[:, hh * 512:(hh + 1) * 512]:
             e.tensor_mul(o, a, b), reads=["zT", "zf"], writes=["zT"], name="cmul")
        p.op("vector", lambda e, o=ph, a=ph, b=kb.zT[:, :]:
             e.tensor_add(o, a, b), reads=["probs", "zT"], writes=["probs"],
             name="ccombo")
        kb.act(kb.zT[:, :], xh, AF.Identity, bias=kb.nmr_o[:, tc:tc + 1],
               scale=kb.rstd_o[:, tc:tc + 1],
               reads=[xk, f"orstd{tc}", f"onmr{tc}"], writes=["zT"], name="crez")
        p.op("vector", lambda e, o=ph, a=ph, b=kb.zT[:, :]:
             e.tensor_add(o, a, b), reads=["probs", "zT"], writes=["probs"],
             name="caddz")
    p.dma("sync", kb.x1_scr[tc * 128:(tc + 1) * 128, :], kb.probs[:, 0:1024],
          reads=["probs"], writes=[f"x1s{tc}"], name="x1spill", ch=0)
    p.op("vector", lambda e, o=xv, a=xv, b=kb.probs[:, 0:1024]:
         e.tensor_add(o, a, b), reads=[xk, "probs"], writes=[xk], name="cadd1")


def _outproj_add(kb, tc, bias_row, bias_key):
    T = kb.T
    aobase = 8192 + 8 * T
    for nh in range(2):
        pk = kb.pbk(nh * 512, 512)
        for j in range(NPAIR):
            kb.mm(kb.psBIG[:, nh * 512:(nh + 1) * 512],
                  kb.ar[:, aobase + j * T + tc * 128: aobase + j * T + (tc + 1) * 128],
                  kb.ar[:, j * 1024 + nh * 512: j * 1024 + (nh + 1) * 512],
                  start=(j == 0), stop=False,
                  reads=kb.ark(aobase + j * T + tc * 128, 128) +
                  kb.ark(j * 1024 + nh * 512, 512),
                  writes=pk, name="pr1mm")
        kb.mm(kb.psBIG[:, nh * 512:(nh + 1) * 512], kb.ones_bf[0:1, :],
              bias_row[0:1, nh * 512:(nh + 1) * 512], start=False, stop=True,
              reads=["ones", bias_key], writes=pk, name="pr1bias")
    xk = f"x{tc}"
    xv = kb.x[:, tc * 1024:(tc + 1) * 1024]
    kb.p.op("vector", lambda e, o=xv, a=kb.psBIG[:, 0:1024]:
            e.scalar_tensor_tensor(o, a, 1.0, o, ALU.mult, ALU.add),
            reads=kb.pbk(0, 1024) + [xk], writes=[xk], name="pr1evac")


def attn_l0(kb, W, x1_scr):
    T = kb.T
    kb.x1_scr = x1_scr
    load_row(kb, kb.row1, W["bV"], 1024, "row1")
    load_row(kb, kb.row2, W["bO"], 1024, "row2")
    load_col_tile(kb, W["bQ"], 1024, kb.qkvcol, 0, "qcol", prescale=0.125)
    load_col_tile(kb, W["bK"], 1024, kb.qkvcol, 8, "kcol")
    kb.p.dma("sync", kb.acol[:, 0:1], W["A"][:, None], writes=["acol"],
             name="aload", ch=1)
    kb.p.dma("gpsimd", kb.cw_sb[:, :], W["Cw"][:, :], writes=["cw"],
             name="cwload", ch=2)
    load_wblock(kb, W["Bw"][:, :], kb.bw_sb, 0, ["bw"], ch=1)
    load_wblock(kb, W["WV"][:, :], kb.ar, 0, kb.ark(0, 8192))
    _v_phase(kb, kb.row1, "row1")
    _ssm_u_scan(kb)
    ci = 0
    queue = []
    LAG0 = 2
    for j in range(NPAIR):
        _qk_pair(kb, W["WQ"][:, :], W["WK"][:, :], j,
                 (kb.qkvcol[:, 0:8], "qcol"), (kb.qkvcol[:, 8:16], "kcol"))
        for h_loc in (0, 64):
            for qc in range(kb.TC):
                if len(queue) >= LAG0:
                    _l0_head_B(kb, *queue.pop(0))
                _l0_head_A(kb, j, h_loc, qc, ci)
                queue.append((j, h_loc, qc, ci))
                ci += 1
    for it in queue:
        _l0_head_B(kb, *it)
    if not globals().get("ATTN0_SKIP_P3", False):
        wg_base = 8192 if 8 * T >= 8192 else 8192 + 16 * T
        load_row(kb, kb.row1, W["bg"], 1024, "row1")
        load_wblock(kb, W["WO"][:, :], kb.ar, 0, kb.ark(0, 8192))
        load_wblock(kb, W["Wg"][:, :], kb.ar, wg_base, kb.ark(wg_base, 8192), ch=1)
        for tc in range(kb.TC):
            _l0_combine(kb, tc, wg_base)


def attn_l1(kb, W):
    T = kb.T
    load_row(kb, kb.row2, W["b_out"], 1024, "row2")
    wqf = W["wq"][:, :, :].rearrange("d h e -> d (h e)")
    wkf = W["wk"][:, :, :].rearrange("d h e -> d (h e)")
    wvf = W["wv"][:, :, :].rearrange("d h e -> d (h e)")
    load_wblock(kb, wvf, kb.ar, 0, kb.ark(0, 8192))
    _v_phase(kb, None, None)
    ci = 0
    queue = []
    LAG1 = 3
    for j in range(A1_NPAIR):
        _qk_pair(kb, wqf, wkf, j, None, None)
        for h_loc in (0, 64):
            for qc in range(kb.TC):
                if len(queue) >= LAG1:
                    _l1_head_B(kb, *queue.pop(0))
                _l1_head_A(kb, j, h_loc, qc, ci)
                queue.append((j, h_loc, qc, ci))
                ci += 1
    for it in queue:
        _l1_head_B(kb, *it)
    kb.p.op("vector", lambda e, o=kb.zf[:, 0:1]: e.tensor_copy(o, kb.zf[:, 0:1]),
            reads=["zfb0", "zfb1", "zfb2", "zfb3",
                   "prb0", "prb1", "prb2", "prb3"], writes=["zf", "probs"],
            name="zfence")
    if not globals().get("ATTN1_SKIP_PROJ", False):
        for j in range(NPAIR):
            kb.p.dma("gpsimd", kb.ar[:, j * 1024:(j + 1) * 1024],
                     W["w_out"][2 * j:2 * j + 2, :, :].rearrange("h e d -> (h e) d"),
                     writes=kb.ark(j * 1024, 1024), name="woload", ch=1)
        for tc in range(kb.TC):
            _outproj_add(kb, tc, kb.row2, "row2")


def _l1_pbuf(kb, par):
    if par == 0:
        base = 2 * kb.T + 2048
        ap = lambda lo, n: kb.ar[:, base + lo: base + lo + n]
        keys = lambda lo, n: kb.ark(base + lo, n)
    elif par == 1:
        zb = kb.zf[:, :].bitcast(BF16)
        ap = lambda lo, n, z=zb: z[:, lo: lo + n]
        keys = lambda lo, n: ["zfb%d" % i for i in
                              range(lo // 512, (lo + n + 511) // 512)]
    else:
        pb = kb.probs[:, :].bitcast(BF16)
        ap = lambda lo, n, z=pb: z[:, lo: lo + n]
        keys = lambda lo, n: ["prb%d" % i for i in
                              range(lo // 512, (lo + n + 511) // 512)]
    return ap, keys


def _l1_head_A(kb, j, h_loc, qc, ci):
    T = kb.T
    par = ci % 2
    s = ci % 32
    nm, nmk = kb.stat("nm", s)
    es, esk = kb.stat("es", s)
    rc, rck = kb.stat("rc", s)
    c4 = (ci % 4) * 4
    pap, pkeys = _l1_pbuf(kb, ci % 3)
    q_ap = kb.ar[h_loc:h_loc + 64, qc * 128:(qc + 1) * 128]
    for sq in range(4):
        kb.mm(kb.psBIG[:, sq * 512:(sq + 1) * 512], q_ap,
              kb.ar[h_loc:h_loc + 64, T + sq * 512: T + (sq + 1) * 512],
              start=True, stop=True,
              reads=kb.ark(qc * 128, 128) + kb.ark(T + sq * 512, 512),
              writes=[f"pb{sq}"], name="smm1")
        kb.p.op("vector", lambda e, o=kb.rstd_o[:, c4 + sq:c4 + sq + 1],
                i=kb.psBIG[:, sq * 512:(sq + 1) * 512]:
                e.reduce_max(o, i, AX.X),
                reads=[f"pb{sq}"], writes=[f"orstd{c4 + sq}"], name="qmax")
    kb.p.op("vector", lambda e, o=nm, i=kb.rstd_o[:, c4:c4 + 4]:
            e.reduce_max(o, i, AX.X, negate=True),
            reads=[f"orstd{c4 + sq}" for sq in range(4)], writes=[nmk],
            name="gmax1")
    for sq in range(4):
        kb.act(pap(sq * 512, 512), kb.psBIG[:, sq * 512:(sq + 1) * 512],
               AF.Exp, bias=nm, accum_out=kb.nmr_o[:, c4 + sq:c4 + sq + 1],
               reads=[f"pb{sq}", nmk],
               writes=pkeys(sq * 512, 512) + [f"onmr{c4 + sq}"], name="sexp1")
    kb.p.op("vector", lambda e, o=es, i=kb.nmr_o[:, c4:c4 + 4]:
            e.reduce_sum(o, i, AX.X),
            reads=[f"onmr{c4 + sq}" for sq in range(4)], writes=[esk],
            name="esum1")
    kb.p.op("vector", lambda e, o=rc, i=es: e.reciprocal(o, i),
            reads=[esk], writes=[rck], name="srecip1")
    for sq in range(4):
        kb.p.op("gpsimd", lambda e, o=pap(sq * 512, 512), s_=rc:
                e.tensor_scalar_mul(o, o, s_),
                reads=pkeys(sq * 512, 512) + [rck],
                writes=pkeys(sq * 512, 512), name="snorm1")


def _l1_head_B(kb, j, h_loc, qc, ci):
    T = kb.T
    vbase, aobase = 8192, 8192 + 8 * T
    head = 2 * j + (1 if h_loc else 0)
    par = ci % 2
    pap, pkeys = _l1_pbuf(kb, ci % 3)
    for r in range(2):
        for kk in range(8):
            c = r * 8 + kk
            kb.tr(kb.psT[:, kk * 128:(kk + 1) * 128], pap(c * 128, 128),
                  kb.ident_bf[:, :], reads=pkeys(c * 128, 128), writes=["psT"],
                  name="str1")
        kb.vop(lambda e, o=pap(r * 1024, 1024), i=kb.psT[:, 0:1024]:
               e.tensor_copy(o, i),
               reads=["psT"], writes=pkeys(r * 1024, 1024), name="sTevac1")
    psA = kb.psB0 if par == 0 else kb.psB1
    psAk = "psB0" if par == 0 else "psB1"
    for kc in range(kb.TC):
        kb.mm(psA[h_loc:h_loc + 64, 0:128],
              kb.ar[:, vbase + kc * 1024 + head * 64: vbase + kc * 1024 + head * 64 + 64],
              pap(kc * 128, 128),
              start=(kc == 0), stop=(kc == kb.TC - 1),
              reads=kb.ark(vbase + kc * 1024, 1024) + pkeys(kc * 128, 128),
              writes=[psAk], name="aomm1", tile_position=(0, h_loc))
    kb.act(kb.ar[h_loc:h_loc + 64, aobase + j * T + qc * 128:
                 aobase + j * T + (qc + 1) * 128],
           psA[h_loc:h_loc + 64, 0:128], AF.Identity,
           reads=[psAk], writes=kb.ark(aobase + j * T + qc * 128, 128),
           name="aoevac1")




def load_x(kb, x_dram):
    kb.p.dma("sync", kb.x[:, :].rearrange("p (c j) -> p c j", c=kb.TC),
             x_dram[:, :].rearrange("(c p) j -> p c j", p=128),
             writes=[f"x{c}" for c in range(kb.TC)], name="xload")


def store_y(kb, y_dram):
    kb.p.dma("sync", y_dram[:, :].rearrange("(c p) j -> p c j", p=128),
             kb.x[:, :].rearrange("p (c j) -> p c j", c=kb.TC),
             reads=[f"x{c}" for c in range(kb.TC)], name="ystore")
    kb.p.barrier_all_dma()


def load_consts(kb, cid_bf, cid_f, cmask):
    kb.p.dma("gpsimd", kb.ident_bf[:, :], cid_bf[:, :], writes=["idb"],
             name="idbload", ch=2)
    kb.p.dma("sync", kb.ident_f[:, :], cid_f[:, :], writes=["idf"],
             name="idfload", ch=1)
    kb.p.dma("gpsimd", kb.mask[:, :], cmask[:, :], writes=["mask"],
             name="maskload", ch=2)
    kb.p.op("vector", lambda e: e.memset(kb.ones_bf[0:1, :], 1.0),
            writes=["ones"], name="onesset")


STAGE_NAMES = ("ffnA", "attn0", "ffnB", "moe1", "attn1", "moe2")

A1_OLDMAX = False
A1_NPAIR = NPAIR
A1_NOACCUM = False
A1_SKIPNORM = False
A1_NOPAR = False
A1_NOSTATB = False
A1_CUT = 5

DEBUG_TAPS = []


def build_program(T=2048, stages=STAGE_NAMES, n_moe_experts=NE):
    import contextlib
    ctx = contextlib.ExitStack()
    nc = bass.Bass("TRN2", target_bir_lowering=False)
    p = Plan()

    dt = {}

    def din(name, shape):
        dt[name] = nc.dram_tensor(name, list(shape), F32, kind="ExternalInput")
        return dt[name]

    din("L0_attn_x", (T, 1024))
    for n, s in [("L0_attn_ln1_g", (1024,)), ("L0_attn_ln1_b", (1024,)),
                 ("L0_attn_WQ", (1024, 1024)), ("L0_attn_bQ", (1024,)),
                 ("L0_attn_WK", (1024, 1024)), ("L0_attn_bK", (1024,)),
                 ("L0_attn_WV", (1024, 1024)), ("L0_attn_bV", (1024,)),
                 ("L0_attn_WO", (1024, 1024)), ("L0_attn_bO", (1024,)),
                 ("L0_attn_Wg", (1024, 1024)), ("L0_attn_bg", (1024,)),
                 ("L0_attn_A", (64,)), ("L0_attn_Bw", (1024, 64)),
                 ("L0_attn_Cw", (64, 1024)),
                 ("L0_attn_ln2_g", (1024,)), ("L0_attn_ln2_b", (1024,)),
                 ("L0_attn_W1", (1024, 4096)), ("L0_attn_b1", (4096,)),
                 ("L0_attn_W2", (4096, 1024)), ("L0_attn_b2", (1024,)),
                 ("L0_ffn_w1", (4096, 1024)), ("L0_ffn_b1", (4096,)),
                 ("L0_ffn_w2", (1024, 4096)), ("L0_ffn_b2", (1024,)),
                 ("L1_attn_norm_scale", (1024,)),
                 ("L1_attn_wq", (1024, 16, 64)), ("L1_attn_wk", (1024, 16, 64)),
                 ("L1_attn_wv", (1024, 16, 64)), ("L1_attn_w_out", (16, 64, 1024)),
                 ("L1_attn_b_out", (1024,)),
                 ("L1_ffn_gate_w", (8, 1024)), ("L1_ffn_w1", (8, 1024, 4096)),
                 ("L1_ffn_b1", (8, 4096)), ("L1_ffn_w2", (8, 4096, 1024)),
                 ("L1_ffn_b2", (8, 1024))]:
        din(n, s)
    din("c_ident_bf", (128, 128))
    din("c_ident_f", (128, 128))
    din("c_mask", (128, 256))
    y = nc.dram_tensor("y", [T, 1024], F32, kind="ExternalOutput")
    scr_up = nc.dram_tensor("scr_up", [4, 128, 8192], BF16, kind="Internal")
    scr_dn = nc.dram_tensor("scr_dn", [4, 128, 8192], BF16, kind="Internal")

    kb = KB(nc, p, T, ctx)
    for v in (EPS_LN, EPS_RMS, -0.5, 0.125, -1.0):
        kb.register_const(v)
    load_consts(kb, dt["c_ident_bf"], dt["c_ident_f"], dt["c_mask"])
    load_x(kb, dt["L0_attn_x"])

    def ffn_l0(first):
        load_col_tile(kb, dt["L0_ffn_b1"], 4096, kb.bup_col, 0, "bup")
        load_row(kb, kb.row0, dt["L0_ffn_b2"], 1024, "row0")
        if first:
            alpha2_prepass(kb, dt["L0_ffn_w2"][:, :])
            up = sign_up_loader(kb, dt["L0_ffn_w1"][:, :], scr_up)
            dn = sign_dn_loader(kb, dt["L0_ffn_w2"][:, :], scr_dn)
        else:
            up = scratch_up_loader(kb, scr_up)
            dn = scratch_dn_loader(kb, scr_dn)
        ffn_stage(kb, up, dn, AF.Gelu, kb.bup_col, kb.alpha1,
                  lambda tc: 0.5, add_b2=True,
                  bias_key="bup", scale_key="alpha_w")

    if "ffnA" in stages:
        ln_stage(kb, "plain")
        ffn_l0(True)
    if "attn0" in stages:
        x1_scr = nc.dram_tensor("x1_scr", [T, 1024], F32, kind="Internal")
        load_col_tile(kb, dt["L0_attn_ln1_g"], 1024, kb.gcol, 0, "gcol")
        load_col_tile(kb, dt["L0_attn_ln1_b"], 1024, kb.bcol, 0, "bcol")
        ln_stage(kb, "double_gb", kb.gcol, kb.bcol)
        attn_l0(kb, dict(
            WQ=dt["L0_attn_WQ"], WK=dt["L0_attn_WK"], WV=dt["L0_attn_WV"],
            WO=dt["L0_attn_WO"], Wg=dt["L0_attn_Wg"],
            bQ=dt["L0_attn_bQ"], bK=dt["L0_attn_bK"], bV=dt["L0_attn_bV"],
            bO=dt["L0_attn_bO"], bg=dt["L0_attn_bg"],
            A=dt["L0_attn_A"], Bw=dt["L0_attn_Bw"], Cw=dt["L0_attn_Cw"]), x1_scr)
        load_col_tile(kb, dt["L0_attn_ln2_g"], 1024, kb.gcol, 0, "gcol")
        load_col_tile(kb, dt["L0_attn_ln2_b"], 1024, kb.bcol, 0, "bcol")
        ln_stage(kb, "single_gb_dram", kb.gcol, kb.bcol)
        load_col_tile(kb, dt["L0_attn_b1"], 4096, kb.bup_col, 0, "bup")
        load_row(kb, kb.row0, dt["L0_attn_b2"], 1024, "row0")
        ffn_stage(kb, simple_up_loader(kb, dt["L0_attn_W1"][:, :]),
                  simple_dn_loader(kb, dt["L0_attn_W2"][:, :]),
                  AF.Gelu, kb.bup_col, None, lambda tc: 1.0, add_b2=True,
                  bias_key="bup", scale_key=None)
    if "ffnB" in stages:
        ln_stage(kb, "plain")
        ffn_l0(False)
    for mname in ("moe1", "moe2"):
        if mname in stages:
            moe_prep_gate(kb, dt["L1_ffn_gate_w"])
            ln_stage(kb, "plain", gate=True)
            moe_gate_softmax(kb)
            moe_experts(kb, dt["L1_ffn_w1"], dt["L1_ffn_w2"],
                        dt["L1_ffn_b1"], dt["L1_ffn_b2"])
        if mname == "moe1" and "attn1" in stages:
            load_col_tile(kb, dt["L1_attn_norm_scale"], 1024, kb.gcol, 0, "gcol")
            ln_stage(kb, "double_rms", kb.gcol)
            attn_l1(kb, dict(
                wq=dt["L1_attn_wq"], wk=dt["L1_attn_wk"], wv=dt["L1_attn_wv"],
                w_out=dt["L1_attn_w_out"], b_out=dt["L1_attn_b_out"]))

    for tname, tshape, tfn in DEBUG_TAPS:
        tdram = nc.dram_tensor(tname, list(tshape), F32, kind="ExternalOutput")
        ap, rkeys = tfn(kb)
        p.dma("gpsimd", tdram[:, :] if len(tshape) == 2 else tdram[:],
              ap, reads=rkeys, name="tap", ch=3)

    store_y(kb, y)

    p.finalize()
    sems = {}
    for axis in p.used_axes():
        sems[axis] = ctx.enter_context(nc.semaphore(axis))
    p.emit(nc, sems)
    ctx.close()
    return nc, p




def make_consts():
    cid_bf = np.eye(128, dtype=np.float32)
    cid_f = np.eye(128, dtype=np.float32)
    m0 = np.where(np.arange(256)[None, 0:128] > np.arange(128)[:, None],
                  0.0, NEG).astype(np.float32)
    m1 = np.where(np.arange(128)[None, :] <= np.arange(128)[:, None],
                  0.0, NEG).astype(np.float32)
    cmask = np.concatenate([m0, m1], axis=1).astype(np.float32)
    return cid_bf, cid_f, cmask


_CACHED = {}


def kernel(**inputs):
    T = 2048
    key = "full"
    if key not in _CACHED:
        _CACHED[key] = build_program(T)
    nc, _ = _CACHED[key]
    cid_bf, cid_f, cmask = make_consts()
    x = np.ascontiguousarray(np.asarray(inputs["L0_attn_x"], dtype=np.float32))
    base = {}
    for k, v in inputs.items():
        if k == "L0_attn_x":
            continue
        base[k] = np.ascontiguousarray(np.asarray(v, dtype=np.float32))
    base["c_ident_bf"] = cid_bf
    base["c_ident_f"] = cid_f
    base["c_mask"] = cmask
    in_maps = []
    for core in range(8):
        m = dict(base)
        m["L0_attn_x"] = x[core]
        in_maps.append(m)
    res = run_bass_kernel_spmd(nc, in_maps, core_ids=list(range(8)))
    out = np.stack([r["y"] for r in res.results], axis=0)
    return out.astype(np.float32)


# version 07
# speedup vs baseline: 1.2937x; 1.2937x over previous
import numpy as np
import concourse.bass as bass
import concourse.mybir as mybir
from concourse.bass_utils import run_bass_kernel_spmd

F32 = mybir.dt.float32
BF16 = mybir.dt.bfloat16
AF = mybir.ActivationFunctionType
ALU = mybir.AluOpType
AX = mybir.AxisListType

ENGINES = ("sync", "scalar", "vector", "tensor", "gpsimd")
DMA_INC = 16



class Op:
    __slots__ = ("engine", "axis", "fn", "deps", "eidx", "tagged", "waits",
                 "clock", "rank", "name")

    def __repr__(self):
        return f"<Op {self.engine}/{self.axis}#{self.eidx} {self.name}>"


class Plan:

    def __init__(self):
        self.ops = []
        self.axis_count = {}
        self.last_write = {}
        self.readers = {}

    def _add(self, engine, axis, fn, reads, writes, deps, name):
        op = Op()
        op.engine = engine
        op.axis = axis
        op.fn = fn
        op.name = name
        op.tagged = "_dma" in axis
        op.waits = None
        op.clock = None
        op.rank = None
        d = []
        for k in reads:
            w = self.last_write.get(k)
            if w is not None:
                d.append(w)
        for k in writes:
            w = self.last_write.get(k)
            if w is not None:
                d.append(w)
            rs = self.readers.get(k)
            if rs:
                d.extend(rs)
        for x in deps:
            if x is not None:
                d.append(x)
        op.deps = d
        op.eidx = self.axis_count.get(axis, 0)
        self.axis_count[axis] = op.eidx + 1
        for k in reads:
            self.readers.setdefault(k, []).append(op)
        for k in writes:
            self.last_write[k] = op
            self.readers[k] = []
        self.ops.append(op)
        return op

    def op(self, engine, fn, reads=(), writes=(), deps=(), name=""):
        return self._add(engine, engine, fn, reads, writes, deps, name)

    def dma(self, engine, out, in_, reads=(), writes=(), deps=(), name="", ch=0,
            **kw):
        def fn(eng, _out=out, _in=in_, _kw=kw):
            return eng.dma_start(_out, _in, **_kw)
        return self._add(engine, "%s_dma%d" % (engine, ch), fn, reads, writes,
                         deps, name)

    def barrier_all_dma(self, engine="gpsimd"):
        deps = []
        for axis, cnt in self.axis_count.items():
            if "_dma" in axis and cnt > 0:
                for o in reversed(self.ops):
                    if o.axis == axis:
                        deps.append(o)
                        break
        self._add(engine, engine, None, (), (), deps, "final_dma_barrier")

    def finalize(self):
        observed = {e: {} for e in ENGINES}
        axis_ops = {}
        for op in self.ops:
            axis_ops.setdefault(op.axis, []).append(op)
        issued = {}
        for op in self.ops:
            obs = observed[op.engine]
            need = {}
            for dop in op.deps:
                v = dop.eidx + 1
                if need.get(dop.axis, 0) < v:
                    need[dop.axis] = v
            if "_dma" in op.axis and op.eidx > 0:
                v = op.eidx
                if need.get(op.axis, 0) < v:
                    need[op.axis] = v
            waits = []
            for axis, v in need.items():
                if axis == "tensor" and op.engine == "tensor":
                    continue
                if obs.get(axis, 0) >= v:
                    continue
                waits.append((axis, v))
            for dop in op.deps:
                c = dop.clock
                if c:
                    for a, v in c.items():
                        if obs.get(a, 0) < v:
                            obs[a] = v
            for axis, v in need.items():
                if obs.get(axis, 0) < v:
                    obs[axis] = v
            for axis, v in waits:
                if obs.get(axis, 0) < v:
                    obs[axis] = v
            op.waits = waits
            for axis, v in waits:
                axis_ops[axis][v - 1].tagged = True
            clk = dict(obs)
            if clk.get(op.axis, 0) < op.eidx + 1:
                clk[op.axis] = op.eidx + 1
            op.clock = clk
            issued[op.axis] = op.eidx + 1
        self.rank_map = {}
        for axis, ops in axis_ops.items():
            r = 0
            ranks = [0] * len(ops)
            for i, o in enumerate(ops):
                if o.tagged:
                    r += 1
                    o.rank = r
                ranks[i] = r
            self.rank_map[axis] = ranks
        self.axis_ops = axis_ops

    def emit(self, nc, sems):
        per_engine = {e: [] for e in ENGINES}
        for op in self.ops:
            per_engine[op.engine].append(op)

        def make_runner(engine_name):
            ops = per_engine[engine_name]

            def run(eng):
                for op in ops:
                    for axis, v in op.waits:
                        r = self.rank_map[axis][v - 1]
                        assert r > 0, (op, axis, v)
                        mult = DMA_INC if "_dma" in axis else 1
                        eng.wait_ge(sems[axis], r * mult)
                    if op.fn is None:
                        continue
                    inst = op.fn(eng)
                    if op.tagged:
                        inst.then_inc(sems[op.axis],
                                      DMA_INC if "_dma" in op.axis else 1)
            return run

        with nc.Block() as block:
            block.sync(make_runner("sync"))
            block.scalar(make_runner("scalar"))
            block.vector(make_runner("vector"))
            block.tensor(make_runner("tensor"))
            block.gpsimd(make_runner("gpsimd"))

    def used_axes(self):
        return [a for a, c in self.axis_count.items() if c > 0]



D = 1024
DC = 8
DFF = 4096
NH = 16
DH = 64
NPAIR = 8
NS = 64
WIN = 128
NE = 8
NEG = -1.0e30


class KB:

    def __init__(self, nc, p, T, ctx):
        self.nc = nc
        self.p = p
        self.T = T
        self.TC = T // 128
        self.NTG = T // 512
        e = ctx.enter_context
        self.x = e(nc.sbuf_tensor("x", [128, self.TC * 1024], F32))
        self.hnT = e(nc.sbuf_tensor("hnT", [128, DC * T], BF16))
        self.AR = max(36864, 8192 + 16 * T)
        self.ar = e(nc.sbuf_tensor("ar", [128, self.AR], BF16))
        self.zf = e(nc.sbuf_tensor("zf", [128, 1024], F32))
        self.zbf = e(nc.sbuf_tensor("zbf", [128, 1024], BF16))
        self.zT = e(nc.sbuf_tensor("zT", [128, 512], F32))
        self.probs = e(nc.sbuf_tensor("probs", [128, 1024], F32))
        self.st_sb = e(nc.sbuf_tensor("st_sb", [64, 2048], BF16))
        self.ab_sb = e(nc.sbuf_tensor("ab_sb", [64, 512], F32))
        self.cw_sb = e(nc.sbuf_tensor("cw_sb", [64, 1024], BF16))
        self.bw_sb = e(nc.sbuf_tensor("bw_sb", [128, DC * 64], BF16))
        self.acol = e(nc.sbuf_tensor("acol", [64, 1], F32))
        st = lambda n: e(nc.sbuf_tensor(n, [128, 16], F32))
        self.nsum = st("nsum")
        self.sumsq = st("sumsq")
        self.var = st("var")
        self.rstd = st("rstd")
        self.nmr = st("nmr")
        self.tmpst = st("tmpst")
        self.negmax = st("negmax")
        self.rstd_o = st("rstd_o")
        self.nmr_o = st("nmr_o")
        self.esum = st("esum")
        self.recip = st("recip")
        self.alpha1 = e(nc.sbuf_tensor("alpha1", [128, 32], F32))
        self.alpha2 = e(nc.sbuf_tensor("alpha2", [128, 8], F32))
        self.bup_col = e(nc.sbuf_tensor("bup_col", [128, 32], F32))
        self.gcol = e(nc.sbuf_tensor("gcol", [128, 8], F32))
        self.bcol = e(nc.sbuf_tensor("bcol", [128, 8], F32))
        self.qkvcol = e(nc.sbuf_tensor("qkvcol", [128, 24], F32))
        self.nscol = e(nc.sbuf_tensor("nscol", [128, 8], F32))
        self.coeff = e(nc.sbuf_tensor("coeff", [128, NE * 16], F32))
        self.logit = e(nc.sbuf_tensor("logit", [128, 16 * 8], F32))
        self.selw = e(nc.sbuf_tensor("selw", [128, 16], F32))
        self.found = e(nc.sbuf_tensor("found", [128, 16], F32))
        self.gwT = e(nc.sbuf_tensor("gwT", [128, 64], F32))
        self.ones_bf = e(nc.sbuf_tensor("ones_bf", [1, 128], BF16))
        self.row0 = e(nc.sbuf_tensor("row0", [1, 1024], BF16))
        self.row1 = e(nc.sbuf_tensor("row1", [1, 1024], BF16))
        self.row2 = e(nc.sbuf_tensor("row2", [1, 1024], BF16))
        self.vspread = e(nc.sbuf_tensor("vspread", [32, 128], F32))
        self.cpool = e(nc.sbuf_tensor("cpool", [128, 8], F32))
        self.cpool_n = 0
        self.ident_bf = e(nc.sbuf_tensor("ident_bf", [128, 128], BF16))
        self.ident_f = e(nc.sbuf_tensor("ident_f", [128, 128], F32))
        self.mask = e(nc.sbuf_tensor("mask", [128, 256], BF16))
        self.psBIG = e(nc.psum_tensor("psBIG", [128, 2048], F32))
        self.psB0 = e(nc.psum_tensor("psB0", [128, 512], F32))
        self.psB1 = e(nc.psum_tensor("psB1", [128, 512], F32))
        self.psT = e(nc.psum_tensor("psT", [128, 1024], BF16))
        self.psS = e(nc.psum_tensor("psS", [128, 512], F32))
        assert nc.psum_banks_remaining == 0, nc.psum_banks_remaining
        self.sbuf_left = nc.sbuf_bytes_remaining

    def ark(self, lo, n):
        return ["ar%d" % i for i in range(lo // 512, (lo + n + 511) // 512)]

    def pbk(self, lo, n):
        return ["pb%d" % i for i in range(lo // 512, (lo + n + 511) // 512)]

    def mm(self, out, lhsT, rhs, start, stop, reads, writes, name="mm",
           tile_position=None):
        return self.p.op(
            "tensor",
            lambda e, o=out, a=lhsT, b=rhs, s=start, t=stop, tp=tile_position:
                e.matmul(o, a, b, start=s, stop=t, skip_group_check=True,
                         tile_position=tp),
            reads=reads, writes=writes, name=name)

    def tr(self, out, in_, ident, reads, writes, name="tr", ident_key="idb"):
        return self.p.op(
            "tensor",
            lambda e, o=out, a=in_, i=ident: e.transpose(o, a, i),
            reads=list(reads) + [ident_key], writes=writes, name=name)

    def stat(self, kind, s):
        tiles = {
            "nm": (self.negmax, "nm", self.sumsq, "sumsq"),
            "es": (self.esum, "es", self.nsum, "nsum"),
            "rc": (self.recip, "rc", self.rstd, "rstd"),
            "tm": (self.tmpst, "tmpst", self.var, "var"),
        }[kind]
        if s < 16:
            t, kp = tiles[0], tiles[1]
        else:
            t, kp = tiles[2], tiles[3]
        c = s % 16
        return t[:, c:c + 1], f"{kp}{c}"

    def register_const(self, v):
        i = self.cpool_n
        self.cpool_n += 1
        assert i < 8
        self.p.op("vector", lambda e, c=self.cpool[:, i:i + 1], vv=v:
                  e.memset(c, vv), writes=["cpool"], name="cpool")
        self.nc.const_aps.register(F32, v, self.cpool[:, i:i + 1])

    def act(self, out, in_, func, bias=0.0, scale=1.0, reads=(), writes=(),
            accum_out=None, name="act"):
        extra = []
        if (isinstance(bias, float) and bias != 0.0) or \
           (isinstance(scale, float) and scale not in (1.0,)):
            extra = ["cpool"]

        def fn(e, o=out, a=in_, f=func, b=bias, s=scale, ac=accum_out):
            if ac is None:
                return e.activation(o, a, f, bias=b, scale=s)
            return e.activation(o, a, f, bias=b, scale=s, accum_out=ac)
        return self.p.op("scalar", fn, reads=list(reads) + extra, writes=writes,
                         name=name)

    def vop(self, fn, reads=(), writes=(), name="vop", eng="vector"):
        return self.p.op(eng, fn, reads=reads, writes=writes, name=name)




def load_col_tile(kb, vec_ap, F, dest, dcol, key, prescale=None):
    P = F // 128
    kb.p.dma("sync", kb.vspread[0:P, 0:128],
             vec_ap.rearrange("(a j) -> a j", j=128),
             writes=["vspread"], name="colspread", ch=1)
    kb.tr(kb.psS[0:128, 0:P], kb.vspread[0:P, 0:128], kb.ident_f[0:P, 0:P],
          reads=["vspread"], writes=["psS"], name="coltr", ident_key="idf")
    kb.vop(lambda e, o=dest[:, dcol:dcol + P], i=kb.psS[0:128, 0:P]:
           e.tensor_copy(o, i),
           reads=["psS"], writes=[key], name="colcopy")
    if prescale is not None:
        kb.vop(lambda e, o=dest[:, dcol:dcol + P], s=prescale:
               e.tensor_scalar_mul(o, o, s),
               reads=[key], writes=[key], name="colscale")


def load_row(kb, row_tile, vec_ap, F, key):
    kb.p.dma("gpsimd", row_tile[0:1, 0:F], vec_ap[None, :], writes=[key],
             name="rowload", ch=2)


def load_wblock(kb, dram_ap_2d, dest, dcol, key, engine="gpsimd", ch=0):
    R, W = dram_ap_2d.shape
    a = R // 128
    kb.p.dma(engine,
             dest[:, dcol:dcol + a * W].rearrange("p (a j) -> p a j", a=a),
             dram_ap_2d.rearrange("(a p) j -> p a j", p=128),
             writes=[key] if isinstance(key, str) else key, name="wblock", ch=ch)



EPS_LN = 1e-5
EPS_RMS = 1e-6


def _ln_stats(kb, src, srckey, c, eps, rms=False, dst=None):
    p = kb.p
    rstd_t, nmr_t, kp = (kb.rstd, kb.nmr, "") if dst is None else dst
    kb.act(kb.zbf[:, :], src, AF.Square, accum_out=kb.sumsq[:, c:c + 1],
           reads=[srckey], writes=["zbf", f"sumsq{c}"], name="sq")
    if not rms:
        kb.act(kb.zbf[:, :], src, AF.Identity, scale=-1.0,
               accum_out=kb.nsum[:, c:c + 1],
               reads=[srckey], writes=["zbf", f"nsum{c}"], name="nsumacc")
        p.op("vector", lambda e, o=kb.tmpst[:, c:c + 1], i=kb.nsum[:, c:c + 1]:
             e.tensor_scalar_mul(o, i, 1.0 / D),
             reads=[f"nsum{c}"], writes=[f"tmpst{c}"], name="nmean")
        p.op("vector", lambda e, o=kb.var[:, c:c + 1], i=kb.tmpst[:, c:c + 1]:
             e.tensor_mul(o, i, i),
             reads=[f"tmpst{c}"], writes=[f"var{c}"], name="mean2")
        p.op("vector", lambda e, o=kb.var[:, c:c + 1], a=kb.sumsq[:, c:c + 1],
             b=kb.var[:, c:c + 1]:
             e.scalar_tensor_tensor(o, a, 1.0 / D, b, ALU.mult, ALU.subtract),
             reads=[f"sumsq{c}", f"var{c}"], writes=[f"var{c}"], name="var")
    else:
        p.op("vector", lambda e, o=kb.var[:, c:c + 1], i=kb.sumsq[:, c:c + 1]:
             e.tensor_scalar_mul(o, i, 1.0 / D),
             reads=[f"sumsq{c}"], writes=[f"var{c}"], name="varrms")
    kb.act(rstd_t[:, c:c + 1], kb.var[:, c:c + 1], AF.Ln, bias=float(eps),
           reads=[f"var{c}"], writes=[f"{kp}rstd{c}"], name="ln")
    kb.act(rstd_t[:, c:c + 1], rstd_t[:, c:c + 1], AF.Exp, scale=-0.5,
           reads=[f"{kp}rstd{c}"], writes=[f"{kp}rstd{c}"], name="exprs")
    if not rms:
        p.op("vector", lambda e, o=nmr_t[:, c:c + 1], a=kb.tmpst[:, c:c + 1],
             b=rstd_t[:, c:c + 1]: e.tensor_mul(o, a, b),
             reads=[f"tmpst{c}", f"{kp}rstd{c}"], writes=[f"{kp}nmr{c}"],
             name="nmr")


def _transpose_to_hnT(kb, c, gb=None):
    T = kb.T
    for j in range(DC):
        kb.tr(kb.psT[:, j * 128:(j + 1) * 128], kb.zbf[:, j * 128:(j + 1) * 128],
              kb.ident_bf[:, :], reads=["zbf"], writes=["psT"], name="trh")
    hview = kb.hnT[:, :].rearrange("p (a t) -> p a t", a=DC)[:, :, c * 128:(c + 1) * 128]
    pview = kb.psT[:, :].rearrange("p (a t) -> p a t", a=DC)
    hkeys = [f"hnT{j}" for j in range(DC)]
    ptk = ["psT"]
    if gb is None:
        kb.vop(lambda e, o=hview, i=pview: e.tensor_copy(o, i),
               reads=ptk, writes=hkeys, name="hevac")
    else:
        mode, g, b = gb
        for j in range(DC):
            dst = kb.hnT[:, j * T + c * 128: j * T + (c + 1) * 128]
            src = kb.psT[:, j * 128:(j + 1) * 128]
            if mode == "gb":
                kb.act(dst, src, AF.Identity, bias=b[:, j:j + 1], scale=g[:, j:j + 1],
                       reads=["psT", "gcol", "bcol"], writes=[f"hnT{j}"],
                       name="hevacgb")
            else:
                kb.act(dst, src, AF.Identity, scale=g[:, j:j + 1],
                       reads=["psT", "gcol"], writes=[f"hnT{j}"], name="hevacs")


def ln_stage(kb, kind, g1col=None, b1col=None, gate=False):
    T = kb.T
    for c in range(kb.TC):
        xc = kb.x[:, c * 1024:(c + 1) * 1024]
        xk = f"x{c}"
        if kind == "single_gb_dram":
            kb.p.dma("sync", kb.probs[:, 0:1024], kb.x1_scr[c * 128:(c + 1) * 128, :],
                     reads=[f"x1s{c}"], writes=["probs"], name="x1load", ch=0)
            _ln_stats(kb, kb.probs[:, 0:1024], "probs", c, EPS_LN)
            kb.act(kb.zbf[:, :], kb.probs[:, 0:1024], AF.Identity,
                   bias=kb.nmr[:, c:c + 1], scale=kb.rstd[:, c:c + 1],
                   reads=["probs", f"rstd{c}", f"nmr{c}"], writes=["zbf"],
                   name="appbf")
            _transpose_to_hnT(kb, c, gb=("gb", g1col, b1col))
            continue
        if kind == "double_gb":
            _ln_stats(kb, xc, xk, c, EPS_LN,
                      dst=(kb.rstd_o, kb.nmr_o, "o"))
        else:
            _ln_stats(kb, xc, xk, c, EPS_LN)
        if kind == "plain":
            if gate:
                kb.act(kb.zf[:, :], xc, AF.Identity, bias=kb.nmr[:, c:c + 1],
                       scale=kb.rstd[:, c:c + 1],
                       reads=[xk, f"rstd{c}", f"nmr{c}"], writes=["zf"], name="app32")
                kb.act(kb.zbf[:, :], kb.zf[:, :], AF.Identity,
                       reads=["zf"], writes=["zbf"], name="appbf")
                _gate_logits(kb, c)
            else:
                kb.act(kb.zbf[:, :], xc, AF.Identity, bias=kb.nmr[:, c:c + 1],
                       scale=kb.rstd[:, c:c + 1],
                       reads=[xk, f"rstd{c}", f"nmr{c}"], writes=["zbf"], name="appbf")
            _transpose_to_hnT(kb, c)
        elif kind == "single_gb":
            kb.act(kb.zbf[:, :], xc, AF.Identity, bias=kb.nmr[:, c:c + 1],
                   scale=kb.rstd[:, c:c + 1],
                   reads=[xk, f"rstd{c}", f"nmr{c}"], writes=["zbf"], name="appbf")
            _transpose_to_hnT(kb, c, gb=("gb", g1col, b1col))
        elif kind == "double_gb":
            kb.act(kb.zf[:, :], xc, AF.Identity, bias=kb.nmr_o[:, c:c + 1],
                   scale=kb.rstd_o[:, c:c + 1],
                   reads=[xk, f"orstd{c}", f"onmr{c}"], writes=["zf"],
                   name="app32")
            _ln_stats(kb, kb.zf[:, :], "zf", c, EPS_LN)
            kb.act(kb.zbf[:, :], kb.zf[:, :], AF.Identity, bias=kb.nmr[:, c:c + 1],
                   scale=kb.rstd[:, c:c + 1],
                   reads=["zf", f"rstd{c}", f"nmr{c}"], writes=["zbf"], name="appbf")
            _transpose_to_hnT(kb, c, gb=("gb", g1col, b1col))
        elif kind == "double_rms":
            kb.act(kb.zf[:, :], xc, AF.Identity, bias=kb.nmr[:, c:c + 1],
                   scale=kb.rstd[:, c:c + 1],
                   reads=[xk, f"rstd{c}", f"nmr{c}"], writes=["zf"], name="app32")
            _ln_stats(kb, kb.zf[:, :], "zf", c, EPS_RMS, rms=True)
            kb.act(kb.zbf[:, :], kb.zf[:, :], AF.Identity, scale=kb.rstd[:, c:c + 1],
                   reads=["zf", f"rstd{c}"], writes=["zbf"], name="appbf")
            _transpose_to_hnT(kb, c, gb=("s", g1col, None))
        else:
            raise ValueError(kind)


def _gate_logits(kb, c):
    for r in range(2):
        for jj in range(4):
            j = r * 4 + jj
            kb.tr(kb.psS[:, jj * 128:(jj + 1) * 128],
                  kb.zf[:, j * 128:(j + 1) * 128], kb.ident_f[:, :],
                  reads=["zf"], writes=["psS"], name="trz", ident_key="idf")
        kb.vop(lambda e, o=kb.zT[:, 0:512], i=kb.psS[:, :]:
               e.tensor_copy(o, i), reads=["psS"],
               writes=["zT"], name="zTevac")
        for jj in range(4):
            dcc = r * 4 + jj
            kb.mm(kb.psB1[:, 0:8], kb.zT[:, jj * 128:(jj + 1) * 128],
                  kb.gwT[:, dcc * 8:(dcc + 1) * 8], start=(dcc == 0),
                  stop=(dcc == DC - 1),
                  reads=["zT", "gwT"], writes=["psB1"], name="gatemm")
    kb.vop(lambda e, o=kb.logit[:, c * 8:(c + 1) * 8], i=kb.psB1[:, 0:8]:
           e.tensor_copy(o, i), reads=["psB1"], writes=[f"logit{c}"], name="logevac")



FF_WUP = (0, 8192)
FF_WDN = (16384, 8192)
FF_HID = (32768, 4096)


def ffn_stage(kb, up_loader, dn_loader, act_fn, bias_col, scale_col, coeff_fn,
              add_b2=True, bias_key=None, scale_key=None):
    p = kb.p
    T = kb.T
    for hq in range(4):
        buf = hq % 2
        up_loader(hq, buf)
        dn_loader(hq, buf)
        upbase = FF_WUP[0] + buf * 8192
        dnbase = FF_WDN[0] + buf * 8192
        upkeys = kb.ark(upbase, 8192)
        dnkeys = kb.ark(dnbase, 8192)
        hidkeys = kb.ark(FF_HID[0], 4096)
        for tg in range(kb.NTG):
            for hc in range(8):
                ps = kb.psB0 if hc % 2 == 0 else kb.psB1
                psk = "psB0" if hc % 2 == 0 else "psB1"
                for dcc in range(DC):
                    kb.mm(ps[:, :],
                          kb.ar[:, upbase + dcc * 1024 + hc * 128:
                                upbase + dcc * 1024 + (hc + 1) * 128],
                          kb.hnT[:, dcc * T + tg * 512: dcc * T + tg * 512 + 512],
                          start=(dcc == 0), stop=(dcc == DC - 1),
                          reads=upkeys + [f"hnT{dcc}"], writes=[psk], name="ffA")
                ghc = hq * 8 + hc
                bias = bias_col[:, ghc:ghc + 1] if bias_col is not None else 0.0
                scale = scale_col[:, ghc:ghc + 1] if scale_col is not None else 1.0
                brd = ([bias_key] if bias_col is not None else []) + \
                      ([scale_key] if scale_col is not None else [])
                kb.act(kb.ar[:, FF_HID[0] + hc * 512: FF_HID[0] + (hc + 1) * 512],
                       ps[:, :], act_fn, bias=bias, scale=scale,
                       reads=[psk] + brd, writes=hidkeys, name="ffact")
            for dh in range(2):
                for i in range(4):
                    pslo = i * 512
                    pkeys = kb.pbk(pslo, 512)
                    with_b2 = (hq == 0 and add_b2)
                    for hc in range(8):
                        kb.mm(kb.psBIG[:, pslo:pslo + 512],
                              kb.ar[:, FF_HID[0] + hc * 512 + i * 128:
                                    FF_HID[0] + hc * 512 + (i + 1) * 128],
                              kb.ar[:, dnbase + hc * 1024 + dh * 512:
                                    dnbase + hc * 1024 + (dh + 1) * 512],
                              start=(hc == 0), stop=(hc == 7 and not with_b2),
                              reads=hidkeys + dnkeys, writes=pkeys, name="ffB")
                    if with_b2:
                        kb.mm(kb.psBIG[:, pslo:pslo + 512], kb.ones_bf[0:1, :],
                              kb.row0[0:1, dh * 512:(dh + 1) * 512],
                              start=False, stop=True,
                              reads=["ones", "row0"], writes=pkeys, name="ffb2")
                    tc = tg * 4 + i
                    coeff = coeff_fn(tc)
                    ckeys = []
                    if isinstance(coeff, tuple):
                        coeff, ck = coeff
                        ckeys = [ck]
                    xsl = kb.x[:, tc * 1024 + dh * 512: tc * 1024 + (dh + 1) * 512]
                    p.op("vector",
                         lambda e, o=xsl, a=kb.psBIG[:, pslo:pslo + 512], s=coeff:
                         e.scalar_tensor_tensor(o, a, s, o, ALU.mult, ALU.add),
                         reads=pkeys + [f"x{tc}"] + ckeys, writes=[f"x{tc}"],
                         name="ffevac")


def simple_up_loader(kb, w_dram):
    def load(hq, buf):
        base = FF_WUP[0] + buf * 8192
        load_wblock(kb, w_dram[:, hq * 1024:(hq + 1) * 1024], kb.ar, base,
                    kb.ark(base, 8192))
    return load


def simple_dn_loader(kb, w_dram):
    def load(hq, buf):
        base = FF_WDN[0] + buf * 8192
        load_wblock(kb, w_dram[hq * 1024:(hq + 1) * 1024, :], kb.ar, base,
                    kb.ark(base, 8192), ch=1)
    return load




def _sign_stripe(kb, dram_stripe, dest_base, rc, alpha_col_dst=None,
                 prescale_col=None, dkeys=()):
    p = kb.p
    p.dma("sync", kb.zf[:, :], dram_stripe, writes=["zf"], name="sgnload")
    if alpha_col_dst is not None:
        p.op("vector", lambda e, o=alpha_col_dst, i=kb.zf[:, :]:
             e.reduce_sum(o, i, AX.X, apply_absolute_value=True),
             reads=["zf"], writes=["alpha_w"], name="absred")
        p.op("vector", lambda e, o=alpha_col_dst: e.tensor_scalar_mul(o, o, 1.0 / D),
             reads=["alpha_w"], writes=["alpha_w"], name="alphasc")
    p.op("scalar", lambda e, o=kb.zbf[:, :], i=kb.zf[:, :]: e.sign(o, i),
         reads=["zf"], writes=["zbf"], name="sign")
    if prescale_col is not None:
        p.op("vector", lambda e, o=kb.zbf[:, :], s=prescale_col:
             e.tensor_scalar_mul(o, o, s),
             reads=["zbf", "alpha2_r"], writes=["zbf"], name="sgnscale")
    for j in range(DC):
        kb.tr(kb.psT[:, j * 128:(j + 1) * 128], kb.zbf[:, j * 128:(j + 1) * 128],
              kb.ident_bf[:, :], reads=["zbf"], writes=["psT"], name="sgntr")
    dview = kb.ar[:, dest_base: dest_base + 8192] \
        .rearrange("p (a j) -> p a j", a=8)[:, :, rc * 128:(rc + 1) * 128]
    pview = kb.psT[:, :].rearrange("p (a j) -> p a j", a=DC)
    kb.vop(lambda e, o=dview, i=pview: e.tensor_copy(o, i),
           reads=["psT"], writes=list(dkeys), name="sgnevac")


def sign_up_loader(kb, w1_dram, scr_up):
    def load(hq, buf):
        base = FF_WUP[0] + buf * 8192
        keys = kb.ark(base, 8192)
        for rc in range(8):
            ghc = hq * 8 + rc
            _sign_stripe(kb, w1_dram[ghc * 128:(ghc + 1) * 128, :], base, rc,
                         alpha_col_dst=kb.alpha1[:, ghc:ghc + 1], dkeys=keys)
        kb.p.dma("sync", scr_up[hq, :, :], kb.ar[:, base:base + 8192],
                 reads=keys, writes=[f"scru{hq}"], name="scrspill")
    return load


def sign_dn_loader(kb, w2_dram, scr_dn):
    def load(hq, buf):
        base = FF_WDN[0] + buf * 8192
        keys = kb.ark(base, 8192)
        for rc in range(8):
            _sign_stripe(kb, w2_dram[rc * 128:(rc + 1) * 128,
                                     hq * 1024:(hq + 1) * 1024],
                         base, rc, prescale_col=kb.alpha2[:, rc:rc + 1], dkeys=keys)
        kb.p.dma("sync", scr_dn[hq, :, :], kb.ar[:, base:base + 8192],
                 reads=keys, writes=[f"scrd{hq}"], name="scrspill2")
    return load


def alpha2_prepass(kb, w2_dram):
    p = kb.p
    for rc in range(8):
        for hh in range(4):
            p.dma("sync", kb.probs[:, 0:1024],
                  w2_dram[rc * 128:(rc + 1) * 128, hh * 1024:(hh + 1) * 1024],
                  writes=["probs"], name="a2load")
            p.op("vector", lambda e, o=kb.tmpst[:, hh:hh + 1], i=kb.probs[:, 0:1024]:
                 e.reduce_sum(o, i, AX.X, apply_absolute_value=True),
                 reads=["probs"], writes=[f"a2p{hh}"], name="a2red")
        p.op("vector", lambda e, o=kb.tmpst[:, 0:1], a=kb.tmpst[:, 0:1],
             b=kb.tmpst[:, 1:2]: e.tensor_add(o, a, b),
             reads=["a2p0", "a2p1"], writes=["a2p0"], name="a2add")
        p.op("vector", lambda e, o=kb.tmpst[:, 2:3], a=kb.tmpst[:, 2:3],
             b=kb.tmpst[:, 3:4]: e.tensor_add(o, a, b),
             reads=["a2p2", "a2p3"], writes=["a2p2"], name="a2add2")
        p.op("vector", lambda e, o=kb.alpha2[:, rc:rc + 1], a=kb.tmpst[:, 0:1],
             b=kb.tmpst[:, 2:3]: e.tensor_add(o, a, b),
             reads=["a2p0", "a2p2"], writes=["alpha2_r"], name="a2add3")
        p.op("vector", lambda e, o=kb.alpha2[:, rc:rc + 1]:
             e.tensor_scalar_mul(o, o, 1.0 / DFF),
             reads=["alpha2_r"], writes=["alpha2_r"], name="a2sc")


def scratch_up_loader(kb, scr_up):
    def load(hq, buf):
        base = FF_WUP[0] + buf * 8192
        kb.p.dma("gpsimd", kb.ar[:, base:base + 8192], scr_up[hq, :, :],
                 reads=[f"scru{hq}"], writes=kb.ark(base, 8192), name="scrload",
                 ch=0)
    return load


def scratch_dn_loader(kb, scr_dn):
    def load(hq, buf):
        base = FF_WDN[0] + buf * 8192
        kb.p.dma("gpsimd", kb.ar[:, base:base + 8192], scr_dn[hq, :, :],
                 reads=[f"scrd{hq}"], writes=kb.ark(base, 8192), name="scrload2",
                 ch=1)
    return load




def moe_prep_gate(kb, gate_w_dram):
    for dcc in range(DC):
        kb.p.dma("sync", kb.vspread[0:8, 0:128],
                 gate_w_dram[:, dcc * 128:(dcc + 1) * 128],
                 writes=["vspread"], name="gwload", ch=1)
        kb.tr(kb.psS[:, dcc * 8:(dcc + 1) * 8],
              kb.vspread[0:8, 0:128], kb.ident_f[0:8, 0:8],
              reads=["vspread"], writes=["psS"], name="gwtr", ident_key="idf")
    kb.vop(lambda e, o=kb.gwT[:, 0:64], i=kb.psS[:, 0:64]: e.tensor_copy(o, i),
           reads=["psS"], writes=["gwT"], name="gwevac")


def moe_gate_softmax(kb):
    p = kb.p
    p.op("vector", lambda e: e.memset(kb.found[:, 0:8], 1.0),
         writes=["ones8"], name="ones8")
    for tc in range(kb.TC):
        L = kb.logit[:, tc * 8:(tc + 1) * 8]
        lk = f"logit{tc}"
        nm = kb.negmax[:, tc:tc + 1]
        es = kb.esum[:, tc:tc + 1]
        p.op("vector", lambda e, o=nm, i=L: e.reduce_max(o, i, AX.X, negate=True),
             reads=[lk], writes=[f"nm{tc}"], name="gmax")
        kb.act(kb.zf[:, tc * 8:(tc + 1) * 8], L, AF.Exp, bias=nm,
               accum_out=es, reads=[lk, f"nm{tc}"], writes=["zf", f"es{tc}"],
               name="gexp")
        p.op("vector", lambda e, o=kb.recip[:, tc:tc + 1], i=es: e.reciprocal(o, i),
             reads=[f"es{tc}"], writes=[f"rc{tc}"], name="grecip")
        p.op("vector", lambda e, o=kb.selw[:, tc:tc + 1], i=kb.recip[:, tc:tc + 1]:
             e.tensor_scalar_mul(o, i, 0.5),
             reads=[f"rc{tc}"], writes=[f"sw{tc}"], name="gselw")
        p.op("vector", lambda e, o=kb.tmpst[:, tc:tc + 1], i=nm:
             e.tensor_scalar_mul(o, i, -1.0),
             reads=[f"nm{tc}"], writes=[f"gm{tc}"], name="gmaxpos")
        eq = kb.zf[:, 128 + tc * 8:128 + (tc + 1) * 8]
        neq = kb.zf[:, 256 + tc * 8:256 + (tc + 1) * 8]
        cp = kb.zf[:, 384 + tc * 8:384 + (tc + 1) * 8]
        sel = kb.zf[:, 512 + tc * 8:512 + (tc + 1) * 8]
        p.op("vector", lambda e, o=eq, i=L, s=kb.tmpst[:, tc:tc + 1]:
             e.tensor_single_scalar(o, i, s, ALU.is_equal),
             reads=[lk, f"gm{tc}"], writes=["zf"], name="geq")
        p.op("vector", lambda e, o=neq, i=eq:
             e.tensor_scalar(o, i, -1.0, 1.0, ALU.mult, ALU.add),
             reads=["zf"], writes=["zf"], name="gneq")
        p.op("vector", lambda e, o=cp, a=neq, b=kb.found[:, 0:8]:
             e.tensor_tensor_scan(o, a, b, 1.0, ALU.mult, ALU.mult),
             reads=["zf", "ones8"], writes=["zf"], name="gcum")
        p.op("vector", lambda e, o=sel[:, 1:8], a=eq[:, 1:8], b=cp[:, 0:7]:
             e.tensor_mul(o, a, b), reads=["zf"], writes=["zf"], name="gsel")
        p.op("vector", lambda e, o=sel[:, 0:1], i=eq[:, 0:1]: e.tensor_copy(o, i),
             reads=["zf"], writes=["zf"], name="gsel0")
        cview = kb.coeff[:, :].rearrange("p (e t) -> p t e", t=16)[:, tc, :]
        p.op("vector", lambda e, o=cview, i=sel, s=kb.selw[:, tc:tc + 1]:
             e.tensor_single_scalar(o, i, s, ALU.mult),
             reads=["zf", f"sw{tc}"], writes=[f"coef{tc}"], name="gcoef")


def moe_experts(kb, w1_3d, w2_3d, b1_2d, b2_2d):
    for ex in range(NE):
        load_col_tile(kb, b1_2d[ex], DFF, kb.bup_col, 0, "bup")
        load_row(kb, kb.row0, b2_2d[ex], 1024, "row0")
        up = simple_up_loader(kb, w1_3d[ex])
        dn = simple_dn_loader(kb, w2_3d[ex])

        def coeff_fn(tc, ex=ex):
            return (kb.coeff[:, ex * 16 + tc: ex * 16 + tc + 1], f"coef{tc}")
        ffn_stage(kb, up, dn, AF.Relu, kb.bup_col, None, coeff_fn, add_b2=True,
                  bias_key="bup", scale_key=None)




def _v_phase(kb, bias_row, bias_key):
    T = kb.T
    vbase = 8192
    wkeys = kb.ark(0, 8192)
    for tc in range(kb.TC):
        for nh in range(2):
            for dcc in range(DC):
                kb.mm(kb.psBIG[:, nh * 512:(nh + 1) * 512],
                      kb.hnT[:, dcc * T + tc * 128: dcc * T + (tc + 1) * 128],
                      kb.ar[:, dcc * 1024 + nh * 512: dcc * 1024 + (nh + 1) * 512],
                      start=(dcc == 0), stop=(dcc == DC - 1 and bias_row is None),
                      reads=wkeys + [f"hnT{dcc}"], writes=kb.pbk(nh * 512, 512),
                      name="vmm")
            if bias_row is not None:
                kb.mm(kb.psBIG[:, nh * 512:(nh + 1) * 512], kb.ones_bf[0:1, :],
                      bias_row[0:1, nh * 512:(nh + 1) * 512], start=False, stop=True,
                      reads=["ones", bias_key], writes=kb.pbk(nh * 512, 512),
                      name="vbias")
        kb.act(kb.ar[:, vbase + tc * 1024: vbase + (tc + 1) * 1024],
               kb.psBIG[:, 0:1024], AF.Identity,
               reads=kb.pbk(0, 1024), writes=kb.ark(vbase + tc * 1024, 1024),
               name="vevac")


def _qk_pair(kb, wq_flat, wk_flat, j, qbias, kbias):
    T = kb.T
    wqb, wkb = 2 * T, 2 * T + 1024
    load_wblock(kb, wq_flat[:, j * 128:(j + 1) * 128], kb.ar, wqb,
                kb.ark(wqb, 1024), ch=0)
    load_wblock(kb, wk_flat[:, j * 128:(j + 1) * 128], kb.ar, wkb,
                kb.ark(wkb, 1024), ch=1)
    for tgq in range(T // 512):
        for dcc in range(DC):
            kb.mm(kb.psB0[:, :], kb.ar[:, wqb + dcc * 128: wqb + (dcc + 1) * 128],
                  kb.hnT[:, dcc * T + tgq * 512: dcc * T + tgq * 512 + 512],
                  start=(dcc == 0), stop=(dcc == DC - 1),
                  reads=kb.ark(wqb, 1024) + [f"hnT{dcc}"], writes=["psB0"],
                  name="qmm")
        kb.act(kb.ar[:, tgq * 512:(tgq + 1) * 512], kb.psB0[:, :], AF.Identity,
               bias=(qbias[0][:, j:j + 1] if qbias else 0.0), scale=0.125,
               reads=["psB0"] + ([qbias[1]] if qbias else []),
               writes=kb.ark(tgq * 512, 512), name="qevac")
        for dcc in range(DC):
            kb.mm(kb.psB1[:, :], kb.ar[:, wkb + dcc * 128: wkb + (dcc + 1) * 128],
                  kb.hnT[:, dcc * T + tgq * 512: dcc * T + tgq * 512 + 512],
                  start=(dcc == 0), stop=(dcc == DC - 1),
                  reads=kb.ark(wkb, 1024) + [f"hnT{dcc}"], writes=["psB1"],
                  name="kmm")
        kb.act(kb.ar[:, T + tgq * 512: T + (tgq + 1) * 512], kb.psB1[:, :], AF.Identity,
               bias=(kbias[0][:, j:j + 1] if kbias else 0.0),
               reads=["psB1"] + ([kbias[1]] if kbias else []),
               writes=kb.ark(T + tgq * 512, 512), name="kevac")


def _l0_head_A(kb, j, h_loc, qc, ci):
    T = kb.T
    par = ci % 2
    s = ci % 32
    nm, nmk = kb.stat("nm", s)
    es, esk = kb.stat("es", s)
    rc, rck = kb.stat("rc", s)
    qk = kb.ark(qc * 128, 128)
    q_ap = kb.ar[h_loc:h_loc + 64, qc * 128:(qc + 1) * 128]
    nkc = 1 if qc == 0 else 2
    width = nkc * 128
    if qc == 0:
        kb.mm(kb.psS[:, par * 256: par * 256 + 128], q_ap,
              kb.ar[h_loc:h_loc + 64, T:T + 128],
              start=True, stop=True, reads=qk + kb.ark(T, 128),
              writes=["psS"], name="smm")
        mview = kb.mask[:, 128:256]
    else:
        kb.mm(kb.psS[:, par * 256: par * 256 + 256], q_ap,
              kb.ar[h_loc:h_loc + 64, T + (qc - 1) * 128: T + (qc + 1) * 128],
              start=True, stop=True,
              reads=qk + kb.ark(T + (qc - 1) * 128, 256), writes=["psS"],
              name="smm")
        mview = kb.mask[:, 0:256]
    pr = kb.probs[:, par * 512: par * 512 + width]
    prk = f"probs{par}"
    kb.p.op("vector", lambda e, o=pr, a=kb.psS[:, par * 256: par * 256 + width],
            b=mview: e.tensor_add(o, a, b),
            reads=["psS", "mask"], writes=[prk], name="smask")
    kb.p.op("vector", lambda e, o=nm, i=pr: e.reduce_max(o, i, AX.X, negate=True),
            reads=[prk], writes=[nmk], name="smax")
    P = kb.zbf[:, par * 512: par * 512 + width]
    Pk = f"zbfp{par}"
    kb.act(P, pr, AF.Exp, bias=nm, accum_out=es,
           reads=[prk, nmk], writes=[Pk, esk], name="sexp")
    kb.p.op("vector", lambda e, o=rc, i=es: e.reciprocal(o, i),
            reads=[esk], writes=[rck], name="srecip")
    kb.p.op("vector", lambda e, o=P, s_=rc: e.tensor_scalar_mul(o, o, s_),
            reads=[Pk, rck], writes=[Pk], name="snormv")


def _l0_head_B(kb, j, h_loc, qc, ci):
    T = kb.T
    vbase, aobase = 8192, 8192 + 8 * T
    head = 2 * j + (1 if h_loc else 0)
    par = ci % 2
    nkc = 1 if qc == 0 else 2
    width = nkc * 128
    Pk = f"zbfp{par}"
    PTk = f"zbft{par}"
    for kk in range(nkc):
        slot = par * 2 + kk
        kb.tr(kb.psT[:, slot * 128:(slot + 1) * 128],
              kb.zbf[:, par * 512 + kk * 128: par * 512 + (kk + 1) * 128],
              kb.ident_bf[:, :], reads=[Pk], writes=["psT"], name="str")
    kb.vop(lambda e, o=kb.zbf[:, par * 512 + 256: par * 512 + 256 + width],
           i=kb.psT[:, par * 256: par * 256 + width]: e.tensor_copy(o, i),
           reads=["psT"], writes=[PTk],
           name="sTevac")
    psA = kb.psB0 if par == 0 else kb.psB1
    psAk = "psB0" if par == 0 else "psB1"
    for kk in range(nkc):
        kc = qc - 1 + kk if qc > 0 else 0
        kb.mm(psA[h_loc:h_loc + 64, 0:128],
              kb.ar[:, vbase + kc * 1024 + head * 64: vbase + kc * 1024 + head * 64 + 64],
              kb.zbf[:, par * 512 + 256 + kk * 128: par * 512 + 256 + (kk + 1) * 128],
              start=(kk == 0), stop=(kk == nkc - 1),
              reads=kb.ark(vbase + kc * 1024, 1024) + [PTk], writes=[psAk],
              name="aomm", tile_position=(0, h_loc))
    kb.act(kb.ar[h_loc:h_loc + 64, aobase + j * T + qc * 128:
                 aobase + j * T + (qc + 1) * 128],
           psA[h_loc:h_loc + 64, 0:128], AF.Identity,
           reads=[psAk], writes=kb.ark(aobase + j * T + qc * 128, 128),
           name="aoevac")


def _ssm_u_scan(kb, acol_key="acol"):
    T = kb.T
    p = kb.p
    p.op("vector", lambda e: e.memset(kb.st_sb[:, 0:512], 1.0),
         writes=["st"], name="stones")
    p.op("vector", lambda e, o=kb.ab_sb[:, 0:512], i=kb.st_sb[:, 0:512],
         s=kb.acol[:, 0:1]: e.tensor_single_scalar(o, i, s, ALU.mult),
         reads=["st", acol_key], writes=["ab"], name="abmul")
    for s in range(T // 512):
        ps = kb.psB0 if s % 2 == 0 else kb.psB1
        psk = "psB0" if s % 2 == 0 else "psB1"
        for dcc in range(DC):
            kb.mm(ps[0:64, :], kb.bw_sb[:, dcc * 64:(dcc + 1) * 64],
                  kb.hnT[:, dcc * T + s * 512: dcc * T + s * 512 + 512],
                  start=(dcc == 0), stop=(dcc == DC - 1),
                  reads=["bw", f"hnT{dcc}"], writes=[psk], name="umm")
        init = 0.0 if s == 0 else kb.st_sb[:, s * 512 - 1: s * 512]
        p.op("vector",
             lambda e, o=kb.st_sb[:, s * 512:(s + 1) * 512],
             a=kb.ab_sb[:, 0:512], b=ps[0:64, :], ini=init:
             e.tensor_tensor_scan(o, a, b, ini, ALU.mult, ALU.add),
             reads=["ab", psk, "st"], writes=["st"], name="scan")


def _l0_combine(kb, tc, wg_base=8192):
    T = kb.T
    aobase = 8192 + 8 * T
    p = kb.p
    kb.mm(kb.psB0[:, :], kb.st_sb[:, tc * 128:(tc + 1) * 128],
          kb.cw_sb[:, 0:512], start=True, stop=True,
          reads=["st", "cw"], writes=["psB0"], name="ssmmm0")
    kb.mm(kb.psB1[:, :], kb.st_sb[:, tc * 128:(tc + 1) * 128],
          kb.cw_sb[:, 512:1024], start=True, stop=True,
          reads=["st", "cw"], writes=["psB1"], name="ssmmm1")
    kb.act(kb.probs[:, 0:512], kb.psB0[:, :], AF.Identity,
           reads=["psB0"], writes=["probs"], name="ssmev0")
    kb.act(kb.probs[:, 512:1024], kb.psB1[:, :], AF.Identity,
           reads=["psB1"], writes=["probs"], name="ssmev1")
    for nh in range(2):
        pk = kb.pbk(nh * 512, 512)
        for j in range(NPAIR):
            kb.mm(kb.psBIG[:, nh * 512:(nh + 1) * 512],
                  kb.ar[:, aobase + j * T + tc * 128: aobase + j * T + (tc + 1) * 128],
                  kb.ar[:, j * 1024 + nh * 512: j * 1024 + (nh + 1) * 512],
                  start=(j == 0), stop=False,
                  reads=kb.ark(aobase + j * T + tc * 128, 128) +
                  kb.ark(j * 1024 + nh * 512, 512),
                  writes=pk, name="prmm")
        kb.mm(kb.psBIG[:, nh * 512:(nh + 1) * 512], kb.ones_bf[0:1, :],
              kb.row2[0:1, nh * 512:(nh + 1) * 512], start=False, stop=True,
              reads=["ones", "row2"], writes=pk, name="prbias")
    for nh in range(2):
        pk = kb.pbk(1024 + nh * 512, 512)
        for dcc in range(DC):
            kb.mm(kb.psBIG[:, 1024 + nh * 512: 1024 + (nh + 1) * 512],
                  kb.hnT[:, dcc * T + tc * 128: dcc * T + (tc + 1) * 128],
                  kb.ar[:, wg_base + dcc * 1024 + nh * 512:
                        wg_base + dcc * 1024 + (nh + 1) * 512],
                  start=(dcc == 0), stop=False,
                  reads=[f"hnT{dcc}"] + kb.ark(wg_base + dcc * 1024 + nh * 512, 512),
                  writes=pk, name="gmm")
        kb.mm(kb.psBIG[:, 1024 + nh * 512: 1024 + (nh + 1) * 512], kb.ones_bf[0:1, :],
              kb.row1[0:1, nh * 512:(nh + 1) * 512], start=False, stop=True,
              reads=["ones", "row1"], writes=pk, name="gbias")
    kb.act(kb.zf[:, :], kb.psBIG[:, 1024:2048], AF.Sigmoid,
           reads=kb.pbk(1024, 1024), writes=["zf"], name="gsig")
    xk = f"x{tc}"
    xv = kb.x[:, tc * 1024:(tc + 1) * 1024]
    for hh in range(2):
        xh = kb.x[:, tc * 1024 + hh * 512: tc * 1024 + (hh + 1) * 512]
        ph = kb.probs[:, hh * 512:(hh + 1) * 512]
        p.op("vector", lambda e, o=kb.zT[:, :],
             a=kb.psBIG[:, hh * 512:(hh + 1) * 512], b=ph:
             e.tensor_sub(o, a, b),
             reads=kb.pbk(hh * 512, 512) + ["probs"], writes=["zT"], name="csub")
        p.op("vector", lambda e, o=kb.zT[:, :], a=kb.zT[:, :],
             b=kb.zf[:, hh * 512:(hh + 1) * 512]:
             e.tensor_mul(o, a, b), reads=["zT", "zf"], writes=["zT"], name="cmul")
        p.op("vector", lambda e, o=ph, a=ph, b=kb.zT[:, :]:
             e.tensor_add(o, a, b), reads=["probs", "zT"], writes=["probs"],
             name="ccombo")
        kb.act(kb.zT[:, :], xh, AF.Identity, bias=kb.nmr_o[:, tc:tc + 1],
               scale=kb.rstd_o[:, tc:tc + 1],
               reads=[xk, f"orstd{tc}", f"onmr{tc}"], writes=["zT"], name="crez")
        p.op("vector", lambda e, o=ph, a=ph, b=kb.zT[:, :]:
             e.tensor_add(o, a, b), reads=["probs", "zT"], writes=["probs"],
             name="caddz")
    p.dma("sync", kb.x1_scr[tc * 128:(tc + 1) * 128, :], kb.probs[:, 0:1024],
          reads=["probs"], writes=[f"x1s{tc}"], name="x1spill", ch=0)
    p.op("vector", lambda e, o=xv, a=xv, b=kb.probs[:, 0:1024]:
         e.tensor_add(o, a, b), reads=[xk, "probs"], writes=[xk], name="cadd1")


def _outproj_add(kb, tc, bias_row, bias_key):
    T = kb.T
    aobase = 8192 + 8 * T
    for nh in range(2):
        pk = kb.pbk(nh * 512, 512)
        for j in range(NPAIR):
            kb.mm(kb.psBIG[:, nh * 512:(nh + 1) * 512],
                  kb.ar[:, aobase + j * T + tc * 128: aobase + j * T + (tc + 1) * 128],
                  kb.ar[:, j * 1024 + nh * 512: j * 1024 + (nh + 1) * 512],
                  start=(j == 0), stop=False,
                  reads=kb.ark(aobase + j * T + tc * 128, 128) +
                  kb.ark(j * 1024 + nh * 512, 512),
                  writes=pk, name="pr1mm")
        kb.mm(kb.psBIG[:, nh * 512:(nh + 1) * 512], kb.ones_bf[0:1, :],
              bias_row[0:1, nh * 512:(nh + 1) * 512], start=False, stop=True,
              reads=["ones", bias_key], writes=pk, name="pr1bias")
    xk = f"x{tc}"
    xv = kb.x[:, tc * 1024:(tc + 1) * 1024]
    kb.p.op("vector", lambda e, o=xv, a=kb.psBIG[:, 0:1024]:
            e.scalar_tensor_tensor(o, a, 1.0, o, ALU.mult, ALU.add),
            reads=kb.pbk(0, 1024) + [xk], writes=[xk], name="pr1evac")


def attn_l0(kb, W, x1_scr):
    T = kb.T
    kb.x1_scr = x1_scr
    load_row(kb, kb.row1, W["bV"], 1024, "row1")
    load_row(kb, kb.row2, W["bO"], 1024, "row2")
    load_col_tile(kb, W["bQ"], 1024, kb.qkvcol, 0, "qcol", prescale=0.125)
    load_col_tile(kb, W["bK"], 1024, kb.qkvcol, 8, "kcol")
    kb.p.dma("sync", kb.acol[:, 0:1], W["A"][:, None], writes=["acol"],
             name="aload", ch=1)
    kb.p.dma("gpsimd", kb.cw_sb[:, :], W["Cw"][:, :], writes=["cw"],
             name="cwload", ch=2)
    load_wblock(kb, W["Bw"][:, :], kb.bw_sb, 0, ["bw"], ch=1)
    load_wblock(kb, W["WV"][:, :], kb.ar, 0, kb.ark(0, 8192))
    _v_phase(kb, kb.row1, "row1")
    _ssm_u_scan(kb)
    ci = 0
    queue = []
    LAG0 = 2
    for j in range(NPAIR):
        _qk_pair(kb, W["WQ"][:, :], W["WK"][:, :], j,
                 (kb.qkvcol[:, 0:8], "qcol"), (kb.qkvcol[:, 8:16], "kcol"))
        for h_loc in (0, 64):
            for qc in range(kb.TC):
                if len(queue) >= LAG0:
                    _l0_head_B(kb, *queue.pop(0))
                _l0_head_A(kb, j, h_loc, qc, ci)
                queue.append((j, h_loc, qc, ci))
                ci += 1
    for it in queue:
        _l0_head_B(kb, *it)
    if not globals().get("ATTN0_SKIP_P3", False):
        wg_base = 8192 if 8 * T >= 8192 else 8192 + 16 * T
        load_row(kb, kb.row1, W["bg"], 1024, "row1")
        load_wblock(kb, W["WO"][:, :], kb.ar, 0, kb.ark(0, 8192))
        load_wblock(kb, W["Wg"][:, :], kb.ar, wg_base, kb.ark(wg_base, 8192), ch=1)
        for tc in range(kb.TC):
            _l0_combine(kb, tc, wg_base)


def attn_l1(kb, W):
    T = kb.T
    load_row(kb, kb.row2, W["b_out"], 1024, "row2")
    wqf = W["wq"][:, :, :].rearrange("d h e -> d (h e)")
    wkf = W["wk"][:, :, :].rearrange("d h e -> d (h e)")
    wvf = W["wv"][:, :, :].rearrange("d h e -> d (h e)")
    load_wblock(kb, wvf, kb.ar, 0, kb.ark(0, 8192))
    _v_phase(kb, None, None)
    ci = 0
    queue = []
    LAG1 = 3
    for j in range(A1_NPAIR):
        _qk_pair(kb, wqf, wkf, j, None, None)
        for h_loc in (0, 64):
            for qc in range(kb.TC):
                if len(queue) >= LAG1:
                    _l1_head_B(kb, *queue.pop(0))
                _l1_head_A(kb, j, h_loc, qc, ci)
                queue.append((j, h_loc, qc, ci))
                ci += 1
    for it in queue:
        _l1_head_B(kb, *it)
    kb.p.op("vector", lambda e, o=kb.zf[:, 0:1]: e.tensor_copy(o, kb.zf[:, 0:1]),
            reads=["zfb0", "zfb1", "zfb2", "zfb3",
                   "prb0", "prb1", "prb2", "prb3"], writes=["zf", "probs"],
            name="zfence")
    if not globals().get("ATTN1_SKIP_PROJ", False):
        for j in range(NPAIR):
            kb.p.dma("gpsimd", kb.ar[:, j * 1024:(j + 1) * 1024],
                     W["w_out"][2 * j:2 * j + 2, :, :].rearrange("h e d -> (h e) d"),
                     writes=kb.ark(j * 1024, 1024), name="woload", ch=1)
        for tc in range(kb.TC):
            _outproj_add(kb, tc, kb.row2, "row2")


def _l1_pbuf(kb, par):
    if par == 0:
        base = 2 * kb.T + 2048
        ap = lambda lo, n: kb.ar[:, base + lo: base + lo + n]
        keys = lambda lo, n: kb.ark(base + lo, n)
    elif par == 1:
        zb = kb.zf[:, :].bitcast(BF16)
        ap = lambda lo, n, z=zb: z[:, lo: lo + n]
        keys = lambda lo, n: ["zfb%d" % i for i in
                              range(lo // 512, (lo + n + 511) // 512)]
    else:
        pb = kb.probs[:, :].bitcast(BF16)
        ap = lambda lo, n, z=pb: z[:, lo: lo + n]
        keys = lambda lo, n: ["prb%d" % i for i in
                              range(lo // 512, (lo + n + 511) // 512)]
    return ap, keys


def _l1_head_A(kb, j, h_loc, qc, ci):
    T = kb.T
    par = ci % 2
    s = ci % 32
    nm, nmk = kb.stat("nm", s)
    es, esk = kb.stat("es", s)
    rc, rck = kb.stat("rc", s)
    c4 = (ci % 4) * 4
    pap, pkeys = _l1_pbuf(kb, ci % 3)
    q_ap = kb.ar[h_loc:h_loc + 64, qc * 128:(qc + 1) * 128]
    for sq in range(4):
        kb.mm(kb.psBIG[:, sq * 512:(sq + 1) * 512], q_ap,
              kb.ar[h_loc:h_loc + 64, T + sq * 512: T + (sq + 1) * 512],
              start=True, stop=True,
              reads=kb.ark(qc * 128, 128) + kb.ark(T + sq * 512, 512),
              writes=[f"pb{sq}"], name="smm1")
        kb.p.op("vector", lambda e, o=kb.rstd_o[:, c4 + sq:c4 + sq + 1],
                i=kb.psBIG[:, sq * 512:(sq + 1) * 512]:
                e.reduce_max(o, i, AX.X),
                reads=[f"pb{sq}"], writes=[f"orstd{c4 + sq}"], name="qmax")
    kb.p.op("vector", lambda e, o=nm, i=kb.rstd_o[:, c4:c4 + 4]:
            e.reduce_max(o, i, AX.X, negate=True),
            reads=[f"orstd{c4 + sq}" for sq in range(4)], writes=[nmk],
            name="gmax1")
    for sq in range(4):
        kb.act(pap(sq * 512, 512), kb.psBIG[:, sq * 512:(sq + 1) * 512],
               AF.Exp, bias=nm, accum_out=kb.nmr_o[:, c4 + sq:c4 + sq + 1],
               reads=[f"pb{sq}", nmk],
               writes=pkeys(sq * 512, 512) + [f"onmr{c4 + sq}"], name="sexp1")
    kb.p.op("vector", lambda e, o=es, i=kb.nmr_o[:, c4:c4 + 4]:
            e.reduce_sum(o, i, AX.X),
            reads=[f"onmr{c4 + sq}" for sq in range(4)], writes=[esk],
            name="esum1")
    kb.p.op("vector", lambda e, o=rc, i=es: e.reciprocal(o, i),
            reads=[esk], writes=[rck], name="srecip1")
    for sq in range(4):
        kb.p.op("vector", lambda e, o=pap(sq * 512, 512), s_=rc:
                e.tensor_scalar_mul(o, o, s_),
                reads=pkeys(sq * 512, 512) + [rck],
                writes=pkeys(sq * 512, 512), name="snorm1")


def _l1_head_B(kb, j, h_loc, qc, ci):
    T = kb.T
    vbase, aobase = 8192, 8192 + 8 * T
    head = 2 * j + (1 if h_loc else 0)
    par = ci % 2
    pap, pkeys = _l1_pbuf(kb, ci % 3)
    for r in range(2):
        for kk in range(8):
            c = r * 8 + kk
            kb.tr(kb.psT[:, kk * 128:(kk + 1) * 128], pap(c * 128, 128),
                  kb.ident_bf[:, :], reads=pkeys(c * 128, 128), writes=["psT"],
                  name="str1")
        kb.vop(lambda e, o=pap(r * 1024, 1024), i=kb.psT[:, 0:1024]:
               e.tensor_copy(o, i),
               reads=["psT"], writes=pkeys(r * 1024, 1024), name="sTevac1")
    psA = kb.psB0 if par == 0 else kb.psB1
    psAk = "psB0" if par == 0 else "psB1"
    for kc in range(kb.TC):
        kb.mm(psA[h_loc:h_loc + 64, 0:128],
              kb.ar[:, vbase + kc * 1024 + head * 64: vbase + kc * 1024 + head * 64 + 64],
              pap(kc * 128, 128),
              start=(kc == 0), stop=(kc == kb.TC - 1),
              reads=kb.ark(vbase + kc * 1024, 1024) + pkeys(kc * 128, 128),
              writes=[psAk], name="aomm1", tile_position=(0, h_loc))
    kb.act(kb.ar[h_loc:h_loc + 64, aobase + j * T + qc * 128:
                 aobase + j * T + (qc + 1) * 128],
           psA[h_loc:h_loc + 64, 0:128], AF.Identity,
           reads=[psAk], writes=kb.ark(aobase + j * T + qc * 128, 128),
           name="aoevac1")




def load_x(kb, x_dram):
    kb.p.dma("sync", kb.x[:, :].rearrange("p (c j) -> p c j", c=kb.TC),
             x_dram[:, :].rearrange("(c p) j -> p c j", p=128),
             writes=[f"x{c}" for c in range(kb.TC)], name="xload")


def store_y(kb, y_dram):
    kb.p.dma("sync", y_dram[:, :].rearrange("(c p) j -> p c j", p=128),
             kb.x[:, :].rearrange("p (c j) -> p c j", c=kb.TC),
             reads=[f"x{c}" for c in range(kb.TC)], name="ystore")
    kb.p.barrier_all_dma()


def load_consts(kb, cid_bf, cid_f, cmask):
    kb.p.dma("gpsimd", kb.ident_bf[:, :], cid_bf[:, :], writes=["idb"],
             name="idbload", ch=2)
    kb.p.dma("sync", kb.ident_f[:, :], cid_f[:, :], writes=["idf"],
             name="idfload", ch=1)
    kb.p.dma("gpsimd", kb.mask[:, :], cmask[:, :], writes=["mask"],
             name="maskload", ch=2)
    kb.p.op("vector", lambda e: e.memset(kb.ones_bf[0:1, :], 1.0),
            writes=["ones"], name="onesset")


STAGE_NAMES = ("ffnA", "attn0", "ffnB", "moe1", "attn1", "moe2")

A1_OLDMAX = False
A1_NPAIR = NPAIR
A1_NOACCUM = False
A1_SKIPNORM = False
A1_NOPAR = False
A1_NOSTATB = False
A1_CUT = 5

DEBUG_TAPS = []


def build_program(T=2048, stages=STAGE_NAMES, n_moe_experts=NE):
    import contextlib
    ctx = contextlib.ExitStack()
    nc = bass.Bass("TRN2", target_bir_lowering=False)
    p = Plan()

    dt = {}

    def din(name, shape):
        dt[name] = nc.dram_tensor(name, list(shape), F32, kind="ExternalInput")
        return dt[name]

    din("L0_attn_x", (T, 1024))
    for n, s in [("L0_attn_ln1_g", (1024,)), ("L0_attn_ln1_b", (1024,)),
                 ("L0_attn_WQ", (1024, 1024)), ("L0_attn_bQ", (1024,)),
                 ("L0_attn_WK", (1024, 1024)), ("L0_attn_bK", (1024,)),
                 ("L0_attn_WV", (1024, 1024)), ("L0_attn_bV", (1024,)),
                 ("L0_attn_WO", (1024, 1024)), ("L0_attn_bO", (1024,)),
                 ("L0_attn_Wg", (1024, 1024)), ("L0_attn_bg", (1024,)),
                 ("L0_attn_A", (64,)), ("L0_attn_Bw", (1024, 64)),
                 ("L0_attn_Cw", (64, 1024)),
                 ("L0_attn_ln2_g", (1024,)), ("L0_attn_ln2_b", (1024,)),
                 ("L0_attn_W1", (1024, 4096)), ("L0_attn_b1", (4096,)),
                 ("L0_attn_W2", (4096, 1024)), ("L0_attn_b2", (1024,)),
                 ("L0_ffn_w1", (4096, 1024)), ("L0_ffn_b1", (4096,)),
                 ("L0_ffn_w2", (1024, 4096)), ("L0_ffn_b2", (1024,)),
                 ("L1_attn_norm_scale", (1024,)),
                 ("L1_attn_wq", (1024, 16, 64)), ("L1_attn_wk", (1024, 16, 64)),
                 ("L1_attn_wv", (1024, 16, 64)), ("L1_attn_w_out", (16, 64, 1024)),
                 ("L1_attn_b_out", (1024,)),
                 ("L1_ffn_gate_w", (8, 1024)), ("L1_ffn_w1", (8, 1024, 4096)),
                 ("L1_ffn_b1", (8, 4096)), ("L1_ffn_w2", (8, 4096, 1024)),
                 ("L1_ffn_b2", (8, 1024))]:
        din(n, s)
    din("c_ident_bf", (128, 128))
    din("c_ident_f", (128, 128))
    din("c_mask", (128, 256))
    y = nc.dram_tensor("y", [T, 1024], F32, kind="ExternalOutput")
    scr_up = nc.dram_tensor("scr_up", [4, 128, 8192], BF16, kind="Internal")
    scr_dn = nc.dram_tensor("scr_dn", [4, 128, 8192], BF16, kind="Internal")

    kb = KB(nc, p, T, ctx)
    for v in (EPS_LN, EPS_RMS, -0.5, 0.125, -1.0):
        kb.register_const(v)
    load_consts(kb, dt["c_ident_bf"], dt["c_ident_f"], dt["c_mask"])
    load_x(kb, dt["L0_attn_x"])

    def ffn_l0(first):
        load_col_tile(kb, dt["L0_ffn_b1"], 4096, kb.bup_col, 0, "bup")
        load_row(kb, kb.row0, dt["L0_ffn_b2"], 1024, "row0")
        if first:
            alpha2_prepass(kb, dt["L0_ffn_w2"][:, :])
            up = sign_up_loader(kb, dt["L0_ffn_w1"][:, :], scr_up)
            dn = sign_dn_loader(kb, dt["L0_ffn_w2"][:, :], scr_dn)
        else:
            up = scratch_up_loader(kb, scr_up)
            dn = scratch_dn_loader(kb, scr_dn)
        ffn_stage(kb, up, dn, AF.Gelu, kb.bup_col, kb.alpha1,
                  lambda tc: 0.5, add_b2=True,
                  bias_key="bup", scale_key="alpha_w")

    if "ffnA" in stages:
        ln_stage(kb, "plain")
        ffn_l0(True)
    if "attn0" in stages:
        x1_scr = nc.dram_tensor("x1_scr", [T, 1024], F32, kind="Internal")
        load_col_tile(kb, dt["L0_attn_ln1_g"], 1024, kb.gcol, 0, "gcol")
        load_col_tile(kb, dt["L0_attn_ln1_b"], 1024, kb.bcol, 0, "bcol")
        ln_stage(kb, "double_gb", kb.gcol, kb.bcol)
        attn_l0(kb, dict(
            WQ=dt["L0_attn_WQ"], WK=dt["L0_attn_WK"], WV=dt["L0_attn_WV"],
            WO=dt["L0_attn_WO"], Wg=dt["L0_attn_Wg"],
            bQ=dt["L0_attn_bQ"], bK=dt["L0_attn_bK"], bV=dt["L0_attn_bV"],
            bO=dt["L0_attn_bO"], bg=dt["L0_attn_bg"],
            A=dt["L0_attn_A"], Bw=dt["L0_attn_Bw"], Cw=dt["L0_attn_Cw"]), x1_scr)
        load_col_tile(kb, dt["L0_attn_ln2_g"], 1024, kb.gcol, 0, "gcol")
        load_col_tile(kb, dt["L0_attn_ln2_b"], 1024, kb.bcol, 0, "bcol")
        ln_stage(kb, "single_gb_dram", kb.gcol, kb.bcol)
        load_col_tile(kb, dt["L0_attn_b1"], 4096, kb.bup_col, 0, "bup")
        load_row(kb, kb.row0, dt["L0_attn_b2"], 1024, "row0")
        ffn_stage(kb, simple_up_loader(kb, dt["L0_attn_W1"][:, :]),
                  simple_dn_loader(kb, dt["L0_attn_W2"][:, :]),
                  AF.Gelu, kb.bup_col, None, lambda tc: 1.0, add_b2=True,
                  bias_key="bup", scale_key=None)
    if "ffnB" in stages:
        ln_stage(kb, "plain")
        ffn_l0(False)
    for mname in ("moe1", "moe2"):
        if mname in stages:
            moe_prep_gate(kb, dt["L1_ffn_gate_w"])
            ln_stage(kb, "plain", gate=True)
            moe_gate_softmax(kb)
            moe_experts(kb, dt["L1_ffn_w1"], dt["L1_ffn_w2"],
                        dt["L1_ffn_b1"], dt["L1_ffn_b2"])
        if mname == "moe1" and "attn1" in stages:
            load_col_tile(kb, dt["L1_attn_norm_scale"], 1024, kb.gcol, 0, "gcol")
            ln_stage(kb, "double_rms", kb.gcol)
            attn_l1(kb, dict(
                wq=dt["L1_attn_wq"], wk=dt["L1_attn_wk"], wv=dt["L1_attn_wv"],
                w_out=dt["L1_attn_w_out"], b_out=dt["L1_attn_b_out"]))

    for tname, tshape, tfn in DEBUG_TAPS:
        tdram = nc.dram_tensor(tname, list(tshape), F32, kind="ExternalOutput")
        ap, rkeys = tfn(kb)
        p.dma("gpsimd", tdram[:, :] if len(tshape) == 2 else tdram[:],
              ap, reads=rkeys, name="tap", ch=3)

    store_y(kb, y)

    p.finalize()
    sems = {}
    for axis in p.used_axes():
        sems[axis] = ctx.enter_context(nc.semaphore(axis))
    p.emit(nc, sems)
    ctx.close()
    return nc, p




def make_consts():
    cid_bf = np.eye(128, dtype=np.float32)
    cid_f = np.eye(128, dtype=np.float32)
    m0 = np.where(np.arange(256)[None, 0:128] > np.arange(128)[:, None],
                  0.0, NEG).astype(np.float32)
    m1 = np.where(np.arange(128)[None, :] <= np.arange(128)[:, None],
                  0.0, NEG).astype(np.float32)
    cmask = np.concatenate([m0, m1], axis=1).astype(np.float32)
    return cid_bf, cid_f, cmask


_CACHED = {}


def kernel(**inputs):
    T = 2048
    key = "full"
    if key not in _CACHED:
        _CACHED[key] = build_program(T)
    nc, _ = _CACHED[key]
    cid_bf, cid_f, cmask = make_consts()
    x = np.ascontiguousarray(np.asarray(inputs["L0_attn_x"], dtype=np.float32))
    base = {}
    for k, v in inputs.items():
        if k == "L0_attn_x":
            continue
        base[k] = np.ascontiguousarray(np.asarray(v, dtype=np.float32))
    base["c_ident_bf"] = cid_bf
    base["c_ident_f"] = cid_f
    base["c_mask"] = cmask
    in_maps = []
    for core in range(8):
        m = dict(base)
        m["L0_attn_x"] = x[core]
        in_maps.append(m)
    res = run_bass_kernel_spmd(nc, in_maps, core_ids=list(range(8)))
    out = np.stack([r["y"] for r in res.results], axis=0)
    return out.astype(np.float32)


# version 08
# speedup vs baseline: 1.8571x; 1.4355x over previous
import numpy as np
import concourse.bass as bass
import concourse.mybir as mybir
from concourse.bass_utils import run_bass_kernel_spmd
from concourse.bass_core_rust import IndirectOffsetOnAxis

F32 = mybir.dt.float32
BF16 = mybir.dt.bfloat16
U32 = mybir.dt.uint32
AF = mybir.ActivationFunctionType
ALU = mybir.AluOpType
AX = mybir.AxisListType

ENGINES = ("sync", "scalar", "vector", "tensor", "gpsimd")
DMA_INC = 16



class Op:
    __slots__ = ("engine", "axis", "fn", "deps", "eidx", "tagged", "waits",
                 "clock", "rank", "name")

    def __repr__(self):
        return f"<Op {self.engine}/{self.axis}#{self.eidx} {self.name}>"


class Plan:

    def __init__(self):
        self.ops = []
        self.axis_count = {}
        self.last_write = {}
        self.readers = {}

    def _add(self, engine, axis, fn, reads, writes, deps, name):
        op = Op()
        op.engine = engine
        op.axis = axis
        op.fn = fn
        op.name = name
        op.tagged = "_dma" in axis
        op.waits = None
        op.clock = None
        op.rank = None
        d = []
        for k in reads:
            w = self.last_write.get(k)
            if w is not None:
                d.append(w)
        for k in writes:
            w = self.last_write.get(k)
            if w is not None:
                d.append(w)
            rs = self.readers.get(k)
            if rs:
                d.extend(rs)
        for x in deps:
            if x is not None:
                d.append(x)
        op.deps = d
        op.eidx = self.axis_count.get(axis, 0)
        self.axis_count[axis] = op.eidx + 1
        for k in reads:
            self.readers.setdefault(k, []).append(op)
        for k in writes:
            self.last_write[k] = op
            self.readers[k] = []
        self.ops.append(op)
        return op

    def op(self, engine, fn, reads=(), writes=(), deps=(), name=""):
        return self._add(engine, engine, fn, reads, writes, deps, name)

    def dma(self, engine, out, in_, reads=(), writes=(), deps=(), name="", ch=0,
            **kw):
        def fn(eng, _out=out, _in=in_, _kw=kw):
            return eng.dma_start(_out, _in, **_kw)
        return self._add(engine, "%s_dma%d" % (engine, ch), fn, reads, writes,
                         deps, name)

    def barrier_all_dma(self, engine="gpsimd"):
        deps = []
        for axis, cnt in self.axis_count.items():
            if "_dma" in axis and cnt > 0:
                for o in reversed(self.ops):
                    if o.axis == axis:
                        deps.append(o)
                        break
        self._add(engine, engine, None, (), (), deps, "final_dma_barrier")

    def finalize(self):
        observed = {e: {} for e in ENGINES}
        axis_ops = {}
        for op in self.ops:
            axis_ops.setdefault(op.axis, []).append(op)
        issued = {}
        for op in self.ops:
            obs = observed[op.engine]
            need = {}
            for dop in op.deps:
                v = dop.eidx + 1
                if need.get(dop.axis, 0) < v:
                    need[dop.axis] = v
            if "_dma" in op.axis and op.eidx > 0:
                v = op.eidx
                if need.get(op.axis, 0) < v:
                    need[op.axis] = v
            waits = []
            for axis, v in need.items():
                if axis == "tensor" and op.engine == "tensor":
                    continue
                if obs.get(axis, 0) >= v:
                    continue
                waits.append((axis, v))
            for dop in op.deps:
                c = dop.clock
                if c:
                    for a, v in c.items():
                        if obs.get(a, 0) < v:
                            obs[a] = v
            for axis, v in need.items():
                if obs.get(axis, 0) < v:
                    obs[axis] = v
            for axis, v in waits:
                if obs.get(axis, 0) < v:
                    obs[axis] = v
            op.waits = waits
            for axis, v in waits:
                axis_ops[axis][v - 1].tagged = True
            clk = dict(obs)
            if clk.get(op.axis, 0) < op.eidx + 1:
                clk[op.axis] = op.eidx + 1
            op.clock = clk
            issued[op.axis] = op.eidx + 1
        self.rank_map = {}
        for axis, ops in axis_ops.items():
            r = 0
            ranks = [0] * len(ops)
            for i, o in enumerate(ops):
                if o.tagged:
                    r += 1
                    o.rank = r
                ranks[i] = r
            self.rank_map[axis] = ranks
        self.axis_ops = axis_ops

    def emit(self, nc, sems):
        per_engine = {e: [] for e in ENGINES}
        for op in self.ops:
            per_engine[op.engine].append(op)

        def make_runner(engine_name):
            ops = per_engine[engine_name]

            def run(eng):
                for op in ops:
                    for axis, v in op.waits:
                        r = self.rank_map[axis][v - 1]
                        assert r > 0, (op, axis, v)
                        mult = DMA_INC if "_dma" in axis else 1
                        eng.wait_ge(sems[axis], r * mult)
                    if op.fn is None:
                        continue
                    inst = op.fn(eng)
                    if op.tagged:
                        inst.then_inc(sems[op.axis],
                                      DMA_INC if "_dma" in op.axis else 1)
            return run

        with nc.Block() as block:
            block.sync(make_runner("sync"))
            block.scalar(make_runner("scalar"))
            block.vector(make_runner("vector"))
            block.tensor(make_runner("tensor"))
            block.gpsimd(make_runner("gpsimd"))

    def used_axes(self):
        return [a for a, c in self.axis_count.items() if c > 0]



D = 1024
DC = 8
DFF = 4096
NH = 16
DH = 64
NPAIR = 8
NS = 64
WIN = 128
NE = 8
NEG = -1.0e30
CAP = 512
NSLOT = NE * CAP
GATHER_MOE = True


class KB:

    def __init__(self, nc, p, T, ctx):
        self.nc = nc
        self.p = p
        self.T = T
        self.TC = T // 128
        self.NTG = T // 512
        e = ctx.enter_context
        self.x = e(nc.sbuf_tensor("x", [128, self.TC * 1024], F32))
        self.hnT = e(nc.sbuf_tensor("hnT", [128, DC * T], BF16))
        self.AR = max(36864, 8192 + 16 * T)
        self.ar = e(nc.sbuf_tensor("ar", [128, self.AR], BF16))
        self.zf = e(nc.sbuf_tensor("zf", [128, 1024], F32))
        self.zbf = e(nc.sbuf_tensor("zbf", [128, 1024], BF16))
        self.zT = e(nc.sbuf_tensor("zT", [128, 512], F32))
        self.probs = e(nc.sbuf_tensor("probs", [128, 1024], F32))
        self.st_sb = e(nc.sbuf_tensor("st_sb", [64, 2048], BF16))
        self.ab_sb = e(nc.sbuf_tensor("ab_sb", [64, 512], F32))
        self.cw_sb = e(nc.sbuf_tensor("cw_sb", [64, 1024], BF16))
        self.bw_sb = e(nc.sbuf_tensor("bw_sb", [128, DC * 64], BF16))
        self.acol = e(nc.sbuf_tensor("acol", [64, 1], F32))
        st = lambda n: e(nc.sbuf_tensor(n, [128, 16], F32))
        self.nsum = st("nsum")
        self.sumsq = st("sumsq")
        self.var = st("var")
        self.rstd = st("rstd")
        self.nmr = st("nmr")
        self.tmpst = st("tmpst")
        self.negmax = st("negmax")
        self.rstd_o = st("rstd_o")
        self.nmr_o = st("nmr_o")
        self.esum = st("esum")
        self.recip = st("recip")
        self.alpha1 = e(nc.sbuf_tensor("alpha1", [128, 32], F32))
        self.alpha2 = e(nc.sbuf_tensor("alpha2", [128, 8], F32))
        self.bup_col = e(nc.sbuf_tensor("bup_col", [128, 32], F32))
        self.gcol = e(nc.sbuf_tensor("gcol", [128, 8], F32))
        self.bcol = e(nc.sbuf_tensor("bcol", [128, 8], F32))
        self.qkvcol = e(nc.sbuf_tensor("qkvcol", [128, 24], F32))
        self.nscol = e(nc.sbuf_tensor("nscol", [128, 8], F32))
        self.coeff = e(nc.sbuf_tensor("coeff", [128, NE * 16], F32))
        self.logit = e(nc.sbuf_tensor("logit", [128, 16 * 8], F32))
        self.selw = e(nc.sbuf_tensor("selw", [128, 16], F32))
        self.found = e(nc.sbuf_tensor("found", [128, 16], F32))
        self.gwT = e(nc.sbuf_tensor("gwT", [128, 64], F32))
        self.idx = e(nc.sbuf_tensor("idx", [128, 16], U32))
        self.ones_bf = e(nc.sbuf_tensor("ones_bf", [1, 128], BF16))
        self.row0 = e(nc.sbuf_tensor("row0", [1, 1024], BF16))
        self.row1 = e(nc.sbuf_tensor("row1", [1, 1024], BF16))
        self.row2 = e(nc.sbuf_tensor("row2", [1, 1024], BF16))
        self.vspread = e(nc.sbuf_tensor("vspread", [32, 128], F32))
        self.cpool = e(nc.sbuf_tensor("cpool", [128, 8], F32))
        self.cpool_n = 0
        self.ident_bf = e(nc.sbuf_tensor("ident_bf", [128, 128], BF16))
        self.ident_f = e(nc.sbuf_tensor("ident_f", [128, 128], F32))
        self.mask = e(nc.sbuf_tensor("mask", [128, 256], BF16))
        self.psBIG = e(nc.psum_tensor("psBIG", [128, 2048], F32))
        self.psB0 = e(nc.psum_tensor("psB0", [128, 512], F32))
        self.psB1 = e(nc.psum_tensor("psB1", [128, 512], F32))
        self.psT = e(nc.psum_tensor("psT", [128, 1024], BF16))
        self.psS = e(nc.psum_tensor("psS", [128, 512], F32))
        assert nc.psum_banks_remaining == 0, nc.psum_banks_remaining
        self.sbuf_left = nc.sbuf_bytes_remaining

    def ark(self, lo, n):
        return ["ar%d" % i for i in range(lo // 512, (lo + n + 511) // 512)]

    def pbk(self, lo, n):
        return ["pb%d" % i for i in range(lo // 512, (lo + n + 511) // 512)]

    def mm(self, out, lhsT, rhs, start, stop, reads, writes, name="mm",
           tile_position=None):
        return self.p.op(
            "tensor",
            lambda e, o=out, a=lhsT, b=rhs, s=start, t=stop, tp=tile_position:
                e.matmul(o, a, b, start=s, stop=t, skip_group_check=True,
                         tile_position=tp),
            reads=reads, writes=writes, name=name)

    def tr(self, out, in_, ident, reads, writes, name="tr", ident_key="idb"):
        return self.p.op(
            "tensor",
            lambda e, o=out, a=in_, i=ident: e.transpose(o, a, i),
            reads=list(reads) + [ident_key], writes=writes, name=name)

    def stat(self, kind, s):
        tiles = {
            "nm": (self.negmax, "nm", self.sumsq, "sumsq"),
            "es": (self.esum, "es", self.nsum, "nsum"),
            "rc": (self.recip, "rc", self.rstd, "rstd"),
            "tm": (self.tmpst, "tmpst", self.var, "var"),
        }[kind]
        if s < 16:
            t, kp = tiles[0], tiles[1]
        else:
            t, kp = tiles[2], tiles[3]
        c = s % 16
        return t[:, c:c + 1], f"{kp}{c}"

    def idma(self, out, out_off, in_, in_off, reads=(), writes=(), ch=3,
             name="idma", **kw):
        def fn(e, _o=out, _oo=out_off, _i=in_, _io=in_off, _kw=kw):
            return e.indirect_dma_start(_o, _oo, _i, _io, **_kw)
        return self.p._add("gpsimd", "gpsimd_dma%d" % ch, fn, reads, writes,
                           (), name)

    def register_const(self, v):
        i = self.cpool_n
        self.cpool_n += 1
        assert i < 8
        self.p.op("vector", lambda e, c=self.cpool[:, i:i + 1], vv=v:
                  e.memset(c, vv), writes=["cpool"], name="cpool")
        self.nc.const_aps.register(F32, v, self.cpool[:, i:i + 1])

    def act(self, out, in_, func, bias=0.0, scale=1.0, reads=(), writes=(),
            accum_out=None, name="act"):
        extra = []
        if (isinstance(bias, float) and bias != 0.0) or \
           (isinstance(scale, float) and scale not in (1.0,)):
            extra = ["cpool"]

        def fn(e, o=out, a=in_, f=func, b=bias, s=scale, ac=accum_out):
            if ac is None:
                return e.activation(o, a, f, bias=b, scale=s)
            return e.activation(o, a, f, bias=b, scale=s, accum_out=ac)
        return self.p.op("scalar", fn, reads=list(reads) + extra, writes=writes,
                         name=name)

    def vop(self, fn, reads=(), writes=(), name="vop", eng="vector"):
        return self.p.op(eng, fn, reads=reads, writes=writes, name=name)




def load_col_tile(kb, vec_ap, F, dest, dcol, key, prescale=None):
    P = F // 128
    kb.p.dma("sync", kb.vspread[0:P, 0:128],
             vec_ap.rearrange("(a j) -> a j", j=128),
             writes=["vspread"], name="colspread", ch=1)
    kb.tr(kb.psS[0:128, 0:P], kb.vspread[0:P, 0:128], kb.ident_f[0:P, 0:P],
          reads=["vspread"], writes=["psS"], name="coltr", ident_key="idf")
    kb.vop(lambda e, o=dest[:, dcol:dcol + P], i=kb.psS[0:128, 0:P]:
           e.tensor_copy(o, i),
           reads=["psS"], writes=[key], name="colcopy")
    if prescale is not None:
        kb.vop(lambda e, o=dest[:, dcol:dcol + P], s=prescale:
               e.tensor_scalar_mul(o, o, s),
               reads=[key], writes=[key], name="colscale")


def load_row(kb, row_tile, vec_ap, F, key):
    kb.p.dma("gpsimd", row_tile[0:1, 0:F], vec_ap[None, :], writes=[key],
             name="rowload", ch=2)


def load_wblock(kb, dram_ap_2d, dest, dcol, key, engine="gpsimd", ch=0):
    R, W = dram_ap_2d.shape
    a = R // 128
    kb.p.dma(engine,
             dest[:, dcol:dcol + a * W].rearrange("p (a j) -> p a j", a=a),
             dram_ap_2d.rearrange("(a p) j -> p a j", p=128),
             writes=[key] if isinstance(key, str) else key, name="wblock", ch=ch)



EPS_LN = 1e-5
EPS_RMS = 1e-6


def _ln_stats(kb, src, srckey, c, eps, rms=False, dst=None):
    p = kb.p
    rstd_t, nmr_t, kp = (kb.rstd, kb.nmr, "") if dst is None else dst
    kb.act(kb.zbf[:, :], src, AF.Square, accum_out=kb.sumsq[:, c:c + 1],
           reads=[srckey], writes=["zbf", f"sumsq{c}"], name="sq")
    if not rms:
        kb.act(kb.zbf[:, :], src, AF.Identity, scale=-1.0,
               accum_out=kb.nsum[:, c:c + 1],
               reads=[srckey], writes=["zbf", f"nsum{c}"], name="nsumacc")
        p.op("vector", lambda e, o=kb.tmpst[:, c:c + 1], i=kb.nsum[:, c:c + 1]:
             e.tensor_scalar_mul(o, i, 1.0 / D),
             reads=[f"nsum{c}"], writes=[f"tmpst{c}"], name="nmean")
        p.op("vector", lambda e, o=kb.var[:, c:c + 1], i=kb.tmpst[:, c:c + 1]:
             e.tensor_mul(o, i, i),
             reads=[f"tmpst{c}"], writes=[f"var{c}"], name="mean2")
        p.op("vector", lambda e, o=kb.var[:, c:c + 1], a=kb.sumsq[:, c:c + 1],
             b=kb.var[:, c:c + 1]:
             e.scalar_tensor_tensor(o, a, 1.0 / D, b, ALU.mult, ALU.subtract),
             reads=[f"sumsq{c}", f"var{c}"], writes=[f"var{c}"], name="var")
    else:
        p.op("vector", lambda e, o=kb.var[:, c:c + 1], i=kb.sumsq[:, c:c + 1]:
             e.tensor_scalar_mul(o, i, 1.0 / D),
             reads=[f"sumsq{c}"], writes=[f"var{c}"], name="varrms")
    kb.act(rstd_t[:, c:c + 1], kb.var[:, c:c + 1], AF.Ln, bias=float(eps),
           reads=[f"var{c}"], writes=[f"{kp}rstd{c}"], name="ln")
    kb.act(rstd_t[:, c:c + 1], rstd_t[:, c:c + 1], AF.Exp, scale=-0.5,
           reads=[f"{kp}rstd{c}"], writes=[f"{kp}rstd{c}"], name="exprs")
    if not rms:
        p.op("vector", lambda e, o=nmr_t[:, c:c + 1], a=kb.tmpst[:, c:c + 1],
             b=rstd_t[:, c:c + 1]: e.tensor_mul(o, a, b),
             reads=[f"tmpst{c}", f"{kp}rstd{c}"], writes=[f"{kp}nmr{c}"],
             name="nmr")


def _transpose_to_hnT(kb, c, gb=None):
    T = kb.T
    for j in range(DC):
        kb.tr(kb.psT[:, j * 128:(j + 1) * 128], kb.zbf[:, j * 128:(j + 1) * 128],
              kb.ident_bf[:, :], reads=["zbf"], writes=["psT"], name="trh")
    hview = kb.hnT[:, :].rearrange("p (a t) -> p a t", a=DC)[:, :, c * 128:(c + 1) * 128]
    pview = kb.psT[:, :].rearrange("p (a t) -> p a t", a=DC)
    hkeys = [f"hnT{j}" for j in range(DC)]
    ptk = ["psT"]
    if gb is None:
        kb.vop(lambda e, o=hview, i=pview: e.tensor_copy(o, i),
               reads=ptk, writes=hkeys, name="hevac")
    else:
        mode, g, b = gb
        for j in range(DC):
            dst = kb.hnT[:, j * T + c * 128: j * T + (c + 1) * 128]
            src = kb.psT[:, j * 128:(j + 1) * 128]
            if mode == "gb":
                kb.act(dst, src, AF.Identity, bias=b[:, j:j + 1], scale=g[:, j:j + 1],
                       reads=["psT", "gcol", "bcol"], writes=[f"hnT{j}"],
                       name="hevacgb")
            else:
                kb.act(dst, src, AF.Identity, scale=g[:, j:j + 1],
                       reads=["psT", "gcol"], writes=[f"hnT{j}"], name="hevacs")


def ln_stage(kb, kind, g1col=None, b1col=None, gate=False, hnt=True):
    T = kb.T
    for c in range(kb.TC):
        xc = kb.x[:, c * 1024:(c + 1) * 1024]
        xk = f"x{c}"
        if kind == "single_gb_dram":
            kb.p.dma("sync", kb.probs[:, 0:1024], kb.x1_scr[c * 128:(c + 1) * 128, :],
                     reads=[f"x1s{c}"], writes=["probs"], name="x1load", ch=0)
            _ln_stats(kb, kb.probs[:, 0:1024], "probs", c, EPS_LN)
            kb.act(kb.zbf[:, :], kb.probs[:, 0:1024], AF.Identity,
                   bias=kb.nmr[:, c:c + 1], scale=kb.rstd[:, c:c + 1],
                   reads=["probs", f"rstd{c}", f"nmr{c}"], writes=["zbf"],
                   name="appbf")
            _transpose_to_hnT(kb, c, gb=("gb", g1col, b1col))
            continue
        if kind == "double_gb":
            _ln_stats(kb, xc, xk, c, EPS_LN,
                      dst=(kb.rstd_o, kb.nmr_o, "o"))
        else:
            _ln_stats(kb, xc, xk, c, EPS_LN)
        if kind == "plain":
            if gate:
                kb.act(kb.zf[:, :], xc, AF.Identity, bias=kb.nmr[:, c:c + 1],
                       scale=kb.rstd[:, c:c + 1],
                       reads=[xk, f"rstd{c}", f"nmr{c}"], writes=["zf"], name="app32")
                if hnt:
                    kb.act(kb.zbf[:, :], kb.zf[:, :], AF.Identity,
                           reads=["zf"], writes=["zbf"], name="appbf")
                _gate_logits(kb, c)
            else:
                kb.act(kb.zbf[:, :], xc, AF.Identity, bias=kb.nmr[:, c:c + 1],
                       scale=kb.rstd[:, c:c + 1],
                       reads=[xk, f"rstd{c}", f"nmr{c}"], writes=["zbf"], name="appbf")
            if hnt:
                _transpose_to_hnT(kb, c)
        elif kind == "single_gb":
            kb.act(kb.zbf[:, :], xc, AF.Identity, bias=kb.nmr[:, c:c + 1],
                   scale=kb.rstd[:, c:c + 1],
                   reads=[xk, f"rstd{c}", f"nmr{c}"], writes=["zbf"], name="appbf")
            _transpose_to_hnT(kb, c, gb=("gb", g1col, b1col))
        elif kind == "double_gb":
            kb.act(kb.zf[:, :], xc, AF.Identity, bias=kb.nmr_o[:, c:c + 1],
                   scale=kb.rstd_o[:, c:c + 1],
                   reads=[xk, f"orstd{c}", f"onmr{c}"], writes=["zf"],
                   name="app32")
            _ln_stats(kb, kb.zf[:, :], "zf", c, EPS_LN)
            kb.act(kb.zbf[:, :], kb.zf[:, :], AF.Identity, bias=kb.nmr[:, c:c + 1],
                   scale=kb.rstd[:, c:c + 1],
                   reads=["zf", f"rstd{c}", f"nmr{c}"], writes=["zbf"], name="appbf")
            _transpose_to_hnT(kb, c, gb=("gb", g1col, b1col))
        elif kind == "double_rms":
            kb.act(kb.zf[:, :], xc, AF.Identity, bias=kb.nmr[:, c:c + 1],
                   scale=kb.rstd[:, c:c + 1],
                   reads=[xk, f"rstd{c}", f"nmr{c}"], writes=["zf"], name="app32")
            _ln_stats(kb, kb.zf[:, :], "zf", c, EPS_RMS, rms=True)
            kb.act(kb.zbf[:, :], kb.zf[:, :], AF.Identity, scale=kb.rstd[:, c:c + 1],
                   reads=["zf", f"rstd{c}"], writes=["zbf"], name="appbf")
            _transpose_to_hnT(kb, c, gb=("s", g1col, None))
        else:
            raise ValueError(kind)


def _gate_logits(kb, c):
    for r in range(2):
        for jj in range(4):
            j = r * 4 + jj
            kb.tr(kb.psS[:, jj * 128:(jj + 1) * 128],
                  kb.zf[:, j * 128:(j + 1) * 128], kb.ident_f[:, :],
                  reads=["zf"], writes=["psS"], name="trz", ident_key="idf")
        kb.vop(lambda e, o=kb.zT[:, 0:512], i=kb.psS[:, :]:
               e.tensor_copy(o, i), reads=["psS"],
               writes=["zT"], name="zTevac")
        for jj in range(4):
            dcc = r * 4 + jj
            kb.mm(kb.psB1[:, 0:8], kb.zT[:, jj * 128:(jj + 1) * 128],
                  kb.gwT[:, dcc * 8:(dcc + 1) * 8], start=(dcc == 0),
                  stop=(dcc == DC - 1),
                  reads=["zT", "gwT"], writes=["psB1"], name="gatemm")
    kb.vop(lambda e, o=kb.logit[:, c * 8:(c + 1) * 8], i=kb.psB1[:, 0:8]:
           e.tensor_copy(o, i), reads=["psB1"], writes=[f"logit{c}"], name="logevac")



FF_WUP = (0, 8192)
FF_WDN = (16384, 8192)
FF_HID = (32768, 4096)


def ffn_stage(kb, up_loader, dn_loader, act_fn, bias_col, scale_col, coeff_fn,
              add_b2=True, bias_key=None, scale_key=None):
    p = kb.p
    T = kb.T
    for hq in range(4):
        buf = hq % 2
        up_loader(hq, buf)
        dn_loader(hq, buf)
        upbase = FF_WUP[0] + buf * 8192
        dnbase = FF_WDN[0] + buf * 8192
        upkeys = kb.ark(upbase, 8192)
        dnkeys = kb.ark(dnbase, 8192)
        hidkeys = kb.ark(FF_HID[0], 4096)
        for tg in range(kb.NTG):
            for hc in range(8):
                ps = kb.psB0 if hc % 2 == 0 else kb.psB1
                psk = "psB0" if hc % 2 == 0 else "psB1"
                for dcc in range(DC):
                    kb.mm(ps[:, :],
                          kb.ar[:, upbase + dcc * 1024 + hc * 128:
                                upbase + dcc * 1024 + (hc + 1) * 128],
                          kb.hnT[:, dcc * T + tg * 512: dcc * T + tg * 512 + 512],
                          start=(dcc == 0), stop=(dcc == DC - 1),
                          reads=upkeys + [f"hnT{dcc}"], writes=[psk], name="ffA")
                ghc = hq * 8 + hc
                bias = bias_col[:, ghc:ghc + 1] if bias_col is not None else 0.0
                scale = scale_col[:, ghc:ghc + 1] if scale_col is not None else 1.0
                brd = ([bias_key] if bias_col is not None else []) + \
                      ([scale_key] if scale_col is not None else [])
                kb.act(kb.ar[:, FF_HID[0] + hc * 512: FF_HID[0] + (hc + 1) * 512],
                       ps[:, :], act_fn, bias=bias, scale=scale,
                       reads=[psk] + brd, writes=hidkeys, name="ffact")
            for dh in range(2):
                for i in range(4):
                    pslo = i * 512
                    pkeys = kb.pbk(pslo, 512)
                    with_b2 = (hq == 0 and add_b2)
                    for hc in range(8):
                        kb.mm(kb.psBIG[:, pslo:pslo + 512],
                              kb.ar[:, FF_HID[0] + hc * 512 + i * 128:
                                    FF_HID[0] + hc * 512 + (i + 1) * 128],
                              kb.ar[:, dnbase + hc * 1024 + dh * 512:
                                    dnbase + hc * 1024 + (dh + 1) * 512],
                              start=(hc == 0), stop=(hc == 7 and not with_b2),
                              reads=hidkeys + dnkeys, writes=pkeys, name="ffB")
                    if with_b2:
                        kb.mm(kb.psBIG[:, pslo:pslo + 512], kb.ones_bf[0:1, :],
                              kb.row0[0:1, dh * 512:(dh + 1) * 512],
                              start=False, stop=True,
                              reads=["ones", "row0"], writes=pkeys, name="ffb2")
                    tc = tg * 4 + i
                    coeff = coeff_fn(tc)
                    ckeys = []
                    if isinstance(coeff, tuple):
                        coeff, ck = coeff
                        ckeys = [ck]
                    xsl = kb.x[:, tc * 1024 + dh * 512: tc * 1024 + (dh + 1) * 512]
                    p.op("vector",
                         lambda e, o=xsl, a=kb.psBIG[:, pslo:pslo + 512], s=coeff:
                         e.scalar_tensor_tensor(o, a, s, o, ALU.mult, ALU.add),
                         reads=pkeys + [f"x{tc}"] + ckeys, writes=[f"x{tc}"],
                         name="ffevac")


def simple_up_loader(kb, w_dram):
    def load(hq, buf):
        base = FF_WUP[0] + buf * 8192
        load_wblock(kb, w_dram[:, hq * 1024:(hq + 1) * 1024], kb.ar, base,
                    kb.ark(base, 8192))
    return load


def simple_dn_loader(kb, w_dram):
    def load(hq, buf):
        base = FF_WDN[0] + buf * 8192
        load_wblock(kb, w_dram[hq * 1024:(hq + 1) * 1024, :], kb.ar, base,
                    kb.ark(base, 8192), ch=1)
    return load




def _sign_stripe(kb, dram_stripe, dest_base, rc, alpha_col_dst=None,
                 prescale_col=None, dkeys=()):
    p = kb.p
    p.dma("sync", kb.zf[:, :], dram_stripe, writes=["zf"], name="sgnload")
    if alpha_col_dst is not None:
        p.op("vector", lambda e, o=alpha_col_dst, i=kb.zf[:, :]:
             e.reduce_sum(o, i, AX.X, apply_absolute_value=True),
             reads=["zf"], writes=["alpha_w"], name="absred")
        p.op("vector", lambda e, o=alpha_col_dst: e.tensor_scalar_mul(o, o, 1.0 / D),
             reads=["alpha_w"], writes=["alpha_w"], name="alphasc")
    p.op("scalar", lambda e, o=kb.zbf[:, :], i=kb.zf[:, :]: e.sign(o, i),
         reads=["zf"], writes=["zbf"], name="sign")
    if prescale_col is not None:
        p.op("vector", lambda e, o=kb.zbf[:, :], s=prescale_col:
             e.tensor_scalar_mul(o, o, s),
             reads=["zbf", "alpha2_r"], writes=["zbf"], name="sgnscale")
    for j in range(DC):
        kb.tr(kb.psT[:, j * 128:(j + 1) * 128], kb.zbf[:, j * 128:(j + 1) * 128],
              kb.ident_bf[:, :], reads=["zbf"], writes=["psT"], name="sgntr")
    dview = kb.ar[:, dest_base: dest_base + 8192] \
        .rearrange("p (a j) -> p a j", a=8)[:, :, rc * 128:(rc + 1) * 128]
    pview = kb.psT[:, :].rearrange("p (a j) -> p a j", a=DC)
    kb.vop(lambda e, o=dview, i=pview: e.tensor_copy(o, i),
           reads=["psT"], writes=list(dkeys), name="sgnevac")


def sign_up_loader(kb, w1_dram, scr_up):
    def load(hq, buf):
        base = FF_WUP[0] + buf * 8192
        keys = kb.ark(base, 8192)
        for rc in range(8):
            ghc = hq * 8 + rc
            _sign_stripe(kb, w1_dram[ghc * 128:(ghc + 1) * 128, :], base, rc,
                         alpha_col_dst=kb.alpha1[:, ghc:ghc + 1], dkeys=keys)
        kb.p.dma("sync", scr_up[hq, :, :], kb.ar[:, base:base + 8192],
                 reads=keys, writes=[f"scru{hq}"], name="scrspill")
    return load


def sign_dn_loader(kb, w2_dram, scr_dn):
    def load(hq, buf):
        base = FF_WDN[0] + buf * 8192
        keys = kb.ark(base, 8192)
        for rc in range(8):
            _sign_stripe(kb, w2_dram[rc * 128:(rc + 1) * 128,
                                     hq * 1024:(hq + 1) * 1024],
                         base, rc, prescale_col=kb.alpha2[:, rc:rc + 1], dkeys=keys)
        kb.p.dma("sync", scr_dn[hq, :, :], kb.ar[:, base:base + 8192],
                 reads=keys, writes=[f"scrd{hq}"], name="scrspill2")
    return load


def alpha2_prepass(kb, w2_dram):
    p = kb.p
    for rc in range(8):
        for hh in range(4):
            p.dma("sync", kb.probs[:, 0:1024],
                  w2_dram[rc * 128:(rc + 1) * 128, hh * 1024:(hh + 1) * 1024],
                  writes=["probs"], name="a2load")
            p.op("vector", lambda e, o=kb.tmpst[:, hh:hh + 1], i=kb.probs[:, 0:1024]:
                 e.reduce_sum(o, i, AX.X, apply_absolute_value=True),
                 reads=["probs"], writes=[f"a2p{hh}"], name="a2red")
        p.op("vector", lambda e, o=kb.tmpst[:, 0:1], a=kb.tmpst[:, 0:1],
             b=kb.tmpst[:, 1:2]: e.tensor_add(o, a, b),
             reads=["a2p0", "a2p1"], writes=["a2p0"], name="a2add")
        p.op("vector", lambda e, o=kb.tmpst[:, 2:3], a=kb.tmpst[:, 2:3],
             b=kb.tmpst[:, 3:4]: e.tensor_add(o, a, b),
             reads=["a2p2", "a2p3"], writes=["a2p2"], name="a2add2")
        p.op("vector", lambda e, o=kb.alpha2[:, rc:rc + 1], a=kb.tmpst[:, 0:1],
             b=kb.tmpst[:, 2:3]: e.tensor_add(o, a, b),
             reads=["a2p0", "a2p2"], writes=["alpha2_r"], name="a2add3")
        p.op("vector", lambda e, o=kb.alpha2[:, rc:rc + 1]:
             e.tensor_scalar_mul(o, o, 1.0 / DFF),
             reads=["alpha2_r"], writes=["alpha2_r"], name="a2sc")


def scratch_up_loader(kb, scr_up):
    def load(hq, buf):
        base = FF_WUP[0] + buf * 8192
        kb.p.dma("gpsimd", kb.ar[:, base:base + 8192], scr_up[hq, :, :],
                 reads=[f"scru{hq}"], writes=kb.ark(base, 8192), name="scrload",
                 ch=0)
    return load


def scratch_dn_loader(kb, scr_dn):
    def load(hq, buf):
        base = FF_WDN[0] + buf * 8192
        kb.p.dma("gpsimd", kb.ar[:, base:base + 8192], scr_dn[hq, :, :],
                 reads=[f"scrd{hq}"], writes=kb.ark(base, 8192), name="scrload2",
                 ch=1)
    return load




def moe_prep_gate(kb, gate_w_dram):
    for dcc in range(DC):
        kb.p.dma("sync", kb.vspread[0:8, 0:128],
                 gate_w_dram[:, dcc * 128:(dcc + 1) * 128],
                 writes=["vspread"], name="gwload", ch=1)
        kb.tr(kb.psS[:, dcc * 8:(dcc + 1) * 8],
              kb.vspread[0:8, 0:128], kb.ident_f[0:8, 0:8],
              reads=["vspread"], writes=["psS"], name="gwtr", ident_key="idf")
    kb.vop(lambda e, o=kb.gwT[:, 0:64], i=kb.psS[:, 0:64]: e.tensor_copy(o, i),
           reads=["psS"], writes=["gwT"], name="gwevac")


def moe_gate_softmax(kb):
    p = kb.p
    p.op("vector", lambda e: e.memset(kb.found[:, 0:8], 1.0),
         writes=["ones8"], name="ones8")
    for tc in range(kb.TC):
        L = kb.logit[:, tc * 8:(tc + 1) * 8]
        lk = f"logit{tc}"
        nm = kb.negmax[:, tc:tc + 1]
        es = kb.esum[:, tc:tc + 1]
        p.op("vector", lambda e, o=nm, i=L: e.reduce_max(o, i, AX.X, negate=True),
             reads=[lk], writes=[f"nm{tc}"], name="gmax")
        kb.act(kb.zf[:, tc * 8:(tc + 1) * 8], L, AF.Exp, bias=nm,
               accum_out=es, reads=[lk, f"nm{tc}"], writes=["zf", f"es{tc}"],
               name="gexp")
        p.op("vector", lambda e, o=kb.recip[:, tc:tc + 1], i=es: e.reciprocal(o, i),
             reads=[f"es{tc}"], writes=[f"rc{tc}"], name="grecip")
        p.op("vector", lambda e, o=kb.selw[:, tc:tc + 1], i=kb.recip[:, tc:tc + 1]:
             e.tensor_scalar_mul(o, i, 0.5),
             reads=[f"rc{tc}"], writes=[f"sw{tc}"], name="gselw")
        p.op("vector", lambda e, o=kb.tmpst[:, tc:tc + 1], i=nm:
             e.tensor_scalar_mul(o, i, -1.0),
             reads=[f"nm{tc}"], writes=[f"gm{tc}"], name="gmaxpos")
        eq = kb.zf[:, 128 + tc * 8:128 + (tc + 1) * 8]
        neq = kb.zf[:, 256 + tc * 8:256 + (tc + 1) * 8]
        cp = kb.zf[:, 384 + tc * 8:384 + (tc + 1) * 8]
        sel = kb.zf[:, 512 + tc * 8:512 + (tc + 1) * 8]
        p.op("vector", lambda e, o=eq, i=L, s=kb.tmpst[:, tc:tc + 1]:
             e.tensor_single_scalar(o, i, s, ALU.is_equal),
             reads=[lk, f"gm{tc}"], writes=["zf"], name="geq")
        p.op("vector", lambda e, o=neq, i=eq:
             e.tensor_scalar(o, i, -1.0, 1.0, ALU.mult, ALU.add),
             reads=["zf"], writes=["zf"], name="gneq")
        p.op("vector", lambda e, o=cp, a=neq, b=kb.found[:, 0:8]:
             e.tensor_tensor_scan(o, a, b, 1.0, ALU.mult, ALU.mult),
             reads=["zf", "ones8"], writes=["zf"], name="gcum")
        p.op("vector", lambda e, o=sel[:, 1:8], a=eq[:, 1:8], b=cp[:, 0:7]:
             e.tensor_mul(o, a, b), reads=["zf"], writes=["zf"], name="gsel")
        p.op("vector", lambda e, o=sel[:, 0:1], i=eq[:, 0:1]: e.tensor_copy(o, i),
             reads=["zf"], writes=["zf"], name="gsel0")
        cview = kb.coeff[:, :].rearrange("p (e t) -> p t e", t=16)[:, tc, :]
        p.op("vector", lambda e, o=cview, i=sel, s=kb.selw[:, tc:tc + 1]:
             e.tensor_single_scalar(o, i, s, ALU.mult),
             reads=["zf", f"sw{tc}"], writes=[f"coef{tc}"], name="gcoef")


def moe_route(kb):
    p = kb.p
    zf = kb.zf
    p.op("vector", lambda e: e.memset(zf[:, 280:296], 1.0), writes=["zf"],
         name="r_ones")
    allcoef = [f"coef{t}" for t in range(16)]
    for tc in range(kb.TC):
        cv = kb.coeff[:, :].rearrange("p (e t) -> p t e", t=16)[:, tc, :]
        p.op("vector", lambda e, o=kb.selw[:, tc:tc + 1], i=cv:
             e.reduce_sum(o, i, AX.X),
             reads=[f"coef{tc}"], writes=[f"sw{tc}"], name="r_w")
    for e8 in range(NE):
        p.op("vector", lambda e, o=zf[:, e8 * 16:(e8 + 1) * 16],
             i=kb.coeff[:, e8 * 16:(e8 + 1) * 16]:
             e.tensor_single_scalar(o, i, 0.0, ALU.is_gt),
             reads=allcoef, writes=["zf"], name="r_sel")
        p.op("vector", lambda e, o=zf[:, 128 + e8 * 16:128 + (e8 + 1) * 16],
             a=zf[:, 280:296], b=zf[:, e8 * 16:(e8 + 1) * 16]:
             e.tensor_tensor_scan(o, a, b, 0.0, ALU.mult, ALU.add),
             reads=["zf"], writes=["zf"], name="r_scan")
    cnt = zf[:, 128:256].rearrange("p (e t) -> p e t", t=16)[:, :, 15]
    kb.tr(kb.psS[0:8, 0:128], cnt, kb.ident_f[:, :], reads=["zf"],
          writes=["psS"], name="r_tr1", ident_key="idf")
    p.op("vector", lambda e, o=kb.zT[0:8, 0:128], i=kb.psS[0:8, 0:128]:
         e.tensor_copy(o, i), reads=["psS"], writes=["zT"], name="r_cvac")
    p.op("vector", lambda e: e.memset(kb.zT[0:8, 128:256], 1.0),
         writes=["zT"], name="r_ones2")
    p.op("vector", lambda e, o=kb.zT[0:8, 256:384], a=kb.zT[0:8, 128:256],
         b=kb.zT[0:8, 0:128]:
         e.tensor_tensor_scan(o, a, b, 0.0, ALU.mult, ALU.add),
         reads=["zT"], writes=["zT"], name="r_scan2")
    p.op("vector", lambda e, o=kb.zT[0:8, 384:512], a=kb.zT[0:8, 256:384],
         b=kb.zT[0:8, 0:128]: e.tensor_sub(o, a, b),
         reads=["zT"], writes=["zT"], name="r_excl")
    kb.tr(kb.psS[0:128, 0:8], kb.zT[0:8, 384:512], kb.ident_f[0:8, 0:8],
          reads=["zT"], writes=["psS"], name="r_tr2", ident_key="idf")
    p.op("vector", lambda e, o=zf[:, 272:280], i=kb.psS[0:128, 0:8]:
         e.tensor_copy(o, i), reads=["psS"], writes=["zf"], name="r_ovac")
    p.op("vector", lambda e: e.memset(zf[:, 256:272], 0.0), writes=["zf"],
         name="r_dz")
    for e8 in range(NE):
        t1 = zf[:, 296:312]
        p.op("vector", lambda e, o=t1, i=zf[:, 128 + e8 * 16:128 + (e8 + 1) * 16],
             s=zf[:, 272 + e8:273 + e8]:
             e.tensor_scalar(o, i, s, -1.0, ALU.add, ALU.add),
             reads=["zf"], writes=["zf"], name="r_d1")
        p.op("vector", lambda e, o=t1, i=t1, c=float(CAP - 1), b=float(e8 * CAP):
             e.tensor_scalar(o, i, c, b, ALU.min, ALU.add),
             reads=["zf"], writes=["zf"], name="r_d2")
        p.op("vector", lambda e, o=t1, a=t1, b=zf[:, e8 * 16:(e8 + 1) * 16]:
             e.tensor_mul(o, a, b), reads=["zf"], writes=["zf"], name="r_d3")
        p.op("vector", lambda e, o=zf[:, 256:272], a=zf[:, 256:272], b=t1:
             e.tensor_add(o, a, b), reads=["zf"], writes=["zf"], name="r_d4")
    p.op("vector", lambda e, o=kb.idx[:, :], i=zf[:, 256:272]:
         e.tensor_copy(o, i), reads=["zf"], writes=["idx"], name="r_u32")


def moe_scatter_x(kb, xslots):
    zfb = kb.zf[:, :].bitcast(BF16)
    for tc in range(kb.TC):
        if tc % 2 == 0:
            buf, bkeys = kb.zbf[:, :], ["zbf"]
        else:
            buf, bkeys = zfb[:, 0:1024], ["zf"]
        kb.act(buf, kb.x[:, tc * 1024:(tc + 1) * 1024], AF.Identity,
               bias=kb.nmr[:, tc:tc + 1], scale=kb.rstd[:, tc:tc + 1],
               reads=[f"x{tc}", f"rstd{tc}", f"nmr{tc}"], writes=bkeys,
               name="sc_ln")
        kb.idma(xslots[:, :], IndirectOffsetOnAxis(kb.idx[:, tc:tc + 1], 0),
                buf, None, reads=bkeys + ["idx"], writes=["xsl"], ch=3,
                name="sc_scat")


def _slotT(kb):
    return kb.hnT[:, 0:DC * CAP]


def _yeacc(kb):
    return kb.hnT[:, :].bitcast(F32)[:, 4096:8192]


def moe_expert_gathered(kb, ex, w1_d, w2_d, b1_d, b2_d, xslots, yslots):
    p = kb.p
    st = _slotT(kb)
    ye = _yeacc(kb)
    zfb = kb.zf[:, :].bitcast(BF16)
    for sc in range(4):
        if sc % 2 == 0:
            buf, bkeys = kb.zbf[:, :], ["zbf"]
        else:
            buf, bkeys = zfb[:, 0:1024], ["zf"]
        p.dma("sync", buf, xslots[ex * CAP + sc * 128: ex * CAP + (sc + 1) * 128, :],
              reads=["xsl"], writes=bkeys, name="slgath", ch=2)
        for jj in range(DC):
            kb.tr(kb.psT[:, jj * 128:(jj + 1) * 128],
                  buf[:, jj * 128:(jj + 1) * 128], kb.ident_bf[:, :],
                  reads=bkeys, writes=["psT"], name="sltr")
        dview = st.rearrange("p (a s) -> p a s", a=DC)[:, :, sc * 128:(sc + 1) * 128]
        pview = kb.psT[:, :].rearrange("p (a s) -> p a s", a=DC)
        kb.vop(lambda e, o=dview, i=pview: e.tensor_copy(o, i),
               reads=["psT"], writes=["hsl"], name="slevac")
    load_col_tile(kb, b1_d, DFF, kb.bup_col, 0, "bup")
    load_row(kb, kb.row0, b2_d, 1024, "row0")
    for hq in range(4):
        buf = hq % 2
        upbase = FF_WUP[0] + buf * 8192
        dnbase = FF_WDN[0] + buf * 8192
        for hf in range(2):
            load_wblock(kb, w1_d[:, hq * 1024 + hf * 512: hq * 1024 + (hf + 1) * 512],
                        kb.ar, upbase + hf * 4096, kb.ark(upbase + hf * 4096, 4096),
                        engine="gpsimd", ch=hf)
            load_wblock(kb, w2_d[hq * 1024 + hf * 512: hq * 1024 + (hf + 1) * 512, :],
                        kb.ar, dnbase + hf * 4096, kb.ark(dnbase + hf * 4096, 4096),
                        engine="gpsimd", ch=4 + hf)
        upkeys = kb.ark(upbase, 8192)
        dnkeys = kb.ark(dnbase, 8192)
        hidkeys = kb.ark(FF_HID[0], 4096)
        for hc in range(8):
            ps = kb.psB0 if hc % 2 == 0 else kb.psB1
            psk = "psB0" if hc % 2 == 0 else "psB1"
            for dcc in range(DC):
                ub = upbase + (hc // 4) * 4096 + dcc * 512 + (hc % 4) * 128
                kb.mm(ps[:, :],
                      kb.ar[:, ub: ub + 128],
                      st[:, dcc * CAP: (dcc + 1) * CAP],
                      start=(dcc == 0), stop=(dcc == DC - 1),
                      reads=upkeys + ["hsl"], writes=[psk], name="gfA")
            ghc = hq * 8 + hc
            kb.act(kb.ar[:, FF_HID[0] + hc * 512: FF_HID[0] + (hc + 1) * 512],
                   ps[:, :], AF.Relu, bias=kb.bup_col[:, ghc:ghc + 1],
                   reads=[psk, "bup"], writes=hidkeys, name="gfact")
        for dh in range(2):
            for sc in range(4):
                pk = [f"pb{sc}"]
                with_b2 = (hq == 0)
                for hc in range(8):
                    kb.mm(kb.psBIG[:, sc * 512:(sc + 1) * 512],
                          kb.ar[:, FF_HID[0] + hc * 512 + sc * 128:
                                FF_HID[0] + hc * 512 + (sc + 1) * 128],
                          kb.ar[:, dnbase + hc * 1024 + dh * 512:
                                dnbase + hc * 1024 + (dh + 1) * 512],
                          start=(hc == 0), stop=(hc == 7 and not with_b2),
                          reads=hidkeys + dnkeys, writes=pk, name="gfB")
                if with_b2:
                    kb.mm(kb.psBIG[:, sc * 512:(sc + 1) * 512], kb.ones_bf[0:1, :],
                          kb.row0[0:1, dh * 512:(dh + 1) * 512],
                          start=False, stop=True,
                          reads=["ones", "row0"], writes=pk, name="gfb2")
                yv = ye[:, sc * 1024 + dh * 512: sc * 1024 + (dh + 1) * 512]
                if hq == 0:
                    p.op("vector", lambda e, o=yv, i=kb.psBIG[:, sc * 512:(sc + 1) * 512]:
                         e.tensor_copy(o, i),
                         reads=pk, writes=["hye"], name="gfev0")
                else:
                    p.op("vector", lambda e, o=yv, i=kb.psBIG[:, sc * 512:(sc + 1) * 512]:
                         e.scalar_tensor_tensor(o, i, 1.0, o, ALU.mult, ALU.add),
                         reads=pk + ["hye"], writes=["hye"], name="gfev1")
    for sc in range(4):
        p.dma("sync", yslots[ex * CAP + sc * 128: ex * CAP + (sc + 1) * 128, :],
              ye[:, sc * 1024:(sc + 1) * 1024],
              reads=["hye"], writes=["ysl"], name="yeout", ch=3)


def moe_gather_combine(kb, yslots):
    for tc in range(kb.TC):
        if tc % 2 == 0:
            buf, bk = kb.probs[:, :], "probs"
        else:
            buf, bk = kb.zf[:, :], "zf"
        kb.idma(buf, None, yslots[:, :],
                IndirectOffsetOnAxis(kb.idx[:, tc:tc + 1], 0),
                reads=["ysl", "idx"], writes=[bk], ch=3, name="yegath")
        kb.p.op("vector", lambda e, o=buf, s=kb.selw[:, tc:tc + 1]:
                e.tensor_scalar_mul(o, o, s),
                reads=[bk, f"sw{tc}"], writes=[bk], name="yew")
        kb.p.op("vector", lambda e, o=kb.x[:, tc * 1024:(tc + 1) * 1024],
                a=kb.x[:, tc * 1024:(tc + 1) * 1024], b=buf:
                e.tensor_add(o, a, b),
                reads=[f"x{tc}", bk], writes=[f"x{tc}"], name="yeadd")


def moe_fence(kb):
    kb.p.op("vector", lambda e, o=kb.zT[0:1, 0:1]: e.tensor_copy(o, kb.zT[0:1, 0:1]),
            reads=["hsl", "hye", "xsl", "ysl"],
            writes=[f"hnT{j}" for j in range(DC)] + ["zT"], name="hfence")


def moe_experts(kb, w1_3d, w2_3d, b1_2d, b2_2d):
    for ex in range(NE):
        load_col_tile(kb, b1_2d[ex], DFF, kb.bup_col, 0, "bup")
        load_row(kb, kb.row0, b2_2d[ex], 1024, "row0")
        up = simple_up_loader(kb, w1_3d[ex])
        dn = simple_dn_loader(kb, w2_3d[ex])

        def coeff_fn(tc, ex=ex):
            return (kb.coeff[:, ex * 16 + tc: ex * 16 + tc + 1], f"coef{tc}")
        ffn_stage(kb, up, dn, AF.Relu, kb.bup_col, None, coeff_fn, add_b2=True,
                  bias_key="bup", scale_key=None)




def _v_phase(kb, bias_row, bias_key):
    T = kb.T
    vbase = 8192
    wkeys = kb.ark(0, 8192)
    for tc in range(kb.TC):
        for nh in range(2):
            for dcc in range(DC):
                kb.mm(kb.psBIG[:, nh * 512:(nh + 1) * 512],
                      kb.hnT[:, dcc * T + tc * 128: dcc * T + (tc + 1) * 128],
                      kb.ar[:, dcc * 1024 + nh * 512: dcc * 1024 + (nh + 1) * 512],
                      start=(dcc == 0), stop=(dcc == DC - 1 and bias_row is None),
                      reads=wkeys + [f"hnT{dcc}"], writes=kb.pbk(nh * 512, 512),
                      name="vmm")
            if bias_row is not None:
                kb.mm(kb.psBIG[:, nh * 512:(nh + 1) * 512], kb.ones_bf[0:1, :],
                      bias_row[0:1, nh * 512:(nh + 1) * 512], start=False, stop=True,
                      reads=["ones", bias_key], writes=kb.pbk(nh * 512, 512),
                      name="vbias")
        kb.act(kb.ar[:, vbase + tc * 1024: vbase + (tc + 1) * 1024],
               kb.psBIG[:, 0:1024], AF.Identity,
               reads=kb.pbk(0, 1024), writes=kb.ark(vbase + tc * 1024, 1024),
               name="vevac")


def _qk_pair(kb, wq_flat, wk_flat, j, qbias, kbias):
    T = kb.T
    wqb, wkb = 2 * T, 2 * T + 1024
    load_wblock(kb, wq_flat[:, j * 128:(j + 1) * 128], kb.ar, wqb,
                kb.ark(wqb, 1024), ch=0)
    load_wblock(kb, wk_flat[:, j * 128:(j + 1) * 128], kb.ar, wkb,
                kb.ark(wkb, 1024), ch=1)
    for tgq in range(T // 512):
        for dcc in range(DC):
            kb.mm(kb.psB0[:, :], kb.ar[:, wqb + dcc * 128: wqb + (dcc + 1) * 128],
                  kb.hnT[:, dcc * T + tgq * 512: dcc * T + tgq * 512 + 512],
                  start=(dcc == 0), stop=(dcc == DC - 1),
                  reads=kb.ark(wqb, 1024) + [f"hnT{dcc}"], writes=["psB0"],
                  name="qmm")
        kb.act(kb.ar[:, tgq * 512:(tgq + 1) * 512], kb.psB0[:, :], AF.Identity,
               bias=(qbias[0][:, j:j + 1] if qbias else 0.0), scale=0.125,
               reads=["psB0"] + ([qbias[1]] if qbias else []),
               writes=kb.ark(tgq * 512, 512), name="qevac")
        for dcc in range(DC):
            kb.mm(kb.psB1[:, :], kb.ar[:, wkb + dcc * 128: wkb + (dcc + 1) * 128],
                  kb.hnT[:, dcc * T + tgq * 512: dcc * T + tgq * 512 + 512],
                  start=(dcc == 0), stop=(dcc == DC - 1),
                  reads=kb.ark(wkb, 1024) + [f"hnT{dcc}"], writes=["psB1"],
                  name="kmm")
        kb.act(kb.ar[:, T + tgq * 512: T + (tgq + 1) * 512], kb.psB1[:, :], AF.Identity,
               bias=(kbias[0][:, j:j + 1] if kbias else 0.0),
               reads=["psB1"] + ([kbias[1]] if kbias else []),
               writes=kb.ark(T + tgq * 512, 512), name="kevac")


def _l0_head_A(kb, j, h_loc, qc, ci):
    T = kb.T
    par = ci % 2
    s = ci % 32
    nm, nmk = kb.stat("nm", s)
    es, esk = kb.stat("es", s)
    rc, rck = kb.stat("rc", s)
    qk = kb.ark(qc * 128, 128)
    q_ap = kb.ar[h_loc:h_loc + 64, qc * 128:(qc + 1) * 128]
    nkc = 1 if qc == 0 else 2
    width = nkc * 128
    if qc == 0:
        kb.mm(kb.psS[:, par * 256: par * 256 + 128], q_ap,
              kb.ar[h_loc:h_loc + 64, T:T + 128],
              start=True, stop=True, reads=qk + kb.ark(T, 128),
              writes=["psS"], name="smm")
        mview = kb.mask[:, 128:256]
    else:
        kb.mm(kb.psS[:, par * 256: par * 256 + 256], q_ap,
              kb.ar[h_loc:h_loc + 64, T + (qc - 1) * 128: T + (qc + 1) * 128],
              start=True, stop=True,
              reads=qk + kb.ark(T + (qc - 1) * 128, 256), writes=["psS"],
              name="smm")
        mview = kb.mask[:, 0:256]
    pr = kb.probs[:, par * 512: par * 512 + width]
    prk = f"probs{par}"
    kb.p.op("vector", lambda e, o=pr, a=kb.psS[:, par * 256: par * 256 + width],
            b=mview: e.tensor_add(o, a, b),
            reads=["psS", "mask"], writes=[prk], name="smask")
    kb.p.op("vector", lambda e, o=nm, i=pr: e.reduce_max(o, i, AX.X, negate=True),
            reads=[prk], writes=[nmk], name="smax")
    P = kb.zbf[:, par * 512: par * 512 + width]
    Pk = f"zbfp{par}"
    kb.act(P, pr, AF.Exp, bias=nm, accum_out=es,
           reads=[prk, nmk], writes=[Pk, esk], name="sexp")
    kb.p.op("vector", lambda e, o=rc, i=es: e.reciprocal(o, i),
            reads=[esk], writes=[rck], name="srecip")
    kb.p.op("vector", lambda e, o=P, s_=rc: e.tensor_scalar_mul(o, o, s_),
            reads=[Pk, rck], writes=[Pk], name="snormv")


def _l0_head_B(kb, j, h_loc, qc, ci):
    T = kb.T
    vbase, aobase = 8192, 8192 + 8 * T
    head = 2 * j + (1 if h_loc else 0)
    par = ci % 2
    nkc = 1 if qc == 0 else 2
    width = nkc * 128
    Pk = f"zbfp{par}"
    PTk = f"zbft{par}"
    for kk in range(nkc):
        slot = par * 2 + kk
        kb.tr(kb.psT[:, slot * 128:(slot + 1) * 128],
              kb.zbf[:, par * 512 + kk * 128: par * 512 + (kk + 1) * 128],
              kb.ident_bf[:, :], reads=[Pk], writes=["psT"], name="str")
    kb.vop(lambda e, o=kb.zbf[:, par * 512 + 256: par * 512 + 256 + width],
           i=kb.psT[:, par * 256: par * 256 + width]: e.tensor_copy(o, i),
           reads=["psT"], writes=[PTk],
           name="sTevac")
    psA = kb.psB0 if par == 0 else kb.psB1
    psAk = "psB0" if par == 0 else "psB1"
    for kk in range(nkc):
        kc = qc - 1 + kk if qc > 0 else 0
        kb.mm(psA[h_loc:h_loc + 64, 0:128],
              kb.ar[:, vbase + kc * 1024 + head * 64: vbase + kc * 1024 + head * 64 + 64],
              kb.zbf[:, par * 512 + 256 + kk * 128: par * 512 + 256 + (kk + 1) * 128],
              start=(kk == 0), stop=(kk == nkc - 1),
              reads=kb.ark(vbase + kc * 1024, 1024) + [PTk], writes=[psAk],
              name="aomm", tile_position=(0, h_loc))
    kb.act(kb.ar[h_loc:h_loc + 64, aobase + j * T + qc * 128:
                 aobase + j * T + (qc + 1) * 128],
           psA[h_loc:h_loc + 64, 0:128], AF.Identity,
           reads=[psAk], writes=kb.ark(aobase + j * T + qc * 128, 128),
           name="aoevac")


def _ssm_u_scan(kb, acol_key="acol"):
    T = kb.T
    p = kb.p
    p.op("vector", lambda e: e.memset(kb.st_sb[:, 0:512], 1.0),
         writes=["st"], name="stones")
    p.op("vector", lambda e, o=kb.ab_sb[:, 0:512], i=kb.st_sb[:, 0:512],
         s=kb.acol[:, 0:1]: e.tensor_single_scalar(o, i, s, ALU.mult),
         reads=["st", acol_key], writes=["ab"], name="abmul")
    for s in range(T // 512):
        ps = kb.psB0 if s % 2 == 0 else kb.psB1
        psk = "psB0" if s % 2 == 0 else "psB1"
        for dcc in range(DC):
            kb.mm(ps[0:64, :], kb.bw_sb[:, dcc * 64:(dcc + 1) * 64],
                  kb.hnT[:, dcc * T + s * 512: dcc * T + s * 512 + 512],
                  start=(dcc == 0), stop=(dcc == DC - 1),
                  reads=["bw", f"hnT{dcc}"], writes=[psk], name="umm")
        init = 0.0 if s == 0 else kb.st_sb[:, s * 512 - 1: s * 512]
        p.op("vector",
             lambda e, o=kb.st_sb[:, s * 512:(s + 1) * 512],
             a=kb.ab_sb[:, 0:512], b=ps[0:64, :], ini=init:
             e.tensor_tensor_scan(o, a, b, ini, ALU.mult, ALU.add),
             reads=["ab", psk, "st"], writes=["st"], name="scan")


def _l0_combine(kb, tc, wg_base=8192):
    T = kb.T
    aobase = 8192 + 8 * T
    p = kb.p
    kb.mm(kb.psB0[:, :], kb.st_sb[:, tc * 128:(tc + 1) * 128],
          kb.cw_sb[:, 0:512], start=True, stop=True,
          reads=["st", "cw"], writes=["psB0"], name="ssmmm0")
    kb.mm(kb.psB1[:, :], kb.st_sb[:, tc * 128:(tc + 1) * 128],
          kb.cw_sb[:, 512:1024], start=True, stop=True,
          reads=["st", "cw"], writes=["psB1"], name="ssmmm1")
    kb.act(kb.probs[:, 0:512], kb.psB0[:, :], AF.Identity,
           reads=["psB0"], writes=["probs"], name="ssmev0")
    kb.act(kb.probs[:, 512:1024], kb.psB1[:, :], AF.Identity,
           reads=["psB1"], writes=["probs"], name="ssmev1")
    for nh in range(2):
        pk = kb.pbk(nh * 512, 512)
        for j in range(NPAIR):
            kb.mm(kb.psBIG[:, nh * 512:(nh + 1) * 512],
                  kb.ar[:, aobase + j * T + tc * 128: aobase + j * T + (tc + 1) * 128],
                  kb.ar[:, j * 1024 + nh * 512: j * 1024 + (nh + 1) * 512],
                  start=(j == 0), stop=False,
                  reads=kb.ark(aobase + j * T + tc * 128, 128) +
                  kb.ark(j * 1024 + nh * 512, 512),
                  writes=pk, name="prmm")
        kb.mm(kb.psBIG[:, nh * 512:(nh + 1) * 512], kb.ones_bf[0:1, :],
              kb.row2[0:1, nh * 512:(nh + 1) * 512], start=False, stop=True,
              reads=["ones", "row2"], writes=pk, name="prbias")
    for nh in range(2):
        pk = kb.pbk(1024 + nh * 512, 512)
        for dcc in range(DC):
            kb.mm(kb.psBIG[:, 1024 + nh * 512: 1024 + (nh + 1) * 512],
                  kb.hnT[:, dcc * T + tc * 128: dcc * T + (tc + 1) * 128],
                  kb.ar[:, wg_base + dcc * 1024 + nh * 512:
                        wg_base + dcc * 1024 + (nh + 1) * 512],
                  start=(dcc == 0), stop=False,
                  reads=[f"hnT{dcc}"] + kb.ark(wg_base + dcc * 1024 + nh * 512, 512),
                  writes=pk, name="gmm")
        kb.mm(kb.psBIG[:, 1024 + nh * 512: 1024 + (nh + 1) * 512], kb.ones_bf[0:1, :],
              kb.row1[0:1, nh * 512:(nh + 1) * 512], start=False, stop=True,
              reads=["ones", "row1"], writes=pk, name="gbias")
    kb.act(kb.zf[:, :], kb.psBIG[:, 1024:2048], AF.Sigmoid,
           reads=kb.pbk(1024, 1024), writes=["zf"], name="gsig")
    xk = f"x{tc}"
    xv = kb.x[:, tc * 1024:(tc + 1) * 1024]
    for hh in range(2):
        xh = kb.x[:, tc * 1024 + hh * 512: tc * 1024 + (hh + 1) * 512]
        ph = kb.probs[:, hh * 512:(hh + 1) * 512]
        p.op("vector", lambda e, o=kb.zT[:, :],
             a=kb.psBIG[:, hh * 512:(hh + 1) * 512], b=ph:
             e.tensor_sub(o, a, b),
             reads=kb.pbk(hh * 512, 512) + ["probs"], writes=["zT"], name="csub")
        p.op("vector", lambda e, o=kb.zT[:, :], a=kb.zT[:, :],
             b=kb.zf[:, hh * 512:(hh + 1) * 512]:
             e.tensor_mul(o, a, b), reads=["zT", "zf"], writes=["zT"], name="cmul")
        p.op("vector", lambda e, o=ph, a=ph, b=kb.zT[:, :]:
             e.tensor_add(o, a, b), reads=["probs", "zT"], writes=["probs"],
             name="ccombo")
        kb.act(kb.zT[:, :], xh, AF.Identity, bias=kb.nmr_o[:, tc:tc + 1],
               scale=kb.rstd_o[:, tc:tc + 1],
               reads=[xk, f"orstd{tc}", f"onmr{tc}"], writes=["zT"], name="crez")
        p.op("vector", lambda e, o=ph, a=ph, b=kb.zT[:, :]:
             e.tensor_add(o, a, b), reads=["probs", "zT"], writes=["probs"],
             name="caddz")
    p.dma("sync", kb.x1_scr[tc * 128:(tc + 1) * 128, :], kb.probs[:, 0:1024],
          reads=["probs"], writes=[f"x1s{tc}"], name="x1spill", ch=0)
    p.op("vector", lambda e, o=xv, a=xv, b=kb.probs[:, 0:1024]:
         e.tensor_add(o, a, b), reads=[xk, "probs"], writes=[xk], name="cadd1")


def _outproj_add(kb, tc, bias_row, bias_key):
    T = kb.T
    aobase = 8192 + 8 * T
    for nh in range(2):
        pk = kb.pbk(nh * 512, 512)
        for j in range(NPAIR):
            kb.mm(kb.psBIG[:, nh * 512:(nh + 1) * 512],
                  kb.ar[:, aobase + j * T + tc * 128: aobase + j * T + (tc + 1) * 128],
                  kb.ar[:, j * 1024 + nh * 512: j * 1024 + (nh + 1) * 512],
                  start=(j == 0), stop=False,
                  reads=kb.ark(aobase + j * T + tc * 128, 128) +
                  kb.ark(j * 1024 + nh * 512, 512),
                  writes=pk, name="pr1mm")
        kb.mm(kb.psBIG[:, nh * 512:(nh + 1) * 512], kb.ones_bf[0:1, :],
              bias_row[0:1, nh * 512:(nh + 1) * 512], start=False, stop=True,
              reads=["ones", bias_key], writes=pk, name="pr1bias")
    xk = f"x{tc}"
    xv = kb.x[:, tc * 1024:(tc + 1) * 1024]
    kb.p.op("vector", lambda e, o=xv, a=kb.psBIG[:, 0:1024]:
            e.scalar_tensor_tensor(o, a, 1.0, o, ALU.mult, ALU.add),
            reads=kb.pbk(0, 1024) + [xk], writes=[xk], name="pr1evac")


def attn_l0(kb, W, x1_scr):
    T = kb.T
    kb.x1_scr = x1_scr
    load_row(kb, kb.row1, W["bV"], 1024, "row1")
    load_row(kb, kb.row2, W["bO"], 1024, "row2")
    load_col_tile(kb, W["bQ"], 1024, kb.qkvcol, 0, "qcol", prescale=0.125)
    load_col_tile(kb, W["bK"], 1024, kb.qkvcol, 8, "kcol")
    kb.p.dma("sync", kb.acol[:, 0:1], W["A"][:, None], writes=["acol"],
             name="aload", ch=1)
    kb.p.dma("gpsimd", kb.cw_sb[:, :], W["Cw"][:, :], writes=["cw"],
             name="cwload", ch=2)
    load_wblock(kb, W["Bw"][:, :], kb.bw_sb, 0, ["bw"], ch=1)
    load_wblock(kb, W["WV"][:, :], kb.ar, 0, kb.ark(0, 8192))
    _v_phase(kb, kb.row1, "row1")
    _ssm_u_scan(kb)
    ci = 0
    queue = []
    LAG0 = 2
    for j in range(NPAIR):
        _qk_pair(kb, W["WQ"][:, :], W["WK"][:, :], j,
                 (kb.qkvcol[:, 0:8], "qcol"), (kb.qkvcol[:, 8:16], "kcol"))
        for h_loc in (0, 64):
            for qc in range(kb.TC):
                if len(queue) >= LAG0:
                    _l0_head_B(kb, *queue.pop(0))
                _l0_head_A(kb, j, h_loc, qc, ci)
                queue.append((j, h_loc, qc, ci))
                ci += 1
    for it in queue:
        _l0_head_B(kb, *it)
    if not globals().get("ATTN0_SKIP_P3", False):
        wg_base = 8192 if 8 * T >= 8192 else 8192 + 16 * T
        load_row(kb, kb.row1, W["bg"], 1024, "row1")
        load_wblock(kb, W["WO"][:, :], kb.ar, 0, kb.ark(0, 8192))
        load_wblock(kb, W["Wg"][:, :], kb.ar, wg_base, kb.ark(wg_base, 8192), ch=1)
        for tc in range(kb.TC):
            _l0_combine(kb, tc, wg_base)


def attn_l1(kb, W):
    T = kb.T
    load_row(kb, kb.row2, W["b_out"], 1024, "row2")
    wqf = W["wq"][:, :, :].rearrange("d h e -> d (h e)")
    wkf = W["wk"][:, :, :].rearrange("d h e -> d (h e)")
    wvf = W["wv"][:, :, :].rearrange("d h e -> d (h e)")
    load_wblock(kb, wvf, kb.ar, 0, kb.ark(0, 8192))
    _v_phase(kb, None, None)
    ci = 0
    queue = []
    LAG1 = 3
    for j in range(A1_NPAIR):
        _qk_pair(kb, wqf, wkf, j, None, None)
        for h_loc in (0, 64):
            for qc in range(kb.TC):
                if len(queue) >= LAG1:
                    _l1_head_B(kb, *queue.pop(0))
                _l1_head_A(kb, j, h_loc, qc, ci)
                queue.append((j, h_loc, qc, ci))
                ci += 1
    for it in queue:
        _l1_head_B(kb, *it)
    kb.p.op("vector", lambda e, o=kb.zf[:, 0:1]: e.tensor_copy(o, kb.zf[:, 0:1]),
            reads=["zfb0", "zfb1", "zfb2", "zfb3",
                   "prb0", "prb1", "prb2", "prb3"], writes=["zf", "probs"],
            name="zfence")
    if not globals().get("ATTN1_SKIP_PROJ", False):
        for j in range(NPAIR):
            kb.p.dma("gpsimd", kb.ar[:, j * 1024:(j + 1) * 1024],
                     W["w_out"][2 * j:2 * j + 2, :, :].rearrange("h e d -> (h e) d"),
                     writes=kb.ark(j * 1024, 1024), name="woload", ch=1)
        for tc in range(kb.TC):
            _outproj_add(kb, tc, kb.row2, "row2")


def _l1_pbuf(kb, par):
    if par == 0:
        base = 2 * kb.T + 2048
        ap = lambda lo, n: kb.ar[:, base + lo: base + lo + n]
        keys = lambda lo, n: kb.ark(base + lo, n)
    elif par == 1:
        zb = kb.zf[:, :].bitcast(BF16)
        ap = lambda lo, n, z=zb: z[:, lo: lo + n]
        keys = lambda lo, n: ["zfb%d" % i for i in
                              range(lo // 512, (lo + n + 511) // 512)]
    else:
        pb = kb.probs[:, :].bitcast(BF16)
        ap = lambda lo, n, z=pb: z[:, lo: lo + n]
        keys = lambda lo, n: ["prb%d" % i for i in
                              range(lo // 512, (lo + n + 511) // 512)]
    return ap, keys


def _l1_head_A(kb, j, h_loc, qc, ci):
    T = kb.T
    par = ci % 2
    s = ci % 32
    nm, nmk = kb.stat("nm", s)
    es, esk = kb.stat("es", s)
    rc, rck = kb.stat("rc", s)
    c4 = (ci % 4) * 4
    pap, pkeys = _l1_pbuf(kb, ci % 3)
    q_ap = kb.ar[h_loc:h_loc + 64, qc * 128:(qc + 1) * 128]
    for sq in range(4):
        kb.mm(kb.psBIG[:, sq * 512:(sq + 1) * 512], q_ap,
              kb.ar[h_loc:h_loc + 64, T + sq * 512: T + (sq + 1) * 512],
              start=True, stop=True,
              reads=kb.ark(qc * 128, 128) + kb.ark(T + sq * 512, 512),
              writes=[f"pb{sq}"], name="smm1")
        kb.p.op("vector", lambda e, o=kb.rstd_o[:, c4 + sq:c4 + sq + 1],
                i=kb.psBIG[:, sq * 512:(sq + 1) * 512]:
                e.reduce_max(o, i, AX.X),
                reads=[f"pb{sq}"], writes=[f"orstd{c4 + sq}"], name="qmax")
    kb.p.op("vector", lambda e, o=nm, i=kb.rstd_o[:, c4:c4 + 4]:
            e.reduce_max(o, i, AX.X, negate=True),
            reads=[f"orstd{c4 + sq}" for sq in range(4)], writes=[nmk],
            name="gmax1")
    for sq in range(4):
        kb.act(pap(sq * 512, 512), kb.psBIG[:, sq * 512:(sq + 1) * 512],
               AF.Exp, bias=nm, accum_out=kb.nmr_o[:, c4 + sq:c4 + sq + 1],
               reads=[f"pb{sq}", nmk],
               writes=pkeys(sq * 512, 512) + [f"onmr{c4 + sq}"], name="sexp1")
    kb.p.op("vector", lambda e, o=es, i=kb.nmr_o[:, c4:c4 + 4]:
            e.reduce_sum(o, i, AX.X),
            reads=[f"onmr{c4 + sq}" for sq in range(4)], writes=[esk],
            name="esum1")
    kb.p.op("vector", lambda e, o=rc, i=es: e.reciprocal(o, i),
            reads=[esk], writes=[rck], name="srecip1")
    for sq in range(4):
        kb.p.op("vector", lambda e, o=pap(sq * 512, 512), s_=rc:
                e.tensor_scalar_mul(o, o, s_),
                reads=pkeys(sq * 512, 512) + [rck],
                writes=pkeys(sq * 512, 512), name="snorm1")


def _l1_head_B(kb, j, h_loc, qc, ci):
    T = kb.T
    vbase, aobase = 8192, 8192 + 8 * T
    head = 2 * j + (1 if h_loc else 0)
    par = ci % 2
    pap, pkeys = _l1_pbuf(kb, ci % 3)
    for r in range(2):
        for kk in range(8):
            c = r * 8 + kk
            kb.tr(kb.psT[:, kk * 128:(kk + 1) * 128], pap(c * 128, 128),
                  kb.ident_bf[:, :], reads=pkeys(c * 128, 128), writes=["psT"],
                  name="str1")
        kb.vop(lambda e, o=pap(r * 1024, 1024), i=kb.psT[:, 0:1024]:
               e.tensor_copy(o, i),
               reads=["psT"], writes=pkeys(r * 1024, 1024), name="sTevac1")
    psA = kb.psB0 if par == 0 else kb.psB1
    psAk = "psB0" if par == 0 else "psB1"
    for kc in range(kb.TC):
        kb.mm(psA[h_loc:h_loc + 64, 0:128],
              kb.ar[:, vbase + kc * 1024 + head * 64: vbase + kc * 1024 + head * 64 + 64],
              pap(kc * 128, 128),
              start=(kc == 0), stop=(kc == kb.TC - 1),
              reads=kb.ark(vbase + kc * 1024, 1024) + pkeys(kc * 128, 128),
              writes=[psAk], name="aomm1", tile_position=(0, h_loc))
    kb.act(kb.ar[h_loc:h_loc + 64, aobase + j * T + qc * 128:
                 aobase + j * T + (qc + 1) * 128],
           psA[h_loc:h_loc + 64, 0:128], AF.Identity,
           reads=[psAk], writes=kb.ark(aobase + j * T + qc * 128, 128),
           name="aoevac1")




def load_x(kb, x_dram):
    kb.p.dma("sync", kb.x[:, :].rearrange("p (c j) -> p c j", c=kb.TC),
             x_dram[:, :].rearrange("(c p) j -> p c j", p=128),
             writes=[f"x{c}" for c in range(kb.TC)], name="xload")


def store_y(kb, y_dram):
    kb.p.dma("sync", y_dram[:, :].rearrange("(c p) j -> p c j", p=128),
             kb.x[:, :].rearrange("p (c j) -> p c j", c=kb.TC),
             reads=[f"x{c}" for c in range(kb.TC)], name="ystore")
    kb.p.barrier_all_dma()


def load_consts(kb, cid_bf, cid_f, cmask):
    kb.p.dma("gpsimd", kb.ident_bf[:, :], cid_bf[:, :], writes=["idb"],
             name="idbload", ch=2)
    kb.p.dma("sync", kb.ident_f[:, :], cid_f[:, :], writes=["idf"],
             name="idfload", ch=1)
    kb.p.dma("gpsimd", kb.mask[:, :], cmask[:, :], writes=["mask"],
             name="maskload", ch=2)
    kb.p.op("vector", lambda e: e.memset(kb.ones_bf[0:1, :], 1.0),
            writes=["ones"], name="onesset")


STAGE_NAMES = ("ffnA", "attn0", "ffnB", "moe1", "attn1", "moe2")

A1_OLDMAX = False
A1_NPAIR = NPAIR
A1_NOACCUM = False
A1_SKIPNORM = False
A1_NOPAR = False
A1_NOSTATB = False
A1_CUT = 5

DEBUG_TAPS = []


def build_program(T=2048, stages=STAGE_NAMES, n_moe_experts=NE):
    import contextlib
    ctx = contextlib.ExitStack()
    nc = bass.Bass("TRN2", target_bir_lowering=False)
    p = Plan()

    dt = {}

    def din(name, shape):
        dt[name] = nc.dram_tensor(name, list(shape), F32, kind="ExternalInput")
        return dt[name]

    din("L0_attn_x", (T, 1024))
    for n, s in [("L0_attn_ln1_g", (1024,)), ("L0_attn_ln1_b", (1024,)),
                 ("L0_attn_WQ", (1024, 1024)), ("L0_attn_bQ", (1024,)),
                 ("L0_attn_WK", (1024, 1024)), ("L0_attn_bK", (1024,)),
                 ("L0_attn_WV", (1024, 1024)), ("L0_attn_bV", (1024,)),
                 ("L0_attn_WO", (1024, 1024)), ("L0_attn_bO", (1024,)),
                 ("L0_attn_Wg", (1024, 1024)), ("L0_attn_bg", (1024,)),
                 ("L0_attn_A", (64,)), ("L0_attn_Bw", (1024, 64)),
                 ("L0_attn_Cw", (64, 1024)),
                 ("L0_attn_ln2_g", (1024,)), ("L0_attn_ln2_b", (1024,)),
                 ("L0_attn_W1", (1024, 4096)), ("L0_attn_b1", (4096,)),
                 ("L0_attn_W2", (4096, 1024)), ("L0_attn_b2", (1024,)),
                 ("L0_ffn_w1", (4096, 1024)), ("L0_ffn_b1", (4096,)),
                 ("L0_ffn_w2", (1024, 4096)), ("L0_ffn_b2", (1024,)),
                 ("L1_attn_norm_scale", (1024,)),
                 ("L1_attn_wq", (1024, 16, 64)), ("L1_attn_wk", (1024, 16, 64)),
                 ("L1_attn_wv", (1024, 16, 64)), ("L1_attn_w_out", (16, 64, 1024)),
                 ("L1_attn_b_out", (1024,)),
                 ("L1_ffn_gate_w", (8, 1024)), ("L1_ffn_w1", (8, 1024, 4096)),
                 ("L1_ffn_b1", (8, 4096)), ("L1_ffn_w2", (8, 4096, 1024)),
                 ("L1_ffn_b2", (8, 1024))]:
        din(n, s)
    din("c_ident_bf", (128, 128))
    din("c_ident_f", (128, 128))
    din("c_mask", (128, 256))
    y = nc.dram_tensor("y", [T, 1024], F32, kind="ExternalOutput")
    scr_up = nc.dram_tensor("scr_up", [4, 128, 8192], BF16, kind="Internal")
    scr_dn = nc.dram_tensor("scr_dn", [4, 128, 8192], BF16, kind="Internal")

    kb = KB(nc, p, T, ctx)
    for v in (EPS_LN, EPS_RMS, -0.5, 0.125, -1.0):
        kb.register_const(v)
    load_consts(kb, dt["c_ident_bf"], dt["c_ident_f"], dt["c_mask"])
    load_x(kb, dt["L0_attn_x"])

    def ffn_l0(first):
        load_col_tile(kb, dt["L0_ffn_b1"], 4096, kb.bup_col, 0, "bup")
        load_row(kb, kb.row0, dt["L0_ffn_b2"], 1024, "row0")
        if first:
            alpha2_prepass(kb, dt["L0_ffn_w2"][:, :])
            up = sign_up_loader(kb, dt["L0_ffn_w1"][:, :], scr_up)
            dn = sign_dn_loader(kb, dt["L0_ffn_w2"][:, :], scr_dn)
        else:
            up = scratch_up_loader(kb, scr_up)
            dn = scratch_dn_loader(kb, scr_dn)
        ffn_stage(kb, up, dn, AF.Gelu, kb.bup_col, kb.alpha1,
                  lambda tc: 0.5, add_b2=True,
                  bias_key="bup", scale_key="alpha_w")

    if "ffnA" in stages:
        ln_stage(kb, "plain")
        ffn_l0(True)
    if "attn0" in stages:
        x1_scr = nc.dram_tensor("x1_scr", [T, 1024], F32, kind="Internal")
        load_col_tile(kb, dt["L0_attn_ln1_g"], 1024, kb.gcol, 0, "gcol")
        load_col_tile(kb, dt["L0_attn_ln1_b"], 1024, kb.bcol, 0, "bcol")
        ln_stage(kb, "double_gb", kb.gcol, kb.bcol)
        attn_l0(kb, dict(
            WQ=dt["L0_attn_WQ"], WK=dt["L0_attn_WK"], WV=dt["L0_attn_WV"],
            WO=dt["L0_attn_WO"], Wg=dt["L0_attn_Wg"],
            bQ=dt["L0_attn_bQ"], bK=dt["L0_attn_bK"], bV=dt["L0_attn_bV"],
            bO=dt["L0_attn_bO"], bg=dt["L0_attn_bg"],
            A=dt["L0_attn_A"], Bw=dt["L0_attn_Bw"], Cw=dt["L0_attn_Cw"]), x1_scr)
        load_col_tile(kb, dt["L0_attn_ln2_g"], 1024, kb.gcol, 0, "gcol")
        load_col_tile(kb, dt["L0_attn_ln2_b"], 1024, kb.bcol, 0, "bcol")
        ln_stage(kb, "single_gb_dram", kb.gcol, kb.bcol)
        load_col_tile(kb, dt["L0_attn_b1"], 4096, kb.bup_col, 0, "bup")
        load_row(kb, kb.row0, dt["L0_attn_b2"], 1024, "row0")
        ffn_stage(kb, simple_up_loader(kb, dt["L0_attn_W1"][:, :]),
                  simple_dn_loader(kb, dt["L0_attn_W2"][:, :]),
                  AF.Gelu, kb.bup_col, None, lambda tc: 1.0, add_b2=True,
                  bias_key="bup", scale_key=None)
    if "ffnB" in stages:
        ln_stage(kb, "plain")
        ffn_l0(False)
    moe_on = [m for m in ("moe1", "moe2") if m in stages]
    if GATHER_MOE and moe_on:
        xslots = nc.dram_tensor("xslots", [NSLOT, 1024], BF16, kind="Internal")
        yslots = nc.dram_tensor("yslots", [NSLOT, 1024], F32, kind="Internal")
    for mname in ("moe1", "moe2"):
        if mname in stages:
            moe_prep_gate(kb, dt["L1_ffn_gate_w"])
            if GATHER_MOE:
                ln_stage(kb, "plain", gate=True, hnt=False)
                moe_gate_softmax(kb)
                moe_route(kb)
                moe_scatter_x(kb, xslots)
                for ex in range(NE):
                    moe_expert_gathered(kb, ex, dt["L1_ffn_w1"][ex],
                                        dt["L1_ffn_w2"][ex],
                                        dt["L1_ffn_b1"][ex],
                                        dt["L1_ffn_b2"][ex], xslots, yslots)
                moe_gather_combine(kb, yslots)
                moe_fence(kb)
            else:
                ln_stage(kb, "plain", gate=True)
                moe_gate_softmax(kb)
                moe_experts(kb, dt["L1_ffn_w1"], dt["L1_ffn_w2"],
                            dt["L1_ffn_b1"], dt["L1_ffn_b2"])
        if mname == "moe1" and "attn1" in stages:
            load_col_tile(kb, dt["L1_attn_norm_scale"], 1024, kb.gcol, 0, "gcol")
            ln_stage(kb, "double_rms", kb.gcol)
            attn_l1(kb, dict(
                wq=dt["L1_attn_wq"], wk=dt["L1_attn_wk"], wv=dt["L1_attn_wv"],
                w_out=dt["L1_attn_w_out"], b_out=dt["L1_attn_b_out"]))

    for tname, tshape, tfn in DEBUG_TAPS:
        tdram = nc.dram_tensor(tname, list(tshape), F32, kind="ExternalOutput")
        ap, rkeys = tfn(kb)
        p.dma("gpsimd", tdram[:, :] if len(tshape) == 2 else tdram[:],
              ap, reads=rkeys, name="tap", ch=3)

    store_y(kb, y)

    p.finalize()
    sems = {}
    for axis in p.used_axes():
        sems[axis] = ctx.enter_context(nc.semaphore(axis))
    p.emit(nc, sems)
    ctx.close()
    return nc, p




def make_consts():
    cid_bf = np.eye(128, dtype=np.float32)
    cid_f = np.eye(128, dtype=np.float32)
    m0 = np.where(np.arange(256)[None, 0:128] > np.arange(128)[:, None],
                  0.0, NEG).astype(np.float32)
    m1 = np.where(np.arange(128)[None, :] <= np.arange(128)[:, None],
                  0.0, NEG).astype(np.float32)
    cmask = np.concatenate([m0, m1], axis=1).astype(np.float32)
    return cid_bf, cid_f, cmask


_CACHED = {}


def kernel(**inputs):
    T = 2048
    key = "full"
    if key not in _CACHED:
        _CACHED[key] = build_program(T)
    nc, _ = _CACHED[key]
    cid_bf, cid_f, cmask = make_consts()
    x = np.ascontiguousarray(np.asarray(inputs["L0_attn_x"], dtype=np.float32))
    base = {}
    for k, v in inputs.items():
        if k == "L0_attn_x":
            continue
        base[k] = np.ascontiguousarray(np.asarray(v, dtype=np.float32))
    base["c_ident_bf"] = cid_bf
    base["c_ident_f"] = cid_f
    base["c_mask"] = cmask
    in_maps = []
    for core in range(8):
        m = dict(base)
        m["L0_attn_x"] = x[core]
        in_maps.append(m)
    res = run_bass_kernel_spmd(nc, in_maps, core_ids=list(range(8)))
    out = np.stack([r["y"] for r in res.results], axis=0)
    return out.astype(np.float32)


# version 09
# speedup vs baseline: 1.8727x; 1.0084x over previous
import numpy as np
import concourse.bass as bass
import concourse.mybir as mybir
from concourse.bass_utils import run_bass_kernel_spmd
from concourse.bass_core_rust import IndirectOffsetOnAxis

F32 = mybir.dt.float32
BF16 = mybir.dt.bfloat16
U32 = mybir.dt.uint32
AF = mybir.ActivationFunctionType
ALU = mybir.AluOpType
AX = mybir.AxisListType

ENGINES = ("sync", "scalar", "vector", "tensor", "gpsimd")
DMA_INC = 16



class Op:
    __slots__ = ("engine", "axis", "fn", "deps", "eidx", "tagged", "waits",
                 "clock", "rank", "name")

    def __repr__(self):
        return f"<Op {self.engine}/{self.axis}#{self.eidx} {self.name}>"


class Plan:

    def __init__(self):
        self.ops = []
        self.axis_count = {}
        self.last_write = {}
        self.readers = {}

    def _add(self, engine, axis, fn, reads, writes, deps, name):
        op = Op()
        op.engine = engine
        op.axis = axis
        op.fn = fn
        op.name = name
        op.tagged = "_dma" in axis
        op.waits = None
        op.clock = None
        op.rank = None
        d = []
        for k in reads:
            w = self.last_write.get(k)
            if w is not None:
                d.append(w)
        for k in writes:
            w = self.last_write.get(k)
            if w is not None:
                d.append(w)
            rs = self.readers.get(k)
            if rs:
                d.extend(rs)
        for x in deps:
            if x is not None:
                d.append(x)
        op.deps = d
        op.eidx = self.axis_count.get(axis, 0)
        self.axis_count[axis] = op.eidx + 1
        for k in reads:
            self.readers.setdefault(k, []).append(op)
        for k in writes:
            self.last_write[k] = op
            self.readers[k] = []
        self.ops.append(op)
        return op

    def op(self, engine, fn, reads=(), writes=(), deps=(), name=""):
        return self._add(engine, engine, fn, reads, writes, deps, name)

    def dma(self, engine, out, in_, reads=(), writes=(), deps=(), name="", ch=0,
            **kw):
        def fn(eng, _out=out, _in=in_, _kw=kw):
            return eng.dma_start(_out, _in, **_kw)
        return self._add(engine, "%s_dma%d" % (engine, ch), fn, reads, writes,
                         deps, name)

    def barrier_all_dma(self, engine="gpsimd"):
        deps = []
        for axis, cnt in self.axis_count.items():
            if "_dma" in axis and cnt > 0:
                for o in reversed(self.ops):
                    if o.axis == axis:
                        deps.append(o)
                        break
        self._add(engine, engine, None, (), (), deps, "final_dma_barrier")

    def finalize(self):
        observed = {e: {} for e in ENGINES}
        axis_ops = {}
        for op in self.ops:
            axis_ops.setdefault(op.axis, []).append(op)
        issued = {}
        for op in self.ops:
            obs = observed[op.engine]
            need = {}
            for dop in op.deps:
                v = dop.eidx + 1
                if need.get(dop.axis, 0) < v:
                    need[dop.axis] = v
            if "_dma" in op.axis and op.eidx > 0:
                v = op.eidx
                if need.get(op.axis, 0) < v:
                    need[op.axis] = v
            waits = []
            for axis, v in need.items():
                if axis == "tensor" and op.engine == "tensor":
                    continue
                if obs.get(axis, 0) >= v:
                    continue
                waits.append((axis, v))
            for dop in op.deps:
                c = dop.clock
                if c:
                    for a, v in c.items():
                        if obs.get(a, 0) < v:
                            obs[a] = v
            for axis, v in need.items():
                if obs.get(axis, 0) < v:
                    obs[axis] = v
            for axis, v in waits:
                if obs.get(axis, 0) < v:
                    obs[axis] = v
            op.waits = waits
            for axis, v in waits:
                axis_ops[axis][v - 1].tagged = True
            clk = dict(obs)
            if clk.get(op.axis, 0) < op.eidx + 1:
                clk[op.axis] = op.eidx + 1
            op.clock = clk
            issued[op.axis] = op.eidx + 1
        self.rank_map = {}
        for axis, ops in axis_ops.items():
            r = 0
            ranks = [0] * len(ops)
            for i, o in enumerate(ops):
                if o.tagged:
                    r += 1
                    o.rank = r
                ranks[i] = r
            self.rank_map[axis] = ranks
        self.axis_ops = axis_ops

    def emit(self, nc, sems):
        per_engine = {e: [] for e in ENGINES}
        for op in self.ops:
            per_engine[op.engine].append(op)

        def make_runner(engine_name):
            ops = per_engine[engine_name]

            def run(eng):
                for op in ops:
                    for axis, v in op.waits:
                        r = self.rank_map[axis][v - 1]
                        assert r > 0, (op, axis, v)
                        mult = DMA_INC if "_dma" in axis else 1
                        eng.wait_ge(sems[axis], r * mult)
                    if op.fn is None:
                        continue
                    inst = op.fn(eng)
                    if op.tagged:
                        inst.then_inc(sems[op.axis],
                                      DMA_INC if "_dma" in op.axis else 1)
            return run

        with nc.Block() as block:
            block.sync(make_runner("sync"))
            block.scalar(make_runner("scalar"))
            block.vector(make_runner("vector"))
            block.tensor(make_runner("tensor"))
            block.gpsimd(make_runner("gpsimd"))

    def used_axes(self):
        return [a for a, c in self.axis_count.items() if c > 0]



D = 1024
DC = 8
DFF = 4096
NH = 16
DH = 64
NPAIR = 8
NS = 64
WIN = 128
NE = 8
NEG = -1.0e30
CAP = 512
NSLOT = NE * CAP
GATHER_MOE = True


class KB:

    def __init__(self, nc, p, T, ctx):
        self.nc = nc
        self.p = p
        self.T = T
        self.TC = T // 128
        self.NTG = T // 512
        e = ctx.enter_context
        self.x = e(nc.sbuf_tensor("x", [128, self.TC * 1024], F32))
        self.hnT = e(nc.sbuf_tensor("hnT", [128, DC * T], BF16))
        self.AR = max(36864, 8192 + 16 * T)
        self.ar = e(nc.sbuf_tensor("ar", [128, self.AR], BF16))
        self.zf = e(nc.sbuf_tensor("zf", [128, 1024], F32))
        self.zbf = e(nc.sbuf_tensor("zbf", [128, 1024], BF16))
        self.zT = e(nc.sbuf_tensor("zT", [128, 512], F32))
        self.probs = e(nc.sbuf_tensor("probs", [128, 1024], F32))
        self.st_sb = e(nc.sbuf_tensor("st_sb", [64, 2048], BF16))
        self.ab_sb = e(nc.sbuf_tensor("ab_sb", [64, 512], F32))
        self.cw_sb = e(nc.sbuf_tensor("cw_sb", [64, 1024], BF16))
        self.bw_sb = e(nc.sbuf_tensor("bw_sb", [128, DC * 64], BF16))
        self.acol = e(nc.sbuf_tensor("acol", [64, 1], F32))
        st = lambda n: e(nc.sbuf_tensor(n, [128, 16], F32))
        self.nsum = st("nsum")
        self.sumsq = st("sumsq")
        self.var = st("var")
        self.rstd = st("rstd")
        self.nmr = st("nmr")
        self.tmpst = st("tmpst")
        self.negmax = st("negmax")
        self.rstd_o = st("rstd_o")
        self.nmr_o = st("nmr_o")
        self.esum = st("esum")
        self.recip = st("recip")
        self.alpha1 = e(nc.sbuf_tensor("alpha1", [128, 32], F32))
        self.alpha2 = e(nc.sbuf_tensor("alpha2", [128, 8], F32))
        self.bup_col = e(nc.sbuf_tensor("bup_col", [128, 32], F32))
        self.gcol = e(nc.sbuf_tensor("gcol", [128, 8], F32))
        self.bcol = e(nc.sbuf_tensor("bcol", [128, 8], F32))
        self.qkvcol = e(nc.sbuf_tensor("qkvcol", [128, 24], F32))
        self.nscol = e(nc.sbuf_tensor("nscol", [128, 8], F32))
        self.coeff = e(nc.sbuf_tensor("coeff", [128, NE * 16], F32))
        self.logit = e(nc.sbuf_tensor("logit", [128, 16 * 8], F32))
        self.selw = e(nc.sbuf_tensor("selw", [128, 16], F32))
        self.found = e(nc.sbuf_tensor("found", [128, 16], F32))
        self.gwT = e(nc.sbuf_tensor("gwT", [128, 64], F32))
        self.idx = e(nc.sbuf_tensor("idx", [128, 16], U32))
        self.ones_bf = e(nc.sbuf_tensor("ones_bf", [1, 128], BF16))
        self.row0 = e(nc.sbuf_tensor("row0", [1, 1024], BF16))
        self.row1 = e(nc.sbuf_tensor("row1", [1, 1024], BF16))
        self.row2 = e(nc.sbuf_tensor("row2", [1, 1024], BF16))
        self.vspread = e(nc.sbuf_tensor("vspread", [32, 128], F32))
        self.cpool = e(nc.sbuf_tensor("cpool", [128, 8], F32))
        self.cpool_n = 0
        self.ident_bf = e(nc.sbuf_tensor("ident_bf", [128, 128], BF16))
        self.ident_f = e(nc.sbuf_tensor("ident_f", [128, 128], F32))
        self.mask = e(nc.sbuf_tensor("mask", [128, 256], BF16))
        self.psBIG = e(nc.psum_tensor("psBIG", [128, 2048], F32))
        self.psB0 = e(nc.psum_tensor("psB0", [128, 512], F32))
        self.psB1 = e(nc.psum_tensor("psB1", [128, 512], F32))
        self.psT = e(nc.psum_tensor("psT", [128, 1024], BF16))
        self.psS = e(nc.psum_tensor("psS", [128, 512], F32))
        assert nc.psum_banks_remaining == 0, nc.psum_banks_remaining
        self.sbuf_left = nc.sbuf_bytes_remaining

    def ark(self, lo, n):
        return ["ar%d" % i for i in range(lo // 512, (lo + n + 511) // 512)]

    def pbk(self, lo, n):
        return ["pb%d" % i for i in range(lo // 512, (lo + n + 511) // 512)]

    def mm(self, out, lhsT, rhs, start, stop, reads, writes, name="mm",
           tile_position=None):
        return self.p.op(
            "tensor",
            lambda e, o=out, a=lhsT, b=rhs, s=start, t=stop, tp=tile_position:
                e.matmul(o, a, b, start=s, stop=t, skip_group_check=True,
                         tile_position=tp),
            reads=reads, writes=writes, name=name)

    def tr(self, out, in_, ident, reads, writes, name="tr", ident_key="idb"):
        return self.p.op(
            "tensor",
            lambda e, o=out, a=in_, i=ident: e.transpose(o, a, i),
            reads=list(reads) + [ident_key], writes=writes, name=name)

    def stat(self, kind, s):
        tiles = {
            "nm": (self.negmax, "nm", self.sumsq, "sumsq"),
            "es": (self.esum, "es", self.nsum, "nsum"),
            "rc": (self.recip, "rc", self.rstd, "rstd"),
            "tm": (self.tmpst, "tmpst", self.var, "var"),
        }[kind]
        if s < 16:
            t, kp = tiles[0], tiles[1]
        else:
            t, kp = tiles[2], tiles[3]
        c = s % 16
        return t[:, c:c + 1], f"{kp}{c}"

    def idma(self, out, out_off, in_, in_off, reads=(), writes=(), ch=3,
             name="idma", **kw):
        def fn(e, _o=out, _oo=out_off, _i=in_, _io=in_off, _kw=kw):
            return e.indirect_dma_start(_o, _oo, _i, _io, **_kw)
        return self.p._add("gpsimd", "gpsimd_dma%d" % ch, fn, reads, writes,
                           (), name)

    def register_const(self, v):
        i = self.cpool_n
        self.cpool_n += 1
        assert i < 8
        self.p.op("vector", lambda e, c=self.cpool[:, i:i + 1], vv=v:
                  e.memset(c, vv), writes=["cpool"], name="cpool")
        self.nc.const_aps.register(F32, v, self.cpool[:, i:i + 1])

    def act(self, out, in_, func, bias=0.0, scale=1.0, reads=(), writes=(),
            accum_out=None, name="act"):
        extra = []
        if (isinstance(bias, float) and bias != 0.0) or \
           (isinstance(scale, float) and scale not in (1.0,)):
            extra = ["cpool"]

        def fn(e, o=out, a=in_, f=func, b=bias, s=scale, ac=accum_out):
            if ac is None:
                return e.activation(o, a, f, bias=b, scale=s)
            return e.activation(o, a, f, bias=b, scale=s, accum_out=ac)
        return self.p.op("scalar", fn, reads=list(reads) + extra, writes=writes,
                         name=name)

    def vop(self, fn, reads=(), writes=(), name="vop", eng="vector"):
        return self.p.op(eng, fn, reads=reads, writes=writes, name=name)




def load_col_tile(kb, vec_ap, F, dest, dcol, key, prescale=None):
    P = F // 128
    kb.p.dma("sync", kb.vspread[0:P, 0:128],
             vec_ap.rearrange("(a j) -> a j", j=128),
             writes=["vspread"], name="colspread", ch=1)
    kb.tr(kb.psS[0:128, 0:P], kb.vspread[0:P, 0:128], kb.ident_f[0:P, 0:P],
          reads=["vspread"], writes=["psS"], name="coltr", ident_key="idf")
    kb.vop(lambda e, o=dest[:, dcol:dcol + P], i=kb.psS[0:128, 0:P]:
           e.tensor_copy(o, i),
           reads=["psS"], writes=[key], name="colcopy")
    if prescale is not None:
        kb.vop(lambda e, o=dest[:, dcol:dcol + P], s=prescale:
               e.tensor_scalar_mul(o, o, s),
               reads=[key], writes=[key], name="colscale")


def load_row(kb, row_tile, vec_ap, F, key):
    kb.p.dma("gpsimd", row_tile[0:1, 0:F], vec_ap[None, :], writes=[key],
             name="rowload", ch=2)


def load_wblock(kb, dram_ap_2d, dest, dcol, key, engine="gpsimd", ch=0):
    R, W = dram_ap_2d.shape
    a = R // 128
    kb.p.dma(engine,
             dest[:, dcol:dcol + a * W].rearrange("p (a j) -> p a j", a=a),
             dram_ap_2d.rearrange("(a p) j -> p a j", p=128),
             writes=[key] if isinstance(key, str) else key, name="wblock", ch=ch)



EPS_LN = 1e-5
EPS_RMS = 1e-6


def _ln_stats(kb, src, srckey, c, eps, rms=False, dst=None):
    p = kb.p
    rstd_t, nmr_t, kp = (kb.rstd, kb.nmr, "") if dst is None else dst
    kb.act(kb.zbf[:, :], src, AF.Square, accum_out=kb.sumsq[:, c:c + 1],
           reads=[srckey], writes=["zbf", f"sumsq{c}"], name="sq")
    if not rms:
        kb.act(kb.zbf[:, :], src, AF.Identity, scale=-1.0,
               accum_out=kb.nsum[:, c:c + 1],
               reads=[srckey], writes=["zbf", f"nsum{c}"], name="nsumacc")
        p.op("vector", lambda e, o=kb.tmpst[:, c:c + 1], i=kb.nsum[:, c:c + 1]:
             e.tensor_scalar_mul(o, i, 1.0 / D),
             reads=[f"nsum{c}"], writes=[f"tmpst{c}"], name="nmean")
        p.op("vector", lambda e, o=kb.var[:, c:c + 1], i=kb.tmpst[:, c:c + 1]:
             e.tensor_mul(o, i, i),
             reads=[f"tmpst{c}"], writes=[f"var{c}"], name="mean2")
        p.op("vector", lambda e, o=kb.var[:, c:c + 1], a=kb.sumsq[:, c:c + 1],
             b=kb.var[:, c:c + 1]:
             e.scalar_tensor_tensor(o, a, 1.0 / D, b, ALU.mult, ALU.subtract),
             reads=[f"sumsq{c}", f"var{c}"], writes=[f"var{c}"], name="var")
    else:
        p.op("vector", lambda e, o=kb.var[:, c:c + 1], i=kb.sumsq[:, c:c + 1]:
             e.tensor_scalar_mul(o, i, 1.0 / D),
             reads=[f"sumsq{c}"], writes=[f"var{c}"], name="varrms")
    kb.act(rstd_t[:, c:c + 1], kb.var[:, c:c + 1], AF.Ln, bias=float(eps),
           reads=[f"var{c}"], writes=[f"{kp}rstd{c}"], name="ln")
    kb.act(rstd_t[:, c:c + 1], rstd_t[:, c:c + 1], AF.Exp, scale=-0.5,
           reads=[f"{kp}rstd{c}"], writes=[f"{kp}rstd{c}"], name="exprs")
    if not rms:
        p.op("vector", lambda e, o=nmr_t[:, c:c + 1], a=kb.tmpst[:, c:c + 1],
             b=rstd_t[:, c:c + 1]: e.tensor_mul(o, a, b),
             reads=[f"tmpst{c}", f"{kp}rstd{c}"], writes=[f"{kp}nmr{c}"],
             name="nmr")


def _transpose_to_hnT(kb, c, gb=None):
    T = kb.T
    for j in range(DC):
        kb.tr(kb.psT[:, j * 128:(j + 1) * 128], kb.zbf[:, j * 128:(j + 1) * 128],
              kb.ident_bf[:, :], reads=["zbf"], writes=["psT"], name="trh")
    hview = kb.hnT[:, :].rearrange("p (a t) -> p a t", a=DC)[:, :, c * 128:(c + 1) * 128]
    pview = kb.psT[:, :].rearrange("p (a t) -> p a t", a=DC)
    hkeys = [f"hnT{j}" for j in range(DC)]
    ptk = ["psT"]
    if gb is None:
        kb.vop(lambda e, o=hview, i=pview: e.tensor_copy(o, i),
               reads=ptk, writes=hkeys, name="hevac")
    else:
        mode, g, b = gb
        for j in range(DC):
            dst = kb.hnT[:, j * T + c * 128: j * T + (c + 1) * 128]
            src = kb.psT[:, j * 128:(j + 1) * 128]
            if mode == "gb":
                kb.act(dst, src, AF.Identity, bias=b[:, j:j + 1], scale=g[:, j:j + 1],
                       reads=["psT", "gcol", "bcol"], writes=[f"hnT{j}"],
                       name="hevacgb")
            else:
                kb.act(dst, src, AF.Identity, scale=g[:, j:j + 1],
                       reads=["psT", "gcol"], writes=[f"hnT{j}"], name="hevacs")


def ln_stage(kb, kind, g1col=None, b1col=None, gate=False, hnt=True):
    T = kb.T
    for c in range(kb.TC):
        xc = kb.x[:, c * 1024:(c + 1) * 1024]
        xk = f"x{c}"
        if kind == "single_gb_dram":
            kb.p.dma("sync", kb.probs[:, 0:1024], kb.x1_scr[c * 128:(c + 1) * 128, :],
                     reads=[f"x1s{c}"], writes=["probs"], name="x1load", ch=0)
            _ln_stats(kb, kb.probs[:, 0:1024], "probs", c, EPS_LN)
            kb.act(kb.zbf[:, :], kb.probs[:, 0:1024], AF.Identity,
                   bias=kb.nmr[:, c:c + 1], scale=kb.rstd[:, c:c + 1],
                   reads=["probs", f"rstd{c}", f"nmr{c}"], writes=["zbf"],
                   name="appbf")
            _transpose_to_hnT(kb, c, gb=("gb", g1col, b1col))
            continue
        if kind == "double_gb":
            _ln_stats(kb, xc, xk, c, EPS_LN,
                      dst=(kb.rstd_o, kb.nmr_o, "o"))
        else:
            _ln_stats(kb, xc, xk, c, EPS_LN)
        if kind == "plain":
            if gate:
                kb.act(kb.zf[:, :], xc, AF.Identity, bias=kb.nmr[:, c:c + 1],
                       scale=kb.rstd[:, c:c + 1],
                       reads=[xk, f"rstd{c}", f"nmr{c}"], writes=["zf"], name="app32")
                if hnt:
                    kb.act(kb.zbf[:, :], kb.zf[:, :], AF.Identity,
                           reads=["zf"], writes=["zbf"], name="appbf")
                _gate_logits(kb, c)
            else:
                kb.act(kb.zbf[:, :], xc, AF.Identity, bias=kb.nmr[:, c:c + 1],
                       scale=kb.rstd[:, c:c + 1],
                       reads=[xk, f"rstd{c}", f"nmr{c}"], writes=["zbf"], name="appbf")
            if hnt:
                _transpose_to_hnT(kb, c)
        elif kind == "single_gb":
            kb.act(kb.zbf[:, :], xc, AF.Identity, bias=kb.nmr[:, c:c + 1],
                   scale=kb.rstd[:, c:c + 1],
                   reads=[xk, f"rstd{c}", f"nmr{c}"], writes=["zbf"], name="appbf")
            _transpose_to_hnT(kb, c, gb=("gb", g1col, b1col))
        elif kind == "double_gb":
            kb.act(kb.zf[:, :], xc, AF.Identity, bias=kb.nmr_o[:, c:c + 1],
                   scale=kb.rstd_o[:, c:c + 1],
                   reads=[xk, f"orstd{c}", f"onmr{c}"], writes=["zf"],
                   name="app32")
            _ln_stats(kb, kb.zf[:, :], "zf", c, EPS_LN)
            kb.act(kb.zbf[:, :], kb.zf[:, :], AF.Identity, bias=kb.nmr[:, c:c + 1],
                   scale=kb.rstd[:, c:c + 1],
                   reads=["zf", f"rstd{c}", f"nmr{c}"], writes=["zbf"], name="appbf")
            _transpose_to_hnT(kb, c, gb=("gb", g1col, b1col))
        elif kind == "double_rms":
            kb.act(kb.zf[:, :], xc, AF.Identity, bias=kb.nmr[:, c:c + 1],
                   scale=kb.rstd[:, c:c + 1],
                   reads=[xk, f"rstd{c}", f"nmr{c}"], writes=["zf"], name="app32")
            _ln_stats(kb, kb.zf[:, :], "zf", c, EPS_RMS, rms=True)
            kb.act(kb.zbf[:, :], kb.zf[:, :], AF.Identity, scale=kb.rstd[:, c:c + 1],
                   reads=["zf", f"rstd{c}"], writes=["zbf"], name="appbf")
            _transpose_to_hnT(kb, c, gb=("s", g1col, None))
        else:
            raise ValueError(kind)


def _gate_logits(kb, c):
    for r in range(2):
        for jj in range(4):
            j = r * 4 + jj
            kb.tr(kb.psS[:, jj * 128:(jj + 1) * 128],
                  kb.zf[:, j * 128:(j + 1) * 128], kb.ident_f[:, :],
                  reads=["zf"], writes=["psS"], name="trz", ident_key="idf")
        kb.vop(lambda e, o=kb.zT[:, 0:512], i=kb.psS[:, :]:
               e.tensor_copy(o, i), reads=["psS"],
               writes=["zT"], name="zTevac")
        for jj in range(4):
            dcc = r * 4 + jj
            kb.mm(kb.psB1[:, 0:8], kb.zT[:, jj * 128:(jj + 1) * 128],
                  kb.gwT[:, dcc * 8:(dcc + 1) * 8], start=(dcc == 0),
                  stop=(dcc == DC - 1),
                  reads=["zT", "gwT"], writes=["psB1"], name="gatemm")
    kb.vop(lambda e, o=kb.logit[:, c * 8:(c + 1) * 8], i=kb.psB1[:, 0:8]:
           e.tensor_copy(o, i), reads=["psB1"], writes=[f"logit{c}"], name="logevac")



FF_WUP = (0, 8192)
FF_WDN = (16384, 8192)
FF_HID = (32768, 4096)


def ffn_stage(kb, up_loader, dn_loader, act_fn, bias_col, scale_col, coeff_fn,
              add_b2=True, bias_key=None, scale_key=None):
    p = kb.p
    T = kb.T
    for hq in range(4):
        buf = hq % 2
        up_loader(hq, buf)
        dn_loader(hq, buf)
        upbase = FF_WUP[0] + buf * 8192
        dnbase = FF_WDN[0] + buf * 8192
        upkeys = kb.ark(upbase, 8192)
        dnkeys = kb.ark(dnbase, 8192)
        hidkeys = kb.ark(FF_HID[0], 4096)
        for tg in range(kb.NTG):
            for hc in range(8):
                ps = kb.psB0 if hc % 2 == 0 else kb.psB1
                psk = "psB0" if hc % 2 == 0 else "psB1"
                for dcc in range(DC):
                    kb.mm(ps[:, :],
                          kb.ar[:, upbase + dcc * 1024 + hc * 128:
                                upbase + dcc * 1024 + (hc + 1) * 128],
                          kb.hnT[:, dcc * T + tg * 512: dcc * T + tg * 512 + 512],
                          start=(dcc == 0), stop=(dcc == DC - 1),
                          reads=upkeys + [f"hnT{dcc}"], writes=[psk], name="ffA")
                ghc = hq * 8 + hc
                bias = bias_col[:, ghc:ghc + 1] if bias_col is not None else 0.0
                scale = scale_col[:, ghc:ghc + 1] if scale_col is not None else 1.0
                brd = ([bias_key] if bias_col is not None else []) + \
                      ([scale_key] if scale_col is not None else [])
                kb.act(kb.ar[:, FF_HID[0] + hc * 512: FF_HID[0] + (hc + 1) * 512],
                       ps[:, :], act_fn, bias=bias, scale=scale,
                       reads=[psk] + brd, writes=hidkeys, name="ffact")
            for dh in range(2):
                for i in range(4):
                    pslo = i * 512
                    pkeys = kb.pbk(pslo, 512)
                    with_b2 = (hq == 0 and add_b2)
                    for hc in range(8):
                        kb.mm(kb.psBIG[:, pslo:pslo + 512],
                              kb.ar[:, FF_HID[0] + hc * 512 + i * 128:
                                    FF_HID[0] + hc * 512 + (i + 1) * 128],
                              kb.ar[:, dnbase + hc * 1024 + dh * 512:
                                    dnbase + hc * 1024 + (dh + 1) * 512],
                              start=(hc == 0), stop=(hc == 7 and not with_b2),
                              reads=hidkeys + dnkeys, writes=pkeys, name="ffB")
                    if with_b2:
                        kb.mm(kb.psBIG[:, pslo:pslo + 512], kb.ones_bf[0:1, :],
                              kb.row0[0:1, dh * 512:(dh + 1) * 512],
                              start=False, stop=True,
                              reads=["ones", "row0"], writes=pkeys, name="ffb2")
                    tc = tg * 4 + i
                    coeff = coeff_fn(tc)
                    ckeys = []
                    if isinstance(coeff, tuple):
                        coeff, ck = coeff
                        ckeys = [ck]
                    xsl = kb.x[:, tc * 1024 + dh * 512: tc * 1024 + (dh + 1) * 512]
                    p.op("vector",
                         lambda e, o=xsl, a=kb.psBIG[:, pslo:pslo + 512], s=coeff:
                         e.scalar_tensor_tensor(o, a, s, o, ALU.mult, ALU.add),
                         reads=pkeys + [f"x{tc}"] + ckeys, writes=[f"x{tc}"],
                         name="ffevac")


def simple_up_loader(kb, w_dram):
    def load(hq, buf):
        base = FF_WUP[0] + buf * 8192
        load_wblock(kb, w_dram[:, hq * 1024:(hq + 1) * 1024], kb.ar, base,
                    kb.ark(base, 8192))
    return load


def simple_dn_loader(kb, w_dram):
    def load(hq, buf):
        base = FF_WDN[0] + buf * 8192
        load_wblock(kb, w_dram[hq * 1024:(hq + 1) * 1024, :], kb.ar, base,
                    kb.ark(base, 8192), ch=1)
    return load




def _sign_stripe(kb, dram_stripe, dest_base, rc, alpha_col_dst=None,
                 prescale_col=None, dkeys=()):
    p = kb.p
    p.dma("sync", kb.zf[:, :], dram_stripe, writes=["zf"], name="sgnload")
    if alpha_col_dst is not None:
        p.op("vector", lambda e, o=alpha_col_dst, i=kb.zf[:, :]:
             e.reduce_sum(o, i, AX.X, apply_absolute_value=True),
             reads=["zf"], writes=["alpha_w"], name="absred")
        p.op("vector", lambda e, o=alpha_col_dst: e.tensor_scalar_mul(o, o, 1.0 / D),
             reads=["alpha_w"], writes=["alpha_w"], name="alphasc")
    p.op("scalar", lambda e, o=kb.zbf[:, :], i=kb.zf[:, :]: e.sign(o, i),
         reads=["zf"], writes=["zbf"], name="sign")
    if prescale_col is not None:
        p.op("vector", lambda e, o=kb.zbf[:, :], s=prescale_col:
             e.tensor_scalar_mul(o, o, s),
             reads=["zbf", "alpha2_r"], writes=["zbf"], name="sgnscale")
    for j in range(DC):
        kb.tr(kb.psT[:, j * 128:(j + 1) * 128], kb.zbf[:, j * 128:(j + 1) * 128],
              kb.ident_bf[:, :], reads=["zbf"], writes=["psT"], name="sgntr")
    dview = kb.ar[:, dest_base: dest_base + 8192] \
        .rearrange("p (a j) -> p a j", a=8)[:, :, rc * 128:(rc + 1) * 128]
    pview = kb.psT[:, :].rearrange("p (a j) -> p a j", a=DC)
    kb.vop(lambda e, o=dview, i=pview: e.tensor_copy(o, i),
           reads=["psT"], writes=list(dkeys), name="sgnevac")


def sign_up_loader(kb, w1_dram, scr_up):
    def load(hq, buf):
        base = FF_WUP[0] + buf * 8192
        keys = kb.ark(base, 8192)
        for rc in range(8):
            ghc = hq * 8 + rc
            _sign_stripe(kb, w1_dram[ghc * 128:(ghc + 1) * 128, :], base, rc,
                         alpha_col_dst=kb.alpha1[:, ghc:ghc + 1], dkeys=keys)
        kb.p.dma("sync", scr_up[hq, :, :], kb.ar[:, base:base + 8192],
                 reads=keys, writes=[f"scru{hq}"], name="scrspill")
    return load


def sign_dn_loader(kb, w2_dram, scr_dn):
    def load(hq, buf):
        base = FF_WDN[0] + buf * 8192
        keys = kb.ark(base, 8192)
        for rc in range(8):
            _sign_stripe(kb, w2_dram[rc * 128:(rc + 1) * 128,
                                     hq * 1024:(hq + 1) * 1024],
                         base, rc, prescale_col=kb.alpha2[:, rc:rc + 1], dkeys=keys)
        kb.p.dma("sync", scr_dn[hq, :, :], kb.ar[:, base:base + 8192],
                 reads=keys, writes=[f"scrd{hq}"], name="scrspill2")
    return load


def alpha2_prepass(kb, w2_dram):
    p = kb.p
    for rc in range(8):
        for hh in range(4):
            p.dma("sync", kb.probs[:, 0:1024],
                  w2_dram[rc * 128:(rc + 1) * 128, hh * 1024:(hh + 1) * 1024],
                  writes=["probs"], name="a2load")
            p.op("vector", lambda e, o=kb.tmpst[:, hh:hh + 1], i=kb.probs[:, 0:1024]:
                 e.reduce_sum(o, i, AX.X, apply_absolute_value=True),
                 reads=["probs"], writes=[f"a2p{hh}"], name="a2red")
        p.op("vector", lambda e, o=kb.tmpst[:, 0:1], a=kb.tmpst[:, 0:1],
             b=kb.tmpst[:, 1:2]: e.tensor_add(o, a, b),
             reads=["a2p0", "a2p1"], writes=["a2p0"], name="a2add")
        p.op("vector", lambda e, o=kb.tmpst[:, 2:3], a=kb.tmpst[:, 2:3],
             b=kb.tmpst[:, 3:4]: e.tensor_add(o, a, b),
             reads=["a2p2", "a2p3"], writes=["a2p2"], name="a2add2")
        p.op("vector", lambda e, o=kb.alpha2[:, rc:rc + 1], a=kb.tmpst[:, 0:1],
             b=kb.tmpst[:, 2:3]: e.tensor_add(o, a, b),
             reads=["a2p0", "a2p2"], writes=["alpha2_r"], name="a2add3")
        p.op("vector", lambda e, o=kb.alpha2[:, rc:rc + 1]:
             e.tensor_scalar_mul(o, o, 1.0 / DFF),
             reads=["alpha2_r"], writes=["alpha2_r"], name="a2sc")


def scratch_up_loader(kb, scr_up):
    def load(hq, buf):
        base = FF_WUP[0] + buf * 8192
        kb.p.dma("gpsimd", kb.ar[:, base:base + 8192], scr_up[hq, :, :],
                 reads=[f"scru{hq}"], writes=kb.ark(base, 8192), name="scrload",
                 ch=0)
    return load


def scratch_dn_loader(kb, scr_dn):
    def load(hq, buf):
        base = FF_WDN[0] + buf * 8192
        kb.p.dma("gpsimd", kb.ar[:, base:base + 8192], scr_dn[hq, :, :],
                 reads=[f"scrd{hq}"], writes=kb.ark(base, 8192), name="scrload2",
                 ch=1)
    return load




def moe_prep_gate(kb, gate_w_dram):
    for dcc in range(DC):
        kb.p.dma("sync", kb.vspread[0:8, 0:128],
                 gate_w_dram[:, dcc * 128:(dcc + 1) * 128],
                 writes=["vspread"], name="gwload", ch=1)
        kb.tr(kb.psS[:, dcc * 8:(dcc + 1) * 8],
              kb.vspread[0:8, 0:128], kb.ident_f[0:8, 0:8],
              reads=["vspread"], writes=["psS"], name="gwtr", ident_key="idf")
    kb.vop(lambda e, o=kb.gwT[:, 0:64], i=kb.psS[:, 0:64]: e.tensor_copy(o, i),
           reads=["psS"], writes=["gwT"], name="gwevac")


def moe_gate_softmax(kb):
    p = kb.p
    p.op("vector", lambda e: e.memset(kb.found[:, 0:8], 1.0),
         writes=["ones8"], name="ones8")
    for tc in range(kb.TC):
        L = kb.logit[:, tc * 8:(tc + 1) * 8]
        lk = f"logit{tc}"
        nm = kb.negmax[:, tc:tc + 1]
        es = kb.esum[:, tc:tc + 1]
        p.op("vector", lambda e, o=nm, i=L: e.reduce_max(o, i, AX.X, negate=True),
             reads=[lk], writes=[f"nm{tc}"], name="gmax")
        kb.act(kb.zf[:, tc * 8:(tc + 1) * 8], L, AF.Exp, bias=nm,
               accum_out=es, reads=[lk, f"nm{tc}"], writes=["zf", f"es{tc}"],
               name="gexp")
        p.op("vector", lambda e, o=kb.recip[:, tc:tc + 1], i=es: e.reciprocal(o, i),
             reads=[f"es{tc}"], writes=[f"rc{tc}"], name="grecip")
        p.op("vector", lambda e, o=kb.selw[:, tc:tc + 1], i=kb.recip[:, tc:tc + 1]:
             e.tensor_scalar_mul(o, i, 0.5),
             reads=[f"rc{tc}"], writes=[f"sw{tc}"], name="gselw")
        p.op("vector", lambda e, o=kb.tmpst[:, tc:tc + 1], i=nm:
             e.tensor_scalar_mul(o, i, -1.0),
             reads=[f"nm{tc}"], writes=[f"gm{tc}"], name="gmaxpos")
        eq = kb.zf[:, 128 + tc * 8:128 + (tc + 1) * 8]
        neq = kb.zf[:, 256 + tc * 8:256 + (tc + 1) * 8]
        cp = kb.zf[:, 384 + tc * 8:384 + (tc + 1) * 8]
        sel = kb.zf[:, 512 + tc * 8:512 + (tc + 1) * 8]
        p.op("vector", lambda e, o=eq, i=L, s=kb.tmpst[:, tc:tc + 1]:
             e.tensor_single_scalar(o, i, s, ALU.is_equal),
             reads=[lk, f"gm{tc}"], writes=["zf"], name="geq")
        p.op("vector", lambda e, o=neq, i=eq:
             e.tensor_scalar(o, i, -1.0, 1.0, ALU.mult, ALU.add),
             reads=["zf"], writes=["zf"], name="gneq")
        p.op("vector", lambda e, o=cp, a=neq, b=kb.found[:, 0:8]:
             e.tensor_tensor_scan(o, a, b, 1.0, ALU.mult, ALU.mult),
             reads=["zf", "ones8"], writes=["zf"], name="gcum")
        p.op("vector", lambda e, o=sel[:, 1:8], a=eq[:, 1:8], b=cp[:, 0:7]:
             e.tensor_mul(o, a, b), reads=["zf"], writes=["zf"], name="gsel")
        p.op("vector", lambda e, o=sel[:, 0:1], i=eq[:, 0:1]: e.tensor_copy(o, i),
             reads=["zf"], writes=["zf"], name="gsel0")
        cview = kb.coeff[:, :].rearrange("p (e t) -> p t e", t=16)[:, tc, :]
        p.op("vector", lambda e, o=cview, i=sel, s=kb.selw[:, tc:tc + 1]:
             e.tensor_single_scalar(o, i, s, ALU.mult),
             reads=["zf", f"sw{tc}"], writes=[f"coef{tc}"], name="gcoef")


def moe_route(kb):
    p = kb.p
    zf = kb.zf
    p.op("vector", lambda e: e.memset(zf[:, 280:296], 1.0), writes=["zf"],
         name="r_ones")
    allcoef = [f"coef{t}" for t in range(16)]
    for tc in range(kb.TC):
        cv = kb.coeff[:, :].rearrange("p (e t) -> p t e", t=16)[:, tc, :]
        p.op("vector", lambda e, o=kb.selw[:, tc:tc + 1], i=cv:
             e.reduce_sum(o, i, AX.X),
             reads=[f"coef{tc}"], writes=[f"sw{tc}"], name="r_w")
    for e8 in range(NE):
        p.op("vector", lambda e, o=zf[:, e8 * 16:(e8 + 1) * 16],
             i=kb.coeff[:, e8 * 16:(e8 + 1) * 16]:
             e.tensor_single_scalar(o, i, 0.0, ALU.is_gt),
             reads=allcoef, writes=["zf"], name="r_sel")
        p.op("vector", lambda e, o=zf[:, 128 + e8 * 16:128 + (e8 + 1) * 16],
             a=zf[:, 280:296], b=zf[:, e8 * 16:(e8 + 1) * 16]:
             e.tensor_tensor_scan(o, a, b, 0.0, ALU.mult, ALU.add),
             reads=["zf"], writes=["zf"], name="r_scan")
    cnt = zf[:, 128:256].rearrange("p (e t) -> p e t", t=16)[:, :, 15]
    kb.tr(kb.psS[0:8, 0:128], cnt, kb.ident_f[:, :], reads=["zf"],
          writes=["psS"], name="r_tr1", ident_key="idf")
    p.op("vector", lambda e, o=kb.zT[0:8, 0:128], i=kb.psS[0:8, 0:128]:
         e.tensor_copy(o, i), reads=["psS"], writes=["zT"], name="r_cvac")
    p.op("vector", lambda e: e.memset(kb.zT[0:8, 128:256], 1.0),
         writes=["zT"], name="r_ones2")
    p.op("vector", lambda e, o=kb.zT[0:8, 256:384], a=kb.zT[0:8, 128:256],
         b=kb.zT[0:8, 0:128]:
         e.tensor_tensor_scan(o, a, b, 0.0, ALU.mult, ALU.add),
         reads=["zT"], writes=["zT"], name="r_scan2")
    p.op("vector", lambda e, o=kb.zT[0:8, 384:512], a=kb.zT[0:8, 256:384],
         b=kb.zT[0:8, 0:128]: e.tensor_sub(o, a, b),
         reads=["zT"], writes=["zT"], name="r_excl")
    kb.tr(kb.psS[0:128, 0:8], kb.zT[0:8, 384:512], kb.ident_f[0:8, 0:8],
          reads=["zT"], writes=["psS"], name="r_tr2", ident_key="idf")
    p.op("vector", lambda e, o=zf[:, 272:280], i=kb.psS[0:128, 0:8]:
         e.tensor_copy(o, i), reads=["psS"], writes=["zf"], name="r_ovac")
    p.op("vector", lambda e: e.memset(zf[:, 256:272], 0.0), writes=["zf"],
         name="r_dz")
    for e8 in range(NE):
        t1 = zf[:, 296:312]
        p.op("vector", lambda e, o=t1, i=zf[:, 128 + e8 * 16:128 + (e8 + 1) * 16],
             s=zf[:, 272 + e8:273 + e8]:
             e.tensor_scalar(o, i, s, -1.0, ALU.add, ALU.add),
             reads=["zf"], writes=["zf"], name="r_d1")
        p.op("vector", lambda e, o=t1, i=t1, c=float(CAP - 1), b=float(e8 * CAP):
             e.tensor_scalar(o, i, c, b, ALU.min, ALU.add),
             reads=["zf"], writes=["zf"], name="r_d2")
        p.op("vector", lambda e, o=t1, a=t1, b=zf[:, e8 * 16:(e8 + 1) * 16]:
             e.tensor_mul(o, a, b), reads=["zf"], writes=["zf"], name="r_d3")
        p.op("vector", lambda e, o=zf[:, 256:272], a=zf[:, 256:272], b=t1:
             e.tensor_add(o, a, b), reads=["zf"], writes=["zf"], name="r_d4")
    p.op("vector", lambda e, o=kb.idx[:, :], i=zf[:, 256:272]:
         e.tensor_copy(o, i), reads=["zf"], writes=["idx"], name="r_u32")


def moe_scatter_x(kb, xslots):
    zfb = kb.zf[:, :].bitcast(BF16)
    for tc in range(kb.TC):
        if tc % 2 == 0:
            buf, bkeys = kb.zbf[:, :], ["zbf"]
        else:
            buf, bkeys = zfb[:, 0:1024], ["zf"]
        kb.act(buf, kb.x[:, tc * 1024:(tc + 1) * 1024], AF.Identity,
               bias=kb.nmr[:, tc:tc + 1], scale=kb.rstd[:, tc:tc + 1],
               reads=[f"x{tc}", f"rstd{tc}", f"nmr{tc}"], writes=bkeys,
               name="sc_ln")
        kb.idma(xslots[:, :], IndirectOffsetOnAxis(kb.idx[:, tc:tc + 1], 0),
                buf, None, reads=bkeys + ["idx"], writes=["xsl"], ch=3,
                name="sc_scat")


def _slotT(kb):
    return kb.hnT[:, 0:DC * CAP]


def _yeacc(kb):
    return kb.hnT[:, :].bitcast(F32)[:, 4096:8192]


def moe_expert_gathered(kb, ex, w1_d, w2_d, b1_d, b2_d, xslots, yslots):
    p = kb.p
    st = _slotT(kb)
    ye = _yeacc(kb)
    zfb = kb.zf[:, :].bitcast(BF16)
    for sc in range(4):
        if sc % 2 == 0:
            buf, bkeys = kb.zbf[:, :], ["zbf"]
        else:
            buf, bkeys = zfb[:, 0:1024], ["zf"]
        p.dma("sync", buf, xslots[ex * CAP + sc * 128: ex * CAP + (sc + 1) * 128, :],
              reads=["xsl"], writes=bkeys, name="slgath", ch=2)
        for jj in range(DC):
            kb.tr(kb.psT[:, jj * 128:(jj + 1) * 128],
                  buf[:, jj * 128:(jj + 1) * 128], kb.ident_bf[:, :],
                  reads=bkeys, writes=["psT"], name="sltr")
        dview = st.rearrange("p (a s) -> p a s", a=DC)[:, :, sc * 128:(sc + 1) * 128]
        pview = kb.psT[:, :].rearrange("p (a s) -> p a s", a=DC)
        kb.vop(lambda e, o=dview, i=pview: e.tensor_copy(o, i),
               reads=["psT"], writes=["hsl"], name="slevac")
    load_col_tile(kb, b1_d, DFF, kb.bup_col, 0, "bup")
    load_row(kb, kb.row0, b2_d, 1024, "row0")
    for hq in range(4):
        buf = hq % 2
        upbase = FF_WUP[0] + buf * 8192
        dnbase = FF_WDN[0] + buf * 8192
        for hf in range(2):
            load_wblock(kb, w1_d[:, hq * 1024 + hf * 512: hq * 1024 + (hf + 1) * 512],
                        kb.ar, upbase + hf * 4096, kb.ark(upbase + hf * 4096, 4096),
                        engine="gpsimd", ch=hf)
            load_wblock(kb, w2_d[hq * 1024 + hf * 512: hq * 1024 + (hf + 1) * 512, :],
                        kb.ar, dnbase + hf * 4096, kb.ark(dnbase + hf * 4096, 4096),
                        engine="gpsimd", ch=4 + hf)
        upkeys = kb.ark(upbase, 8192)
        dnkeys = kb.ark(dnbase, 8192)
        hidkeys = kb.ark(FF_HID[0], 4096)
        for hc in range(8):
            ps = kb.psB0 if hc % 2 == 0 else kb.psB1
            psk = "psB0" if hc % 2 == 0 else "psB1"
            for dcc in range(DC):
                ub = upbase + (hc // 4) * 4096 + dcc * 512 + (hc % 4) * 128
                kb.mm(ps[:, :],
                      kb.ar[:, ub: ub + 128],
                      st[:, dcc * CAP: (dcc + 1) * CAP],
                      start=(dcc == 0), stop=(dcc == DC - 1),
                      reads=upkeys + ["hsl"], writes=[psk], name="gfA")
            ghc = hq * 8 + hc
            kb.act(kb.ar[:, FF_HID[0] + hc * 512: FF_HID[0] + (hc + 1) * 512],
                   ps[:, :], AF.Relu, bias=kb.bup_col[:, ghc:ghc + 1],
                   reads=[psk, "bup"], writes=hidkeys, name="gfact")
        for dh in range(2):
            for sc in range(4):
                pk = [f"pb{sc}"]
                with_b2 = (hq == 0)
                for hc in range(8):
                    kb.mm(kb.psBIG[:, sc * 512:(sc + 1) * 512],
                          kb.ar[:, FF_HID[0] + hc * 512 + sc * 128:
                                FF_HID[0] + hc * 512 + (sc + 1) * 128],
                          kb.ar[:, dnbase + hc * 1024 + dh * 512:
                                dnbase + hc * 1024 + (dh + 1) * 512],
                          start=(hc == 0), stop=(hc == 7 and not with_b2),
                          reads=hidkeys + dnkeys, writes=pk, name="gfB")
                if with_b2:
                    kb.mm(kb.psBIG[:, sc * 512:(sc + 1) * 512], kb.ones_bf[0:1, :],
                          kb.row0[0:1, dh * 512:(dh + 1) * 512],
                          start=False, stop=True,
                          reads=["ones", "row0"], writes=pk, name="gfb2")
                yv = ye[:, sc * 1024 + dh * 512: sc * 1024 + (dh + 1) * 512]
                if hq == 0:
                    p.op("vector", lambda e, o=yv, i=kb.psBIG[:, sc * 512:(sc + 1) * 512]:
                         e.tensor_copy(o, i),
                         reads=pk, writes=["hye"], name="gfev0")
                else:
                    p.op("vector", lambda e, o=yv, i=kb.psBIG[:, sc * 512:(sc + 1) * 512]:
                         e.scalar_tensor_tensor(o, i, 1.0, o, ALU.mult, ALU.add),
                         reads=pk + ["hye"], writes=["hye"], name="gfev1")
    for sc in range(4):
        p.dma("sync", yslots[ex * CAP + sc * 128: ex * CAP + (sc + 1) * 128, :],
              ye[:, sc * 1024:(sc + 1) * 1024],
              reads=["hye"], writes=["ysl"], name="yeout", ch=3)


def moe_gather_combine(kb, yslots):
    for tc in range(kb.TC):
        if tc % 2 == 0:
            buf, bk = kb.probs[:, :], "probs"
        else:
            buf, bk = kb.zf[:, :], "zf"
        kb.idma(buf, None, yslots[:, :],
                IndirectOffsetOnAxis(kb.idx[:, tc:tc + 1], 0),
                reads=["ysl", "idx"], writes=[bk], ch=3, name="yegath")
        kb.p.op("vector", lambda e, o=buf, s=kb.selw[:, tc:tc + 1]:
                e.tensor_scalar_mul(o, o, s),
                reads=[bk, f"sw{tc}"], writes=[bk], name="yew")
        kb.p.op("vector", lambda e, o=kb.x[:, tc * 1024:(tc + 1) * 1024],
                a=kb.x[:, tc * 1024:(tc + 1) * 1024], b=buf:
                e.tensor_add(o, a, b),
                reads=[f"x{tc}", bk], writes=[f"x{tc}"], name="yeadd")


def moe_fence(kb):
    kb.p.op("vector", lambda e, o=kb.zT[0:1, 0:1]: e.tensor_copy(o, kb.zT[0:1, 0:1]),
            reads=["hsl", "hye", "xsl", "ysl"],
            writes=[f"hnT{j}" for j in range(DC)] + ["zT"], name="hfence")


def moe_experts(kb, w1_3d, w2_3d, b1_2d, b2_2d):
    for ex in range(NE):
        load_col_tile(kb, b1_2d[ex], DFF, kb.bup_col, 0, "bup")
        load_row(kb, kb.row0, b2_2d[ex], 1024, "row0")
        up = simple_up_loader(kb, w1_3d[ex])
        dn = simple_dn_loader(kb, w2_3d[ex])

        def coeff_fn(tc, ex=ex):
            return (kb.coeff[:, ex * 16 + tc: ex * 16 + tc + 1], f"coef{tc}")
        ffn_stage(kb, up, dn, AF.Relu, kb.bup_col, None, coeff_fn, add_b2=True,
                  bias_key="bup", scale_key=None)




def _v_phase(kb, bias_row, bias_key):
    T = kb.T
    vbase = 8192
    wkeys = kb.ark(0, 8192)
    for tc in range(kb.TC):
        for nh in range(2):
            for dcc in range(DC):
                kb.mm(kb.psBIG[:, nh * 512:(nh + 1) * 512],
                      kb.hnT[:, dcc * T + tc * 128: dcc * T + (tc + 1) * 128],
                      kb.ar[:, dcc * 1024 + nh * 512: dcc * 1024 + (nh + 1) * 512],
                      start=(dcc == 0), stop=(dcc == DC - 1 and bias_row is None),
                      reads=wkeys + [f"hnT{dcc}"], writes=kb.pbk(nh * 512, 512),
                      name="vmm")
            if bias_row is not None:
                kb.mm(kb.psBIG[:, nh * 512:(nh + 1) * 512], kb.ones_bf[0:1, :],
                      bias_row[0:1, nh * 512:(nh + 1) * 512], start=False, stop=True,
                      reads=["ones", bias_key], writes=kb.pbk(nh * 512, 512),
                      name="vbias")
        kb.act(kb.ar[:, vbase + tc * 1024: vbase + (tc + 1) * 1024],
               kb.psBIG[:, 0:1024], AF.Identity,
               reads=kb.pbk(0, 1024), writes=kb.ark(vbase + tc * 1024, 1024),
               name="vevac")


def _qk_pair(kb, wq_flat, wk_flat, j, qbias, kbias):
    T = kb.T
    wqb, wkb = 2 * T, 2 * T + 1024
    load_wblock(kb, wq_flat[:, j * 128:(j + 1) * 128], kb.ar, wqb,
                kb.ark(wqb, 1024), ch=0)
    load_wblock(kb, wk_flat[:, j * 128:(j + 1) * 128], kb.ar, wkb,
                kb.ark(wkb, 1024), ch=1)
    for tgq in range(T // 512):
        for dcc in range(DC):
            kb.mm(kb.psB0[:, :], kb.ar[:, wqb + dcc * 128: wqb + (dcc + 1) * 128],
                  kb.hnT[:, dcc * T + tgq * 512: dcc * T + tgq * 512 + 512],
                  start=(dcc == 0), stop=(dcc == DC - 1),
                  reads=kb.ark(wqb, 1024) + [f"hnT{dcc}"], writes=["psB0"],
                  name="qmm")
        kb.act(kb.ar[:, tgq * 512:(tgq + 1) * 512], kb.psB0[:, :], AF.Identity,
               bias=(qbias[0][:, j:j + 1] if qbias else 0.0), scale=0.125,
               reads=["psB0"] + ([qbias[1]] if qbias else []),
               writes=kb.ark(tgq * 512, 512), name="qevac")
        for dcc in range(DC):
            kb.mm(kb.psB1[:, :], kb.ar[:, wkb + dcc * 128: wkb + (dcc + 1) * 128],
                  kb.hnT[:, dcc * T + tgq * 512: dcc * T + tgq * 512 + 512],
                  start=(dcc == 0), stop=(dcc == DC - 1),
                  reads=kb.ark(wkb, 1024) + [f"hnT{dcc}"], writes=["psB1"],
                  name="kmm")
        kb.act(kb.ar[:, T + tgq * 512: T + (tgq + 1) * 512], kb.psB1[:, :], AF.Identity,
               bias=(kbias[0][:, j:j + 1] if kbias else 0.0),
               reads=["psB1"] + ([kbias[1]] if kbias else []),
               writes=kb.ark(T + tgq * 512, 512), name="kevac")


def _l0_head_A(kb, j, h_loc, qc, ci):
    T = kb.T
    par = ci % 2
    s = ci % 32
    nm, nmk = kb.stat("nm", s)
    es, esk = kb.stat("es", s)
    rc, rck = kb.stat("rc", s)
    qk = kb.ark(qc * 128, 128)
    q_ap = kb.ar[h_loc:h_loc + 64, qc * 128:(qc + 1) * 128]
    nkc = 1 if qc == 0 else 2
    width = nkc * 128
    if qc == 0:
        kb.mm(kb.psS[:, par * 256: par * 256 + 128], q_ap,
              kb.ar[h_loc:h_loc + 64, T:T + 128],
              start=True, stop=True, reads=qk + kb.ark(T, 128),
              writes=["psS"], name="smm")
        mview = kb.mask[:, 128:256]
    else:
        kb.mm(kb.psS[:, par * 256: par * 256 + 256], q_ap,
              kb.ar[h_loc:h_loc + 64, T + (qc - 1) * 128: T + (qc + 1) * 128],
              start=True, stop=True,
              reads=qk + kb.ark(T + (qc - 1) * 128, 256), writes=["psS"],
              name="smm")
        mview = kb.mask[:, 0:256]
    pr = kb.probs[:, par * 512: par * 512 + width]
    prk = f"probs{par}"
    kb.p.op("vector", lambda e, o=pr, a=kb.psS[:, par * 256: par * 256 + width],
            b=mview: e.tensor_add(o, a, b),
            reads=["psS", "mask"], writes=[prk], name="smask")
    kb.p.op("vector", lambda e, o=nm, i=pr: e.reduce_max(o, i, AX.X, negate=True),
            reads=[prk], writes=[nmk], name="smax")
    P = kb.zbf[:, par * 512: par * 512 + width]
    Pk = f"zbfp{par}"
    kb.act(P, pr, AF.Exp, bias=nm, accum_out=es,
           reads=[prk, nmk], writes=[Pk, esk], name="sexp")
    kb.p.op("vector", lambda e, o=rc, i=es: e.reciprocal(o, i),
            reads=[esk], writes=[rck], name="srecip")
    kb.p.op("vector", lambda e, o=P, s_=rc: e.tensor_scalar_mul(o, o, s_),
            reads=[Pk, rck], writes=[Pk], name="snormv")


def _l0_head_B(kb, j, h_loc, qc, ci):
    T = kb.T
    vbase, aobase = 8192, 8192 + 8 * T
    head = 2 * j + (1 if h_loc else 0)
    par = ci % 2
    nkc = 1 if qc == 0 else 2
    width = nkc * 128
    Pk = f"zbfp{par}"
    PTk = f"zbft{par}"
    for kk in range(nkc):
        slot = par * 2 + kk
        kb.tr(kb.psT[:, slot * 128:(slot + 1) * 128],
              kb.zbf[:, par * 512 + kk * 128: par * 512 + (kk + 1) * 128],
              kb.ident_bf[:, :], reads=[Pk], writes=["psT"], name="str")
    kb.vop(lambda e, o=kb.zbf[:, par * 512 + 256: par * 512 + 256 + width],
           i=kb.psT[:, par * 256: par * 256 + width]: e.tensor_copy(o, i),
           reads=["psT"], writes=[PTk],
           name="sTevac")
    psA = kb.psB0 if par == 0 else kb.psB1
    psAk = "psB0" if par == 0 else "psB1"
    for kk in range(nkc):
        kc = qc - 1 + kk if qc > 0 else 0
        kb.mm(psA[h_loc:h_loc + 64, 0:128],
              kb.ar[:, vbase + kc * 1024 + head * 64: vbase + kc * 1024 + head * 64 + 64],
              kb.zbf[:, par * 512 + 256 + kk * 128: par * 512 + 256 + (kk + 1) * 128],
              start=(kk == 0), stop=(kk == nkc - 1),
              reads=kb.ark(vbase + kc * 1024, 1024) + [PTk], writes=[psAk],
              name="aomm", tile_position=(0, h_loc))
    kb.act(kb.ar[h_loc:h_loc + 64, aobase + j * T + qc * 128:
                 aobase + j * T + (qc + 1) * 128],
           psA[h_loc:h_loc + 64, 0:128], AF.Identity,
           reads=[psAk], writes=kb.ark(aobase + j * T + qc * 128, 128),
           name="aoevac")


def _ssm_u_scan(kb, acol_key="acol"):
    T = kb.T
    p = kb.p
    p.op("vector", lambda e: e.memset(kb.st_sb[:, 0:512], 1.0),
         writes=["st"], name="stones")
    p.op("vector", lambda e, o=kb.ab_sb[:, 0:512], i=kb.st_sb[:, 0:512],
         s=kb.acol[:, 0:1]: e.tensor_single_scalar(o, i, s, ALU.mult),
         reads=["st", acol_key], writes=["ab"], name="abmul")
    for s in range(T // 512):
        ps = kb.psB0 if s % 2 == 0 else kb.psB1
        psk = "psB0" if s % 2 == 0 else "psB1"
        for dcc in range(DC):
            kb.mm(ps[0:64, :], kb.bw_sb[:, dcc * 64:(dcc + 1) * 64],
                  kb.hnT[:, dcc * T + s * 512: dcc * T + s * 512 + 512],
                  start=(dcc == 0), stop=(dcc == DC - 1),
                  reads=["bw", f"hnT{dcc}"], writes=[psk], name="umm")
        init = 0.0 if s == 0 else kb.st_sb[:, s * 512 - 1: s * 512]
        p.op("vector",
             lambda e, o=kb.st_sb[:, s * 512:(s + 1) * 512],
             a=kb.ab_sb[:, 0:512], b=ps[0:64, :], ini=init:
             e.tensor_tensor_scan(o, a, b, ini, ALU.mult, ALU.add),
             reads=["ab", psk, "st"], writes=["st"], name="scan")


def _l0_combine(kb, tc, wg_base=8192):
    T = kb.T
    aobase = 8192 + 8 * T
    p = kb.p
    kb.mm(kb.psB0[:, :], kb.st_sb[:, tc * 128:(tc + 1) * 128],
          kb.cw_sb[:, 0:512], start=True, stop=True,
          reads=["st", "cw"], writes=["psB0"], name="ssmmm0")
    kb.mm(kb.psB1[:, :], kb.st_sb[:, tc * 128:(tc + 1) * 128],
          kb.cw_sb[:, 512:1024], start=True, stop=True,
          reads=["st", "cw"], writes=["psB1"], name="ssmmm1")
    kb.act(kb.probs[:, 0:512], kb.psB0[:, :], AF.Identity,
           reads=["psB0"], writes=["probs"], name="ssmev0")
    kb.act(kb.probs[:, 512:1024], kb.psB1[:, :], AF.Identity,
           reads=["psB1"], writes=["probs"], name="ssmev1")
    for nh in range(2):
        pk = kb.pbk(nh * 512, 512)
        for j in range(NPAIR):
            kb.mm(kb.psBIG[:, nh * 512:(nh + 1) * 512],
                  kb.ar[:, aobase + j * T + tc * 128: aobase + j * T + (tc + 1) * 128],
                  kb.ar[:, j * 1024 + nh * 512: j * 1024 + (nh + 1) * 512],
                  start=(j == 0), stop=False,
                  reads=kb.ark(aobase + j * T + tc * 128, 128) +
                  kb.ark(j * 1024 + nh * 512, 512),
                  writes=pk, name="prmm")
        kb.mm(kb.psBIG[:, nh * 512:(nh + 1) * 512], kb.ones_bf[0:1, :],
              kb.row2[0:1, nh * 512:(nh + 1) * 512], start=False, stop=True,
              reads=["ones", "row2"], writes=pk, name="prbias")
    for nh in range(2):
        pk = kb.pbk(1024 + nh * 512, 512)
        for dcc in range(DC):
            kb.mm(kb.psBIG[:, 1024 + nh * 512: 1024 + (nh + 1) * 512],
                  kb.hnT[:, dcc * T + tc * 128: dcc * T + (tc + 1) * 128],
                  kb.ar[:, wg_base + dcc * 1024 + nh * 512:
                        wg_base + dcc * 1024 + (nh + 1) * 512],
                  start=(dcc == 0), stop=False,
                  reads=[f"hnT{dcc}"] + kb.ark(wg_base + dcc * 1024 + nh * 512, 512),
                  writes=pk, name="gmm")
        kb.mm(kb.psBIG[:, 1024 + nh * 512: 1024 + (nh + 1) * 512], kb.ones_bf[0:1, :],
              kb.row1[0:1, nh * 512:(nh + 1) * 512], start=False, stop=True,
              reads=["ones", "row1"], writes=pk, name="gbias")
    kb.act(kb.zf[:, :], kb.psBIG[:, 1024:2048], AF.Sigmoid,
           reads=kb.pbk(1024, 1024), writes=["zf"], name="gsig")
    xk = f"x{tc}"
    xv = kb.x[:, tc * 1024:(tc + 1) * 1024]
    for hh in range(2):
        xh = kb.x[:, tc * 1024 + hh * 512: tc * 1024 + (hh + 1) * 512]
        ph = kb.probs[:, hh * 512:(hh + 1) * 512]
        p.op("vector", lambda e, o=kb.zT[:, :],
             a=kb.psBIG[:, hh * 512:(hh + 1) * 512], b=ph:
             e.tensor_sub(o, a, b),
             reads=kb.pbk(hh * 512, 512) + ["probs"], writes=["zT"], name="csub")
        p.op("vector", lambda e, o=kb.zT[:, :], a=kb.zT[:, :],
             b=kb.zf[:, hh * 512:(hh + 1) * 512]:
             e.tensor_mul(o, a, b), reads=["zT", "zf"], writes=["zT"], name="cmul")
        p.op("vector", lambda e, o=ph, a=ph, b=kb.zT[:, :]:
             e.tensor_add(o, a, b), reads=["probs", "zT"], writes=["probs"],
             name="ccombo")
        kb.act(kb.zT[:, :], xh, AF.Identity, bias=kb.nmr_o[:, tc:tc + 1],
               scale=kb.rstd_o[:, tc:tc + 1],
               reads=[xk, f"orstd{tc}", f"onmr{tc}"], writes=["zT"], name="crez")
        p.op("vector", lambda e, o=ph, a=ph, b=kb.zT[:, :]:
             e.tensor_add(o, a, b), reads=["probs", "zT"], writes=["probs"],
             name="caddz")
    p.dma("sync", kb.x1_scr[tc * 128:(tc + 1) * 128, :], kb.probs[:, 0:1024],
          reads=["probs"], writes=[f"x1s{tc}"], name="x1spill", ch=0)
    p.op("vector", lambda e, o=xv, a=xv, b=kb.probs[:, 0:1024]:
         e.tensor_add(o, a, b), reads=[xk, "probs"], writes=[xk], name="cadd1")


def _outproj_add(kb, tc, bias_row, bias_key):
    T = kb.T
    aobase = 8192 + 8 * T
    for nh in range(2):
        pk = kb.pbk(nh * 512, 512)
        for j in range(NPAIR):
            kb.mm(kb.psBIG[:, nh * 512:(nh + 1) * 512],
                  kb.ar[:, aobase + j * T + tc * 128: aobase + j * T + (tc + 1) * 128],
                  kb.ar[:, j * 1024 + nh * 512: j * 1024 + (nh + 1) * 512],
                  start=(j == 0), stop=False,
                  reads=kb.ark(aobase + j * T + tc * 128, 128) +
                  kb.ark(j * 1024 + nh * 512, 512),
                  writes=pk, name="pr1mm")
        kb.mm(kb.psBIG[:, nh * 512:(nh + 1) * 512], kb.ones_bf[0:1, :],
              bias_row[0:1, nh * 512:(nh + 1) * 512], start=False, stop=True,
              reads=["ones", bias_key], writes=pk, name="pr1bias")
    xk = f"x{tc}"
    xv = kb.x[:, tc * 1024:(tc + 1) * 1024]
    kb.p.op("vector", lambda e, o=xv, a=kb.psBIG[:, 0:1024]:
            e.scalar_tensor_tensor(o, a, 1.0, o, ALU.mult, ALU.add),
            reads=kb.pbk(0, 1024) + [xk], writes=[xk], name="pr1evac")


def attn_l0(kb, W, x1_scr):
    T = kb.T
    kb.x1_scr = x1_scr
    load_row(kb, kb.row1, W["bV"], 1024, "row1")
    load_row(kb, kb.row2, W["bO"], 1024, "row2")
    load_col_tile(kb, W["bQ"], 1024, kb.qkvcol, 0, "qcol", prescale=0.125)
    load_col_tile(kb, W["bK"], 1024, kb.qkvcol, 8, "kcol")
    kb.p.dma("sync", kb.acol[:, 0:1], W["A"][:, None], writes=["acol"],
             name="aload", ch=1)
    kb.p.dma("gpsimd", kb.cw_sb[:, :], W["Cw"][:, :], writes=["cw"],
             name="cwload", ch=2)
    load_wblock(kb, W["Bw"][:, :], kb.bw_sb, 0, ["bw"], ch=1)
    load_wblock(kb, W["WV"][:, :], kb.ar, 0, kb.ark(0, 8192))
    _v_phase(kb, kb.row1, "row1")
    _ssm_u_scan(kb)
    ci = 0
    queue = []
    LAG0 = 2
    for j in range(NPAIR):
        _qk_pair(kb, W["WQ"][:, :], W["WK"][:, :], j,
                 (kb.qkvcol[:, 0:8], "qcol"), (kb.qkvcol[:, 8:16], "kcol"))
        for h_loc in (0, 64):
            for qc in range(kb.TC):
                if len(queue) >= LAG0:
                    _l0_head_B(kb, *queue.pop(0))
                _l0_head_A(kb, j, h_loc, qc, ci)
                queue.append((j, h_loc, qc, ci))
                ci += 1
    for it in queue:
        _l0_head_B(kb, *it)
    if not globals().get("ATTN0_SKIP_P3", False):
        wg_base = 8192 if 8 * T >= 8192 else 8192 + 16 * T
        load_row(kb, kb.row1, W["bg"], 1024, "row1")
        load_wblock(kb, W["WO"][:, :], kb.ar, 0, kb.ark(0, 8192))
        load_wblock(kb, W["Wg"][:, :], kb.ar, wg_base, kb.ark(wg_base, 8192), ch=1)
        for tc in range(kb.TC):
            _l0_combine(kb, tc, wg_base)


def attn_l1(kb, W):
    T = kb.T
    load_row(kb, kb.row2, W["b_out"], 1024, "row2")
    wqf = W["wq"][:, :, :].rearrange("d h e -> d (h e)")
    wkf = W["wk"][:, :, :].rearrange("d h e -> d (h e)")
    wvf = W["wv"][:, :, :].rearrange("d h e -> d (h e)")
    load_wblock(kb, wvf, kb.ar, 0, kb.ark(0, 8192))
    _v_phase(kb, None, None)
    ci = 0
    queue = []
    LAG1 = 3
    for j in range(A1_NPAIR):
        _qk_pair(kb, wqf, wkf, j, None, None)
        for h_loc in (0, 64):
            for qc in range(kb.TC):
                if len(queue) >= LAG1:
                    _l1_head_B(kb, *queue.pop(0))
                _l1_head_A(kb, j, h_loc, qc, ci)
                queue.append((j, h_loc, qc, ci))
                ci += 1
    for it in queue:
        _l1_head_B(kb, *it)
    kb.p.op("vector", lambda e, o=kb.zf[:, 0:1]: e.tensor_copy(o, kb.zf[:, 0:1]),
            reads=["zfb0", "zfb1", "zfb2", "zfb3",
                   "prb0", "prb1", "prb2", "prb3"], writes=["zf", "probs"],
            name="zfence")
    if not globals().get("ATTN1_SKIP_PROJ", False):
        for j in range(NPAIR):
            kb.p.dma("gpsimd", kb.ar[:, j * 1024:(j + 1) * 1024],
                     W["w_out"][2 * j:2 * j + 2, :, :].rearrange("h e d -> (h e) d"),
                     writes=kb.ark(j * 1024, 1024), name="woload", ch=1)
        for tc in range(kb.TC):
            _outproj_add(kb, tc, kb.row2, "row2")


def _l1_pbuf(kb, par):
    if par == 0:
        base = 2 * kb.T + 2048
        ap = lambda lo, n: kb.ar[:, base + lo: base + lo + n]
        keys = lambda lo, n: kb.ark(base + lo, n)
    elif par == 1:
        zb = kb.zf[:, :].bitcast(BF16)
        ap = lambda lo, n, z=zb: z[:, lo: lo + n]
        keys = lambda lo, n: ["zfb%d" % i for i in
                              range(lo // 512, (lo + n + 511) // 512)]
    else:
        pb = kb.probs[:, :].bitcast(BF16)
        ap = lambda lo, n, z=pb: z[:, lo: lo + n]
        keys = lambda lo, n: ["prb%d" % i for i in
                              range(lo // 512, (lo + n + 511) // 512)]
    return ap, keys


def _l1_head_A(kb, j, h_loc, qc, ci):
    T = kb.T
    par = ci % 2
    s = ci % 32
    nm, nmk = kb.stat("nm", s)
    es, esk = kb.stat("es", s)
    rc, rck = kb.stat("rc", s)
    c4 = (ci % 4) * 4
    pap, pkeys = _l1_pbuf(kb, ci % 3)
    q_ap = kb.ar[h_loc:h_loc + 64, qc * 128:(qc + 1) * 128]

    def sbank(sq):
        b = (ci * 4 + sq) % 6
        if b < 4:
            return kb.psBIG[:, b * 512:(b + 1) * 512], f"pb{b}"
        return (kb.psB0 if b == 4 else kb.psB1)[:, :], ("psB0" if b == 4 else "psB1")
    for sq in range(4):
        sb, sbk = sbank(sq)
        kb.mm(sb, q_ap,
              kb.ar[h_loc:h_loc + 64, T + sq * 512: T + (sq + 1) * 512],
              start=True, stop=True,
              reads=kb.ark(qc * 128, 128) + kb.ark(T + sq * 512, 512),
              writes=[sbk], name="smm1")
        kb.p.op("vector", lambda e, o=kb.rstd_o[:, c4 + sq:c4 + sq + 1],
                i=sb: e.reduce_max(o, i, AX.X),
                reads=[sbk], writes=[f"orstd{c4 + sq}"], name="qmax")
    kb.p.op("vector", lambda e, o=nm, i=kb.rstd_o[:, c4:c4 + 4]:
            e.reduce_max(o, i, AX.X, negate=True),
            reads=[f"orstd{c4 + sq}" for sq in range(4)], writes=[nmk],
            name="gmax1")
    for sq in range(4):
        sb, sbk = sbank(sq)
        kb.act(pap(sq * 512, 512), sb,
               AF.Exp, bias=nm, accum_out=kb.nmr_o[:, c4 + sq:c4 + sq + 1],
               reads=[sbk, nmk],
               writes=pkeys(sq * 512, 512) + [f"onmr{c4 + sq}"], name="sexp1")
    kb.p.op("vector", lambda e, o=es, i=kb.nmr_o[:, c4:c4 + 4]:
            e.reduce_sum(o, i, AX.X),
            reads=[f"onmr{c4 + sq}" for sq in range(4)], writes=[esk],
            name="esum1")
    kb.p.op("vector", lambda e, o=rc, i=es: e.reciprocal(o, i),
            reads=[esk], writes=[rck], name="srecip1")
    for sq in range(4):
        kb.p.op("vector", lambda e, o=pap(sq * 512, 512), s_=rc:
                e.tensor_scalar_mul(o, o, s_),
                reads=pkeys(sq * 512, 512) + [rck],
                writes=pkeys(sq * 512, 512), name="snorm1")


def _l1_head_B(kb, j, h_loc, qc, ci):
    T = kb.T
    vbase, aobase = 8192, 8192 + 8 * T
    head = 2 * j + (1 if h_loc else 0)
    par = ci % 2
    pap, pkeys = _l1_pbuf(kb, ci % 3)
    for r in range(2):
        for kk in range(8):
            c = r * 8 + kk
            kb.tr(kb.psT[:, kk * 128:(kk + 1) * 128], pap(c * 128, 128),
                  kb.ident_bf[:, :], reads=pkeys(c * 128, 128), writes=["psT"],
                  name="str1")
        kb.vop(lambda e, o=pap(r * 1024, 1024), i=kb.psT[:, 0:1024]:
               e.tensor_copy(o, i),
               reads=["psT"], writes=pkeys(r * 1024, 1024), name="sTevac1")
    psA = kb.psS[h_loc:h_loc + 64, par * 128:(par + 1) * 128]
    for kc in range(kb.TC):
        kb.mm(psA,
              kb.ar[:, vbase + kc * 1024 + head * 64: vbase + kc * 1024 + head * 64 + 64],
              pap(kc * 128, 128),
              start=(kc == 0), stop=(kc == kb.TC - 1),
              reads=kb.ark(vbase + kc * 1024, 1024) + pkeys(kc * 128, 128),
              writes=["psS"], name="aomm1", tile_position=(0, h_loc))
    kb.act(kb.ar[h_loc:h_loc + 64, aobase + j * T + qc * 128:
                 aobase + j * T + (qc + 1) * 128],
           psA, AF.Identity,
           reads=["psS"], writes=kb.ark(aobase + j * T + qc * 128, 128),
           name="aoevac1")




def load_x(kb, x_dram):
    kb.p.dma("sync", kb.x[:, :].rearrange("p (c j) -> p c j", c=kb.TC),
             x_dram[:, :].rearrange("(c p) j -> p c j", p=128),
             writes=[f"x{c}" for c in range(kb.TC)], name="xload")


def store_y(kb, y_dram):
    kb.p.dma("sync", y_dram[:, :].rearrange("(c p) j -> p c j", p=128),
             kb.x[:, :].rearrange("p (c j) -> p c j", c=kb.TC),
             reads=[f"x{c}" for c in range(kb.TC)], name="ystore")
    kb.p.barrier_all_dma()


def load_consts(kb, cid_bf, cid_f, cmask):
    kb.p.dma("gpsimd", kb.ident_bf[:, :], cid_bf[:, :], writes=["idb"],
             name="idbload", ch=2)
    kb.p.dma("sync", kb.ident_f[:, :], cid_f[:, :], writes=["idf"],
             name="idfload", ch=1)
    kb.p.dma("gpsimd", kb.mask[:, :], cmask[:, :], writes=["mask"],
             name="maskload", ch=2)
    kb.p.op("vector", lambda e: e.memset(kb.ones_bf[0:1, :], 1.0),
            writes=["ones"], name="onesset")


STAGE_NAMES = ("ffnA", "attn0", "ffnB", "moe1", "attn1", "moe2")

A1_OLDMAX = False
A1_NPAIR = NPAIR
A1_NOACCUM = False
A1_SKIPNORM = False
A1_NOPAR = False
A1_NOSTATB = False
A1_CUT = 5

DEBUG_TAPS = []


def build_program(T=2048, stages=STAGE_NAMES, n_moe_experts=NE):
    import contextlib
    ctx = contextlib.ExitStack()
    nc = bass.Bass("TRN2", target_bir_lowering=False)
    p = Plan()

    dt = {}

    def din(name, shape):
        dt[name] = nc.dram_tensor(name, list(shape), F32, kind="ExternalInput")
        return dt[name]

    din("L0_attn_x", (T, 1024))
    for n, s in [("L0_attn_ln1_g", (1024,)), ("L0_attn_ln1_b", (1024,)),
                 ("L0_attn_WQ", (1024, 1024)), ("L0_attn_bQ", (1024,)),
                 ("L0_attn_WK", (1024, 1024)), ("L0_attn_bK", (1024,)),
                 ("L0_attn_WV", (1024, 1024)), ("L0_attn_bV", (1024,)),
                 ("L0_attn_WO", (1024, 1024)), ("L0_attn_bO", (1024,)),
                 ("L0_attn_Wg", (1024, 1024)), ("L0_attn_bg", (1024,)),
                 ("L0_attn_A", (64,)), ("L0_attn_Bw", (1024, 64)),
                 ("L0_attn_Cw", (64, 1024)),
                 ("L0_attn_ln2_g", (1024,)), ("L0_attn_ln2_b", (1024,)),
                 ("L0_attn_W1", (1024, 4096)), ("L0_attn_b1", (4096,)),
                 ("L0_attn_W2", (4096, 1024)), ("L0_attn_b2", (1024,)),
                 ("L0_ffn_w1", (4096, 1024)), ("L0_ffn_b1", (4096,)),
                 ("L0_ffn_w2", (1024, 4096)), ("L0_ffn_b2", (1024,)),
                 ("L1_attn_norm_scale", (1024,)),
                 ("L1_attn_wq", (1024, 16, 64)), ("L1_attn_wk", (1024, 16, 64)),
                 ("L1_attn_wv", (1024, 16, 64)), ("L1_attn_w_out", (16, 64, 1024)),
                 ("L1_attn_b_out", (1024,)),
                 ("L1_ffn_gate_w", (8, 1024)), ("L1_ffn_w1", (8, 1024, 4096)),
                 ("L1_ffn_b1", (8, 4096)), ("L1_ffn_w2", (8, 4096, 1024)),
                 ("L1_ffn_b2", (8, 1024))]:
        din(n, s)
    din("c_ident_bf", (128, 128))
    din("c_ident_f", (128, 128))
    din("c_mask", (128, 256))
    y = nc.dram_tensor("y", [T, 1024], F32, kind="ExternalOutput")
    scr_up = nc.dram_tensor("scr_up", [4, 128, 8192], BF16, kind="Internal")
    scr_dn = nc.dram_tensor("scr_dn", [4, 128, 8192], BF16, kind="Internal")

    kb = KB(nc, p, T, ctx)
    for v in (EPS_LN, EPS_RMS, -0.5, 0.125, -1.0):
        kb.register_const(v)
    load_consts(kb, dt["c_ident_bf"], dt["c_ident_f"], dt["c_mask"])
    load_x(kb, dt["L0_attn_x"])

    def ffn_l0(first):
        load_col_tile(kb, dt["L0_ffn_b1"], 4096, kb.bup_col, 0, "bup")
        load_row(kb, kb.row0, dt["L0_ffn_b2"], 1024, "row0")
        if first:
            alpha2_prepass(kb, dt["L0_ffn_w2"][:, :])
            up = sign_up_loader(kb, dt["L0_ffn_w1"][:, :], scr_up)
            dn = sign_dn_loader(kb, dt["L0_ffn_w2"][:, :], scr_dn)
        else:
            up = scratch_up_loader(kb, scr_up)
            dn = scratch_dn_loader(kb, scr_dn)
        ffn_stage(kb, up, dn, AF.Gelu, kb.bup_col, kb.alpha1,
                  lambda tc: 0.5, add_b2=True,
                  bias_key="bup", scale_key="alpha_w")

    if "ffnA" in stages:
        ln_stage(kb, "plain")
        ffn_l0(True)
    if "attn0" in stages:
        x1_scr = nc.dram_tensor("x1_scr", [T, 1024], F32, kind="Internal")
        load_col_tile(kb, dt["L0_attn_ln1_g"], 1024, kb.gcol, 0, "gcol")
        load_col_tile(kb, dt["L0_attn_ln1_b"], 1024, kb.bcol, 0, "bcol")
        ln_stage(kb, "double_gb", kb.gcol, kb.bcol)
        attn_l0(kb, dict(
            WQ=dt["L0_attn_WQ"], WK=dt["L0_attn_WK"], WV=dt["L0_attn_WV"],
            WO=dt["L0_attn_WO"], Wg=dt["L0_attn_Wg"],
            bQ=dt["L0_attn_bQ"], bK=dt["L0_attn_bK"], bV=dt["L0_attn_bV"],
            bO=dt["L0_attn_bO"], bg=dt["L0_attn_bg"],
            A=dt["L0_attn_A"], Bw=dt["L0_attn_Bw"], Cw=dt["L0_attn_Cw"]), x1_scr)
        load_col_tile(kb, dt["L0_attn_ln2_g"], 1024, kb.gcol, 0, "gcol")
        load_col_tile(kb, dt["L0_attn_ln2_b"], 1024, kb.bcol, 0, "bcol")
        ln_stage(kb, "single_gb_dram", kb.gcol, kb.bcol)
        load_col_tile(kb, dt["L0_attn_b1"], 4096, kb.bup_col, 0, "bup")
        load_row(kb, kb.row0, dt["L0_attn_b2"], 1024, "row0")
        ffn_stage(kb, simple_up_loader(kb, dt["L0_attn_W1"][:, :]),
                  simple_dn_loader(kb, dt["L0_attn_W2"][:, :]),
                  AF.Gelu, kb.bup_col, None, lambda tc: 1.0, add_b2=True,
                  bias_key="bup", scale_key=None)
    if "ffnB" in stages:
        ln_stage(kb, "plain")
        ffn_l0(False)
    moe_on = [m for m in ("moe1", "moe2") if m in stages]
    if GATHER_MOE and moe_on:
        xslots = nc.dram_tensor("xslots", [NSLOT, 1024], BF16, kind="Internal")
        yslots = nc.dram_tensor("yslots", [NSLOT, 1024], F32, kind="Internal")
    for mname in ("moe1", "moe2"):
        if mname in stages:
            moe_prep_gate(kb, dt["L1_ffn_gate_w"])
            if GATHER_MOE:
                ln_stage(kb, "plain", gate=True, hnt=False)
                moe_gate_softmax(kb)
                moe_route(kb)
                moe_scatter_x(kb, xslots)
                for ex in range(NE):
                    moe_expert_gathered(kb, ex, dt["L1_ffn_w1"][ex],
                                        dt["L1_ffn_w2"][ex],
                                        dt["L1_ffn_b1"][ex],
                                        dt["L1_ffn_b2"][ex], xslots, yslots)
                moe_gather_combine(kb, yslots)
                moe_fence(kb)
            else:
                ln_stage(kb, "plain", gate=True)
                moe_gate_softmax(kb)
                moe_experts(kb, dt["L1_ffn_w1"], dt["L1_ffn_w2"],
                            dt["L1_ffn_b1"], dt["L1_ffn_b2"])
        if mname == "moe1" and "attn1" in stages:
            load_col_tile(kb, dt["L1_attn_norm_scale"], 1024, kb.gcol, 0, "gcol")
            ln_stage(kb, "double_rms", kb.gcol)
            attn_l1(kb, dict(
                wq=dt["L1_attn_wq"], wk=dt["L1_attn_wk"], wv=dt["L1_attn_wv"],
                w_out=dt["L1_attn_w_out"], b_out=dt["L1_attn_b_out"]))

    for tname, tshape, tfn in DEBUG_TAPS:
        tdram = nc.dram_tensor(tname, list(tshape), F32, kind="ExternalOutput")
        ap, rkeys = tfn(kb)
        p.dma("gpsimd", tdram[:, :] if len(tshape) == 2 else tdram[:],
              ap, reads=rkeys, name="tap", ch=3)

    store_y(kb, y)

    p.finalize()
    sems = {}
    for axis in p.used_axes():
        sems[axis] = ctx.enter_context(nc.semaphore(axis))
    p.emit(nc, sems)
    ctx.close()
    return nc, p




def make_consts():
    cid_bf = np.eye(128, dtype=np.float32)
    cid_f = np.eye(128, dtype=np.float32)
    m0 = np.where(np.arange(256)[None, 0:128] > np.arange(128)[:, None],
                  0.0, NEG).astype(np.float32)
    m1 = np.where(np.arange(128)[None, :] <= np.arange(128)[:, None],
                  0.0, NEG).astype(np.float32)
    cmask = np.concatenate([m0, m1], axis=1).astype(np.float32)
    return cid_bf, cid_f, cmask


_CACHED = {}


def kernel(**inputs):
    T = 2048
    key = "full"
    if key not in _CACHED:
        _CACHED[key] = build_program(T)
    nc, _ = _CACHED[key]
    cid_bf, cid_f, cmask = make_consts()
    x = np.ascontiguousarray(np.asarray(inputs["L0_attn_x"], dtype=np.float32))
    base = {}
    for k, v in inputs.items():
        if k == "L0_attn_x":
            continue
        base[k] = np.ascontiguousarray(np.asarray(v, dtype=np.float32))
    base["c_ident_bf"] = cid_bf
    base["c_ident_f"] = cid_f
    base["c_mask"] = cmask
    in_maps = []
    for core in range(8):
        m = dict(base)
        m["L0_attn_x"] = x[core]
        in_maps.append(m)
    res = run_bass_kernel_spmd(nc, in_maps, core_ids=list(range(8)))
    out = np.stack([r["y"] for r in res.results], axis=0)
    return out.astype(np.float32)
